# version 11
# baseline (speedup 1.0000x reference)
import numpy as np
from contextlib import ExitStack

import concourse.bass as bass
import concourse.mybir as mybir
from concourse.bass_utils import run_bass_kernel_spmd


F32 = mybir.dt.float32
BF16 = mybir.dt.bfloat16
I32 = mybir.dt.int32
AF = mybir.ActivationFunctionType
ALU = mybir.AluOpType
AX = mybir.AxisListType

B = 2
L = 2048
DM = 1024
DI = 2048
NST = 16
DTR = 64
KC = 4
MDI, MN, MDTR, MKC = 64, 4, 4, 2
CH = 256
NCH = L // CH
NT = DI // 128
EPS = 1e-5

ENGINES = ("sync", "tensor", "vector", "scalar", "gpsimd")


class Sem:
    __slots__ = ("h", "n", "owner", "name", "is_dma")

    def __init__(self, h, owner, name, is_dma=False):
        self.h, self.n, self.owner, self.name = h, 0, owner, name
        self.is_dma = is_dma


class Sched:

    def __init__(self, nc, stack):
        self.nc = nc
        self.ops = {e: [] for e in ENGINES}
        self.waited = {e: {} for e in ENGINES}
        self.stack = stack

    def sem(self, owner, name, is_dma=False):
        h = self.stack.enter_context(self.nc.semaphore(name))
        return Sem(h, owner, name, is_dma)

    def call(self, _eng, _opname, *args, wait=(), inc=None, amt=1,
             raw=False, **kwargs):
        wl = []
        for tok in wait:
            if tok is None:
                continue
            s, v = tok
            if s.owner == _eng and not s.is_dma:
                continue
            prev = self.waited[_eng].get(id(s), 0)
            if v > prev:
                self.waited[_eng][id(s)] = v
                wl.append((s.h, v))
        if inc is not None:
            inc.n += amt
        tok_out = (inc, inc.n) if inc is not None else None
        import sys as _sys
        _ln = _sys._getframe(1).f_lineno
        self.ops[_eng].append((wl, _opname, args, kwargs,
                               inc.h if inc is not None else None, amt, _ln,
                               raw))
        return tok_out

    def mark(self, eng, sem, amt=1):
        wl, opname, args, kwargs, inch, a, _ln, raw = self.ops[eng][-1]
        if inch is not None:
            assert inch is sem.h, "last op already incs a different sem"
            return (sem, sem.n)
        sem.n += amt
        self.ops[eng][-1] = (wl, opname, args, kwargs, sem.h, amt, _ln, raw)
        return (sem, sem.n)

    def finalize(self):
        with self.nc.Block() as block:
            for name in ENGINES:
                ops = self.ops[name]

                def body(engine, ops=ops, name=name):
                    compute = name in ("vector", "scalar", "gpsimd")

                    def small_out(a):
                        try:
                            sh = a[0].shape
                            n = 1
                            for s in sh[1:]:
                                n *= s
                            return n <= 32
                        except Exception:
                            return False

                    for wl, opname, args, kwargs, inch, amt, _ln, raw in ops:
                        for h, v in wl:
                            engine.wait_ge(h, v)
                        try:
                            ins = getattr(engine, opname)(*args, **kwargs)
                        except Exception as e:
                            raise RuntimeError(
                                f"emit failed: {name}.{opname} from "
                                f"kernel.py:{_ln}: {e}") from e
                        if compute and inch is None and \
                                opname not in ("dma_start", "nop", "memset") \
                                and (opname == "tensor_reduce"
                                     or small_out(args)):
                            engine.drain(fusable=True)
                        if inch is not None:
                            if compute and opname != "dma_start" and not raw:
                                engine.drain(fusable=True).then_inc(inch, amt)
                            else:
                                ins.then_inc(inch, amt)
                getattr(block, name)(body)


def build_nc(dbg=False, phases="ABCF"):
    nc = bass.Bass("TRN2", target_bir_lowering=False,
                   detect_race_conditions=False)
    stack = ExitStack()

    def din(name, shape):
        return nc.dram_tensor(name, list(shape), F32, kind="ExternalInput").ap()

    dx = din("x_fm", (B, DM, L))
    d_inWT = din("inWT", (DM, 2 * DI))
    d_convW = din("convW", (DI, KC))
    d_convB = din("convB", (DI, 1))
    d_xprojWT = din("xprojWT", (DI, DTR + 2 * NST))
    d_Alog = din("Alog", (DI, NST))
    d_D = din("Dvec", (DI, 1))
    d_dtowT = din("dtowT", (DTR, DI))
    d_dtoB = din("dtoB", (DI, 1))
    d_opWT = din("opWT", (DI, DM))
    d_minWT = din("minWT", (DTR, 2 * MDI))
    d_mconvW = din("mconvW", (MDI, MKC))
    d_mconvB = din("mconvB", (MDI, 1))
    d_mxprojWT = din("mxprojWT", (MDI, MDTR + 2 * MN))
    d_mdtWT = din("mdtWT", (MDTR, MDI))
    d_mdtB = din("mdtB", (MDI, 1))
    d_mAlog = din("mAlog", (MDI, MN))
    d_mD = din("mDvec", (MDI, 1))
    d_moutWT = din("moutWT", (MDI, DTR))
    d_gWT = din("gWT", (DM, 4096))
    d_uWT = din("uWT", (DM, 4096))
    d_dWT = din("dWT", (4096, DM))

    d_out = nc.dram_tensor("out_fm", [B, DM, L], F32, kind="ExternalOutput").ap()

    scr_kind = "ExternalOutput" if dbg else "Internal"
    scr_u = nc.dram_tensor("scr_u", [B, DI, L], F32, kind=scr_kind).ap()
    scr_sz = nc.dram_tensor("scr_sz", [B, DI, L], F32, kind=scr_kind).ap()
    scr_yg = nc.dram_tensor("scr_yg", [B, DI, L], F32, kind=scr_kind).ap()
    scr_x2 = nc.dram_tensor("scr_x2", [B, DM, L], F32,
                            kind="ExternalInput" if phases == "F"
                            else scr_kind).ap()
    if dbg:
        d_xdbl = nc.dram_tensor("dbg_xdbl", [96, B * L], F32,
                                kind="ExternalOutput").ap()
        d_dtm = nc.dram_tensor("dbg_dtm", [DTR, B * L], F32,
                               kind="ExternalOutput").ap()
        d_xq = nc.dram_tensor("dbg_xq", [128, 8 * 512], F32,
                              kind="ExternalOutput").ap()
        d_hp = nc.dram_tensor("dbg_hp", [128, 32 * 512], F32,
                              kind="ExternalOutput").ap()
        d_fc = nc.dram_tensor("dbg_fc", [128, 8 * 512], F32,
                              kind="ExternalOutput").ap()

    sch = Sched(nc, stack)
    LD = sch.sem("sync", "ld", is_dma=True)
    ST = sch.sem("gpsimd", "st", is_dma=True)
    TS = sch.sem("tensor", "tsm")
    VS = sch.sem("vector", "vsm")
    SS = sch.sem("scalar", "ssm")
    GS = sch.sem("gpsimd", "gsm")

    def fence(eng, semq):
        sch.call(eng, "nop", wait=[(semq, semq.n)])

    alloc = nc.alloc_sbuf_tensor

    def salloc(scope, name, shape, dtype=F32):
        return scope.enter_context(nc.sbuf_tensor(name, shape, dtype))

    ones_col = alloc("ones_col", [128, 1], F32)
    ones_row = alloc("ones_row", [1, 128], F32)
    ident = alloc("ident", [128, 128], F32)
    tmp_init = alloc("tmp_init", [128, 256], F32)
    cells = alloc("cells", [1, 16], F32)
    mk_scope = ExitStack()
    a_neg = salloc(mk_scope, "a_neg", [128, NT * NST])
    convw_c = salloc(mk_scope, "convw_c", [128, NT * KC])
    convb_c = salloc(mk_scope, "convb_c", [128, NT])
    dvec_c = salloc(mk_scope, "dvec_c", [128, NT])
    dtob_c = salloc(mk_scope, "dtob_c", [128, NT])
    m_aneg = salloc(mk_scope, "m_aneg", [128, MN])
    m_convw = salloc(mk_scope, "m_convw", [128, MKC])
    m_convb = salloc(mk_scope, "m_convb", [128, 1])
    m_dvec = salloc(mk_scope, "m_dvec", [128, 1])
    m_dtb = salloc(mk_scope, "m_dtb", [128, 1])
    x_dbl = salloc(mk_scope, "x_dbl", [96, B * L])
    dtm_sb = salloc(mk_scope, "dtm_sb", [DTR, B * L])
    states = salloc(mk_scope, "states", [128, B * NT * NST])
    w_dtow = salloc(mk_scope, "w_dtow", [DTR, DI])
    w_minW = salloc(mk_scope, "w_minW", [DTR, 2 * MDI])
    w_mxproj = salloc(mk_scope, "w_mxproj", [MDI, MDTR + 2 * MN])
    w_mdt = salloc(mk_scope, "w_mdt", [MDTR, MDI])
    w_mout = salloc(mk_scope, "w_mout", [MDI, DTR])
    w_xproj = [salloc(mk_scope, f"w_xproj{k}", [128, 96]) for k in range(NT)]
    rows = salloc(mk_scope, "rows", [1, 2560])

    pb = [nc.alloc_psum_tensor(f"pb{i}", [128, 512], F32) for i in range(8)]
    bank_free = [None] * 8

    def load(dst_ap, src_ap, wait=()):
        return sch.call("sync", "dma_start", dst_ap, src_ap,
                        wait=wait, inc=LD, amt=16)

    def store(dst_ap, src_ap, wait=()):
        return sch.call("gpsimd", "dma_start", dst_ap, src_ap,
                        wait=wait, inc=ST, amt=16)

    for t in range(NT):
        load(tmp_init.ap()[:, t * NST:(t + 1) * NST],
             d_Alog[t * 128:(t + 1) * 128, :])
    tok_alog = (LD, LD.n)
    fence("sync", LD)
    for t in range(NT):
        load(convw_c.ap()[:, t * KC:(t + 1) * KC],
             d_convW[t * 128:(t + 1) * 128, :])
        load(convb_c.ap()[:, t:t + 1], d_convB[t * 128:(t + 1) * 128, :])
        load(dvec_c.ap()[:, t:t + 1], d_D[t * 128:(t + 1) * 128, :])
        load(dtob_c.ap()[:, t:t + 1], d_dtoB[t * 128:(t + 1) * 128, :])
    load(m_aneg.ap()[0:MDI, :], d_mAlog[:, :])
    load(m_convw.ap()[0:MDI, :], d_mconvW[:, :])
    load(m_convb.ap()[0:MDI, :], d_mconvB[:, :])
    load(m_dvec.ap()[0:MDI, :], d_mD[:, :])
    load(m_dtb.ap()[0:MDI, :], d_mdtB[:, :])
    load(w_dtow.ap(), d_dtowT)
    load(w_minW.ap(), d_minWT)
    load(w_mxproj.ap(), d_mxprojWT)
    load(w_mdt.ap(), d_mdtWT)
    load(w_mout.ap(), d_moutWT)
    for k in range(NT):
        load(w_xproj[k].ap(), d_xprojWT[k * 128:(k + 1) * 128, :])
    tok_small = (LD, LD.n)
    fence("sync", LD)

    t1 = sch.call("scalar", "activation", a_neg.ap(), tmp_init.ap(), AF.Exp,
                  wait=[tok_alog], inc=SS)
    t2 = sch.call("scalar", "activation", m_aneg.ap()[0:MDI, :],
                  m_aneg.ap()[0:MDI, :], AF.Exp, wait=[tok_small], inc=SS)
    sch.call("vector", "tensor_scalar", a_neg.ap(), a_neg.ap(), -1.0, None,
             wait=[t1], op0=ALU.mult)
    sch.call("vector", "tensor_scalar", m_aneg.ap()[0:MDI, :],
             m_aneg.ap()[0:MDI, :], -1.0, None, wait=[t2], op0=ALU.mult)
    sch.call("vector", "memset", ones_col.ap(), 1.0)
    sch.call("vector", "memset", ones_row.ap(), 1.0)
    tok_ones = sch.call("vector", "memset", states.ap(), 0.0, inc=VS)
    gi = sch.call("gpsimd", "iota", ident.ap(), [[0, 128]],
                  channel_multiplier=1,
                  allow_small_or_imprecise_dtypes=True, wait=[t1])
    gi = sch.call("gpsimd", "iota", tmp_init.ap()[:, 0:128], [[1, 128]],
                  channel_multiplier=0,
                  allow_small_or_imprecise_dtypes=True, inc=GS)
    tok_ident = sch.call("vector", "tensor_tensor", ident.ap(), ident.ap(),
                         tmp_init.ap()[:, 0:128], wait=[gi],
                         op=ALU.is_equal, inc=VS)

    run_abc = "A" in phases
    a_scope = ExitStack()
    CHA = 512
    NCHA = L // CHA
    wa_in = salloc(a_scope, "wa_in", [128, 8 * 2048])
    xa = salloc(a_scope, "xa", [128, 2 * 8 * CHA])
    acta = salloc(a_scope, "acta", [128, NT * CHA])
    hist = salloc(a_scope, "hist", [128, B * NT * 4])
    sqa = salloc(a_scope, "sqa", [128, CHA])
    csum = salloc(a_scope, "csum", [1, 2048])

    mean_r = rows.ap()[:, 0:CHA]
    var_r = rows.ap()[:, CHA:2 * CHA]
    m2_r = rows.ap()[:, 2 * CHA:3 * CHA]
    rstd_r = rows.ap()[:, 3 * CHA:4 * CHA]
    brow_r = rows.ap()[:, 4 * CHA:5 * CHA]

    itersA = [(b, c) for c in range(NCHA) for b in range(B)]
    iters = [(b, c) for c in range(NCH) for b in range(B)]
    xld = {}
    prev = {"stores": None, "tdone": None, "rows_S": None, "stats_rd": None}

    sch.call("vector", "memset", hist.ap(), 0.0, inc=VS)

    for half in (range(2) if run_abc else []):
        fence("sync", LD)
        for k in range(8):
            load(wa_in.ap()[:, k * 2048:(k + 1) * 2048],
                 d_inWT[k * 128:(k + 1) * 128, half * 2048:(half + 1) * 2048],
                 wait=[prev["tdone"]])
        tok_win = (LD, LD.n)

        tok_csum = None
        for cc in range(4):
            for k in range(8):
                sch.call("tensor", "matmul", pb[0].ap()[0:1, 0:512],
                         ones_col.ap(),
                         wa_in.ap()[:, k * 2048 + cc * 512:
                                    k * 2048 + (cc + 1) * 512],
                         wait=[tok_win, bank_free[0], tok_ones],
                         start=(k == 0), stop=(k == 7))
            tcs = sch.mark("tensor", TS)
            bank_free[0] = sch.call("scalar", "copy",
                                    csum.ap()[:, cc * 512:(cc + 1) * 512],
                                    pb[0].ap()[0:1, 0:512], wait=[tcs], inc=SS)
        tok_csum = bank_free[0]
        prev["stats_rd"] = tok_csum

        def load_x(it):
            b, c = itersA[it]
            buf = it % 2
            fence("sync", LD)
            load(xa.ap()[:, buf * 8 * CHA:(buf + 1) * 8 * CHA],
                 dx[b, :, c * CHA:(c + 1) * CHA]
                 .rearrange("(k p) j -> p k j", p=128),
                 wait=[prev["tdone"]])
            xld[it] = (LD, LD.n)

        load_x(0)

        for it, (b, c) in enumerate(itersA):
            buf = it % 2
            xt = [xa.ap()[:, (buf * 8 + k) * CHA:(buf * 8 + k + 1) * CHA]
                  for k in range(8)]
            if it + 1 < len(itersA):
                load_x(it + 1)

            sq_prev = None
            tok_pb0 = None
            for k in range(8):
                stok = sch.call("scalar", "activation", sqa.ap(), xt[k],
                                AF.Square,
                                wait=[xld[it], prev["stats_rd"], sq_prev],
                                inc=SS)
                sch.call("tensor", "matmul", pb[0].ap()[0:1, 0:CHA],
                         ones_col.ap(), xt[k],
                         wait=[xld[it], prev["stats_rd"], tok_ones],
                         start=(k == 0), stop=(k == 7))
                if k == 7:
                    tok_pb0 = sch.mark("tensor", TS)
                sq_prev = sch.call("tensor", "matmul", pb[1].ap()[0:1, 0:CHA],
                                   ones_col.ap(), sqa.ap(), wait=[stok],
                                   start=(k == 0), stop=(k == 7), inc=TS)
            tstat = sq_prev

            sch.call("vector", "tensor_scalar", mean_r, pb[0].ap()[0:1, 0:CHA],
                     1.0 / DM, None,
                     wait=[tstat, tok_pb0, prev["tdone"], prev["rows_S"]],
                     op0=ALU.mult)
            sch.call("vector", "tensor_scalar", var_r, pb[1].ap()[0:1, 0:CHA],
                     1.0 / DM, None, op0=ALU.mult)
            sch.call("vector", "tensor_tensor", m2_r, mean_r, mean_r,
                     op=ALU.mult)
            sch.call("vector", "tensor_tensor", var_r, var_r, m2_r,
                     op=ALU.subtract)
            stats_rd = sch.call("vector", "tensor_scalar", var_r, var_r, EPS,
                                None, op0=ALU.add, inc=VS)
            prev["stats_rd"] = stats_rd
            sl = sch.call("scalar", "activation", rstd_r, var_r, AF.Ln,
                          wait=[stats_rd])
            sl = sch.call("scalar", "activation", rstd_r, rstd_r, AF.Exp,
                          scale=-0.5, inc=SS)
            sch.call("vector", "tensor_tensor", brow_r, mean_r, rstd_r,
                     wait=[sl], op=ALU.mult)
            rows_done = sch.call("vector", "tensor_scalar", brow_r, brow_r,
                                 -1.0, None, op0=ALU.mult, inc=VS)
            prev["rows_S"] = sl
            tbr = sch.call("tensor", "matmul", pb[2].ap()[:, 0:CHA],
                           ones_row.ap(), rstd_r,
                           wait=[rows_done, tok_ident, bank_free[2]],
                           start=True, stop=True, inc=TS)
            xh_tok = None
            for k in range(8):
                xh_tok = sch.call("vector", "tensor_tensor", xt[k], xt[k],
                                  pb[2].ap()[:, 0:CHA], wait=[tbr],
                                  op=ALU.mult, inc=VS)
            bank_free[2] = xh_tok

            for t in range(NT):
                bi = 3 + (t % 2)
                for k in range(8):
                    sch.call("tensor", "matmul", pb[bi].ap()[:, 0:CHA],
                             wa_in.ap()[:, k * 2048 + t * 128:
                                        k * 2048 + (t + 1) * 128],
                             xt[k],
                             wait=[xh_tok, tok_win, bank_free[bi]],
                             start=(k == 0), stop=False)
                tmm = sch.call("tensor", "matmul", pb[bi].ap()[:, 0:CHA],
                               csum.ap()[:, t * 128:(t + 1) * 128], brow_r,
                               wait=[tok_csum, rows_done],
                               start=False, stop=True, inc=TS)
                aap = acta.ap()[:, t * CHA:(t + 1) * CHA]
                if half == 0:
                    hap = hist.ap()[:, (b * NT + t) * 4:(b * NT + t) * 4 + 4]
                    wcol = [convw_c.ap()[:, t * KC + j:t * KC + j + 1]
                            for j in range(KC)]
                    sch.call("vector", "tensor_scalar", aap,
                             pb[bi].ap()[:, 0:CHA], wcol[3], None,
                             wait=[tmm, prev["stores"], prev["tdone"]],
                             op0=ALU.mult)
                    for j in range(3):
                        sh = 3 - j
                        sch.call("vector", "scalar_tensor_tensor",
                                 aap[:, sh:CHA], pb[bi].ap()[:, 0:CHA - sh],
                                 wcol[j], aap[:, sh:CHA],
                                 op0=ALU.mult, op1=ALU.add)
                        sch.call("vector", "scalar_tensor_tensor",
                                 aap[:, 0:sh], hap[:, 1 + j:1 + j + sh],
                                 wcol[j], aap[:, 0:sh],
                                 op0=ALU.mult, op1=ALU.add)
                    vconv = sch.call("vector", "tensor_copy", hap[:, 1:4],
                                     pb[bi].ap()[:, CHA - 3:CHA], inc=VS)
                    bank_free[bi] = vconv
                    su = sch.call("scalar", "activation", aap, aap, AF.Silu,
                                  bias=convb_c.ap()[:, t:t + 1],
                                  wait=[vconv], inc=SS)
                    sch.call("tensor", "matmul", pb[5].ap()[0:96, 0:CHA],
                             w_xproj[t].ap(), aap,
                             wait=[su, tok_small, bank_free[5]],
                             start=(t == 0), stop=(t == NT - 1))
                    a_st_tok = su
                else:
                    stok = sch.call("scalar", "activation", aap,
                                    pb[bi].ap()[:, 0:CHA], AF.Silu,
                                    wait=[tmm, prev["stores"],
                                          prev["tdone"]], inc=SS)
                    bank_free[bi] = stok
                    a_st_tok = stok
            dstA = scr_u if half == 0 else scr_sz
            store(dstA[b, :, c * CHA:(c + 1) * CHA]
                  .rearrange("(k p) j -> p k j", p=128),
                  acta.ap(), wait=[a_st_tok])
            txp = sch.mark("tensor", TS)
            if half == 0:
                bank_free[5] = sch.call(
                    "scalar", "copy",
                    x_dbl.ap()[:, b * L + c * CHA:b * L + (c + 1) * CHA],
                    pb[5].ap()[0:96, 0:CHA], wait=[txp], inc=SS)
            prev["stores"] = (ST, ST.n)
            prev["tdone"] = txp
            fence("gpsimd", ST)

    a_scope.close()
    A_done_S = (SS, SS.n)
    A_done_V = (VS, VS.n)
    A_done_T = (TS, TS.n)
    A_done_ST = (ST, ST.n)

    bi_scope = ExitStack()
    xzm = salloc(bi_scope, "xzm", [128, L])
    xum = salloc(bi_scope, "xum", [64, L])
    szm = salloc(bi_scope, "szm", [64, L])
    xdm = salloc(bi_scope, "xdm", [12, L])
    xbcm = salloc(bi_scope, "xbcm", [1, L])
    dmm = salloc(bi_scope, "dmm", [64, L])
    dtms = salloc(bi_scope, "dtms", [64, L])
    tma = salloc(bi_scope, "tma", [64, L])
    tmb = salloc(bi_scope, "tmb", [64, L])
    w_m = salloc(bi_scope, "w_m", [64, L])
    bcm = salloc(bi_scope, "bcm", [64, L])
    sm = salloc(bi_scope, "sm", [64, L])
    ym = salloc(bi_scope, "ym", [64, L])

    def mm512(dst_sb, lhsT, rhs_base_ap, n_cc, psum_rows, wait=(), kname=None):
        toks = []
        for cc in range(n_cc):
            tmmi = sch.call("tensor", "matmul",
                            pb[6].ap()[0:psum_rows, 0:512], lhsT,
                            rhs_base_ap[:, cc * 512:(cc + 1) * 512],
                            wait=list(wait) + [bank_free[6]],
                            start=True, stop=True, inc=TS)
            bank_free[6] = sch.call("scalar", "copy",
                                    dst_sb[:, cc * 512:(cc + 1) * 512],
                                    pb[6].ap()[0:psum_rows, 0:512],
                                    wait=[tmmi], inc=SS)
        return bank_free[6]

    tok_dtm = None
    for b in (range(B) if run_abc else []):
        xdb = x_dbl.ap()[0:64, b * L:(b + 1) * L]
        tk = mm512(xzm.ap(), w_minW.ap(), xdb, 4, 128,
                   wait=[A_done_S, tok_small])
        x_m = xzm.ap()[0:64, :]
        z_m = xzm.ap()[64:128, :]
        w0 = m_convw.ap()[0:64, 0:1]
        w1 = m_convw.ap()[0:64, 1:2]
        sch.call("vector", "tensor_scalar", xum.ap(), x_m, w1, None,
                 wait=[tk], op0=ALU.mult)
        vtk = sch.call("vector", "scalar_tensor_tensor", xum.ap()[:, 1:L],
                       x_m[:, 0:L - 1], w0, xum.ap()[:, 1:L],
                       op0=ALU.mult, op1=ALU.add, inc=VS)
        stk = sch.call("scalar", "activation", xum.ap(), xum.ap(), AF.Silu,
                       bias=m_convb.ap()[0:64, :], wait=[vtk], inc=SS)
        szk = sch.call("scalar", "activation", szm.ap(), z_m, AF.Silu,
                       wait=[tk], inc=SS)
        tk = mm512(xdm.ap(), w_mxproj.ap(), xum.ap(), 4, 12, wait=[stk])
        tok_xdm = tk
        tk = mm512(dmm.ap(), w_mdt.ap(), xdm.ap()[0:4, :], 4, 64, wait=[tk])
        sch.call("scalar", "activation", tma.ap(), dmm.ap(), AF.Abs,
                 bias=m_dtb.ap()[0:64, :], wait=[tk])
        sch.call("scalar", "activation", tma.ap(), tma.ap(), AF.Exp,
                 scale=-1.0)
        sch.call("scalar", "add", tma.ap(), tma.ap(), 1.0)
        sch.call("scalar", "activation", tma.ap(), tma.ap(), AF.Ln)
        ssp = sch.call("scalar", "activation", tmb.ap(), dmm.ap(), AF.Relu,
                       bias=m_dtb.ap()[0:64, :], inc=SS)
        vdt = sch.call("vector", "tensor_add", dtms.ap(), tma.ap(), tmb.ap(),
                       wait=[ssp])
        vw = sch.call("vector", "tensor_mul", w_m.ap(), dtms.ap(), xum.ap(),
                      inc=VS)
        vym = sch.call("vector", "tensor_scalar", ym.ap(), xum.ap(),
                       m_dvec.ap()[0:64, :], None, op0=ALU.mult, inc=VS)
        vsc = None
        vym_prev = vym
        tkb_prev = None
        for n in range(MN):
            sda = sch.call("scalar", "activation", tma.ap(), dtms.ap(),
                           AF.Exp, scale=m_aneg.ap()[0:64, n:n + 1],
                           wait=[vw, vsc], inc=SS)
            fence("sync", LD)
            gb = load(xbcm.ap(), xdm.ap()[4 + n:5 + n, :],
                      wait=[tok_xdm, tkb_prev])
            tkb = mm512(bcm.ap(), ones_row.ap()[0:1, 0:64],
                        xbcm.ap(), 4, 64, wait=[vw, vym_prev, gb])
            vdb = sch.call("vector", "tensor_mul", tmb.ap(), w_m.ap(),
                           bcm.ap(), wait=[tkb])
            vsc = sch.call("vector", "tensor_tensor_scan", sm.ap(), tma.ap(),
                           tmb.ap(), 0.0, wait=[sda],
                           op0=ALU.mult, op1=ALU.add, inc=VS)
            fence("sync", LD)
            gc = load(xbcm.ap(), xdm.ap()[8 + n:9 + n, :],
                      wait=[tok_xdm, tkb])
            tkb_prev = tkc = mm512(bcm.ap(), ones_row.ap()[0:1, 0:64],
                                   xbcm.ap(), 4, 64, wait=[vsc, gc])
            sch.call("vector", "tensor_mul", sm.ap(), sm.ap(), bcm.ap(),
                     wait=[tkc])
            vym_prev = vym = sch.call("vector", "tensor_add", ym.ap(),
                                      ym.ap(), sm.ap(), inc=VS)
        vyg = sch.call("vector", "tensor_mul", w_m.ap(), ym.ap(), szm.ap(),
                       wait=[szk], inc=VS)
        tok_dtm = mm512(dtm_sb.ap()[:, b * L:(b + 1) * L], w_mout.ap(),
                        w_m.ap(), 4, 64, wait=[vyg])
    bi_scope.close()

    bm_scope = ExitStack()
    ub = salloc(bm_scope, "ub", [128, NT * CH])
    bc_sb = salloc(bm_scope, "bc_sb", [128, 2 * NST * CH])
    s3 = salloc(bm_scope, "s3", [128, NST, CH])
    da3 = salloc(bm_scope, "da3", [128, 2, NST, CH])
    dbu_b = salloc(bm_scope, "dbu_b", [128, CH])
    dt4 = salloc(bm_scope, "dt4", [128, 2, CH])
    w4 = salloc(bm_scope, "w4", [128, 4, CH])
    t14 = salloc(bm_scope, "t14", [128, 2, CH])
    t24 = salloc(bm_scope, "t24", [128, 2, CH])
    yg_sb = salloc(bm_scope, "yg_sb", [128, 2 * CH])
    xbc = salloc(bm_scope, "xbc", [1, 2 * NST * CH])

    ut_tok = {}
    ztoks = {}
    da_t = {}
    pd_free = [A_done_V, A_done_V]
    py_free = [A_done_V, A_done_V]
    yg_st = [None, None]
    last_z = (VS, VS.n)
    BI = len(iters)

    def u_slice(t):
        return ub.ap()[:, t * CH:(t + 1) * CH]

    def emit_u_group(it, t0, wait_tok):
        b, c = iters[it]
        fence("sync", LD)
        load(ub.ap()[:, t0 * CH:(t0 + 4) * CH],
             scr_u[b, t0 * 128:(t0 + 4) * 128, c * CH:(c + 1) * CH]
             .rearrange("(k p) j -> p k j", p=128),
             wait=[A_done_ST, wait_tok])
        tok = (LD, LD.n)
        for t in range(t0, t0 + 4):
            ut_tok[(it, t)] = tok

    for g4 in range(4):
        emit_u_group(0, 4 * g4, None)
    fence("sync", LD)
    b0, c0 = iters[0]
    for n in range(2 * NST):
        load(xbc.ap()[:, n * CH:(n + 1) * CH],
             x_dbl.ap()[64 + n:65 + n, b0 * L + c0 * CH:b0 * L + (c0 + 1) * CH],
             wait=[A_done_S])
    gld = {0: (LD, LD.n)}
    tok_bc = {}
    pend = []

    def flush_pend(tok, nxt_it):
        while pend:
            pit, ps, ygap = pend.pop(0)
            pb_, pc_ = iters[pit]
            yg_st[ps % 2] = store(
                scr_yg[pb_, ps * 128:(ps + 1) * 128,
                       pc_ * CH:(pc_ + 1) * CH], ygap, wait=[tok])
            if nxt_it is not None and ps % 4 == 3:
                emit_u_group(nxt_it, ps - 3, tok)

    def emit_scans(it, s):
        b, c = iters[it]
        par = s % 2
        g = it * NT + s
        dab = da3.ap()[:, par, :, :]
        sti = states.ap()[:, (b * NT + s) * NST:(b * NT + s + 1) * NST]
        for n in range(NST):
            sch.call("vector", "tensor_mul", dbu_b.ap(),
                     w4.ap()[:, s % 4, :],
                     bc_sb.ap()[:, n * CH:(n + 1) * CH],
                     wait=[da_t[g], tok_bc[it]])
            sch.call("vector", "tensor_tensor_scan", s3.ap()[:, n, :],
                     dab[:, n, :], dbu_b.ap(), sti[:, n:n + 1],
                     op0=ALU.mult, op1=ALU.add)
        sch.call("vector", "tensor_copy", sti, s3.ap()[:, :, CH - 1])
        for n in range(NST):
            sch.call("vector", "tensor_mul", s3.ap()[:, n, :],
                     s3.ap()[:, n, :],
                     bc_sb.ap()[:, (NST + n) * CH:(NST + n + 1) * CH])
        ztoks[g] = tok_z = sch.mark("vector", VS)
        nxt = it + 1 if it + 1 < BI else None
        flush_pend(tok_z, nxt)
        py = pb[2 + par]
        for n in range(NST):
            sch.call("tensor", "matmul", py.ap()[:, 0:CH], ident.ap(),
                     s3.ap()[:, n, :], wait=[tok_z, py_free[par]],
                     start=(n == 0), stop=(n == NST - 1))
        tok_y = sch.mark("tensor", TS)
        ygap = yg_sb.ap()[:, par * CH:(par + 1) * CH]
        vev = sch.call("vector", "scalar_tensor_tensor", ygap, u_slice(s),
                       dvec_c.ap()[:, s:s + 1], py.ap()[:, 0:CH],
                       wait=[tok_y, yg_st[par]],
                       op0=ALU.mult, op1=ALU.add, inc=VS, raw=True)
        py_free[par] = vev
        pend.append((it, s, ygap))

    for it, (b, c) in (enumerate(iters) if run_abc else []):
        bcdone = None
        for n in range(2 * NST):
            bi = 4 + (n % 2)
            tmmi = sch.call("tensor", "matmul", pb[bi].ap()[:, 0:CH],
                            ones_row.ap(),
                            xbc.ap()[:, n * CH:(n + 1) * CH],
                            wait=[gld[it], bank_free[bi], tok_ident],
                            start=True, stop=True, inc=TS)
            bank_free[bi] = sch.call("scalar", "copy",
                                     bc_sb.ap()[:, n * CH:(n + 1) * CH],
                                     pb[bi].ap()[:, 0:CH],
                                     wait=[tmmi, last_z], inc=SS)
            bcdone = tmmi
        tok_bc[it] = bank_free[5]
        if it + 1 < BI:
            bn, cn = iters[it + 1]
            fence("sync", LD)
            for n in range(2 * NST):
                load(xbc.ap()[:, n * CH:(n + 1) * CH],
                     x_dbl.ap()[64 + n:65 + n,
                                bn * L + cn * CH:bn * L + (cn + 1) * CH],
                     wait=[bcdone])
            gld[it + 1] = (LD, LD.n)

        sp_t = {}

        def emit_dsp(t):
            par = t % 2
            g = it * NT + t
            pd = pb[par]
            tok_pd = sch.call("tensor", "matmul", pd.ap()[:, 0:CH],
                              w_dtow.ap()[:, t * 128:(t + 1) * 128],
                              dtm_sb.ap()[:, b * L + c * CH:
                                          b * L + (c + 1) * CH],
                              wait=[tok_dtm, pd_free[par]],
                              start=True, stop=True, inc=TS)
            bcol = dtob_c.ap()[:, t:t + 1]
            t1 = t14.ap()[:, par, :]
            t2 = t24.ap()[:, par, :]
            sch.call("scalar", "activation", t1, pd.ap()[:, 0:CH],
                     AF.Abs, bias=bcol, wait=[tok_pd, ztoks.get(g - 4)])
            sch.call("scalar", "activation", t1, t1, AF.Exp, scale=-1.0)
            sch.call("scalar", "add", t1, t1, 1.0)
            sch.call("scalar", "activation", t1, t1, AF.Ln)
            sp_t[t] = sch.call("scalar", "activation", t2, pd.ap()[:, 0:CH],
                               AF.Relu, bias=bcol, inc=SS)
            pd_free[par] = sp_t[t]

        emit_dsp(0)
        for t in range(NT):
            par = t % 2
            p4 = t % 4
            g = it * NT + t
            if t + 1 < NT:
                emit_dsp(t + 1)
            sch.call("vector", "tensor_add", dt4.ap()[:, par, :],
                     t14.ap()[:, par, :], t24.ap()[:, par, :],
                     wait=[sp_t[t], ut_tok[(it, t)], da_t.get(g - 2)])
            wtok = sch.call("vector", "tensor_mul", w4.ap()[:, p4, :],
                            dt4.ap()[:, par, :], u_slice(t),
                            inc=VS if t < 2 else None)
            if t >= 2:
                emit_scans(it, t - 2)
            dab = da3.ap()[:, par, :, :]
            for n in range(NST):
                sch.call("scalar", "activation", dab[:, n, :],
                         dt4.ap()[:, par, :], AF.Exp,
                         scale=a_neg.ap()[:, t * NST + n:t * NST + n + 1],
                         wait=[ztoks.get(g - 2), wtok])
            da_t[g] = sch.mark("scalar", SS)
        emit_scans(it, NT - 2)
        emit_scans(it, NT - 1)
        ztoks[it * NT + NT] = fin = sch.call("vector", "engine_nop",
                                             inc=VS)
        flush_pend(fin, it + 1 if it + 1 < BI else None)
        last_z = fin
        fence("gpsimd", ST)

    bm_scope.close()
    B_done_V = (VS, VS.n)
    B_done_S = (SS, SS.n)
    B_done_T = (TS, TS.n)
    B_done_ST = (ST, ST.n)

    if dbg and run_abc:
        fence("gpsimd", ST)
        store(d_xdbl, x_dbl.ap(), wait=[A_done_S])
        store(d_dtm, dtm_sb.ap(), wait=[tok_dtm])
        fence("gpsimd", ST)

    c_scope = ExitStack()
    mean_r = rows.ap()[:, 0:CH]
    var_r = rows.ap()[:, CH:2 * CH]
    m2_r = rows.ap()[:, 2 * CH:3 * CH]
    rstd_r = rows.ap()[:, 3 * CH:4 * CH]
    brow_r = rows.ap()[:, 4 * CH:5 * CH]
    wop = salloc(c_scope, "wop", [128, NT * DM])
    ygc = salloc(c_scope, "ygc", [128, NT * CH])
    szc = salloc(c_scope, "szc", [128, NT * CH])
    xc = salloc(c_scope, "xc", [128, 8 * CH])
    oc = salloc(c_scope, "oc", [128, 8 * CH])
    sqc = salloc(c_scope, "sqc", [128, CH])

    fence("sync", LD)
    for k in range(NT):
        load(wop.ap()[:, k * DM:(k + 1) * DM],
             d_opWT[k * 128:(k + 1) * 128, :], wait=[B_done_T])
    tok_wop = (LD, LD.n)

    prevC = {"vdone": B_done_V, "tdone": B_done_T, "sdone": B_done_S,
             "stores": B_done_ST, "rows_S": None, "stats_rd": None,
             "rowsT": None}

    for it, (b, c) in (enumerate(iters) if run_abc else []):
        fence("sync", LD)
        load(ygc.ap(),
             scr_yg[b, :, c * CH:(c + 1) * CH]
             .rearrange("(k p) j -> p k j", p=128),
             wait=[B_done_ST, prevC["vdone"], prevC["tdone"]])
        load(szc.ap(),
             scr_sz[b, :, c * CH:(c + 1) * CH]
             .rearrange("(k p) j -> p k j", p=128))
        load(xc.ap(),
             dx[b, :, c * CH:(c + 1) * CH]
             .rearrange("(k p) j -> p k j", p=128))
        cld = (LD, LD.n)

        for t in range(NT):
            sch.call("vector", "tensor_mul", ygc.ap()[:, t * CH:(t + 1) * CH],
                     ygc.ap()[:, t * CH:(t + 1) * CH],
                     szc.ap()[:, t * CH:(t + 1) * CH],
                     wait=[cld, prevC["tdone"]])
        tok_yf = sch.mark("vector", VS)

        sq_prev = None
        tok_pb2 = None
        for o in range(8):
            bi = o % 2
            for k in range(NT):
                sch.call("tensor", "matmul", pb[bi].ap()[:, 0:CH],
                         wop.ap()[:, k * DM + o * 128:k * DM + (o + 1) * 128],
                         ygc.ap()[:, k * CH:(k + 1) * CH],
                         wait=[tok_yf, tok_wop, bank_free[bi]],
                         start=(k == 0), stop=(k == NT - 1))
            tmm = sch.mark("tensor", TS)
            ocap = oc.ap()[:, o * CH:(o + 1) * CH]
            bank_free[bi] = sch.call("scalar", "copy", ocap,
                                     pb[bi].ap()[:, 0:CH],
                                     wait=[tmm, prevC["vdone"]], inc=SS)
            stok = sch.call("scalar", "activation", sqc.ap(), ocap, AF.Square,
                            wait=[sq_prev], inc=SS)
            sch.call("tensor", "matmul", pb[2].ap()[0:1, 0:CH], ones_col.ap(),
                     ocap, wait=[bank_free[bi], prevC["stats_rd"]],
                     start=(o == 0), stop=(o == 7))
            if o == 7:
                tok_pb2 = sch.mark("tensor", TS)
            sq_prev = sch.call("tensor", "matmul", pb[3].ap()[0:1, 0:CH],
                               ones_col.ap(), sqc.ap(),
                               wait=[stok, prevC["stats_rd"]],
                               start=(o == 0), stop=(o == 7), inc=TS)
        sch.call("vector", "tensor_scalar", mean_r, pb[2].ap()[0:1, 0:CH],
                 1.0 / DM, None, wait=[tok_pb2, sq_prev, prevC["rowsT"],
                                       prevC["rows_S"]], op0=ALU.mult)
        sch.call("vector", "tensor_scalar", var_r, pb[3].ap()[0:1, 0:CH],
                 1.0 / DM, None, op0=ALU.mult)
        sch.call("vector", "tensor_tensor", m2_r, mean_r, mean_r, op=ALU.mult)
        sch.call("vector", "tensor_tensor", var_r, var_r, m2_r,
                 op=ALU.subtract)
        stats_rd = sch.call("vector", "tensor_scalar", var_r, var_r, EPS,
                            None, op0=ALU.add, inc=VS)
        prevC["stats_rd"] = stats_rd
        sl = sch.call("scalar", "activation", rstd_r, var_r, AF.Ln,
                      wait=[stats_rd])
        sl = sch.call("scalar", "activation", rstd_r, rstd_r, AF.Exp,
                      scale=-0.5, inc=SS)
        sch.call("vector", "tensor_tensor", brow_r, mean_r, rstd_r,
                 wait=[sl], op=ALU.mult)
        rows_done = sch.call("vector", "tensor_scalar", brow_r, brow_r, -1.0,
                             None, op0=ALU.mult, inc=VS)
        prevC["rows_S"] = sl
        tb1 = sch.call("tensor", "matmul", pb[4].ap()[:, 0:CH],
                       ones_row.ap(), rstd_r,
                       wait=[rows_done, bank_free[4]],
                       start=True, stop=True)
        tb2 = sch.call("tensor", "matmul", pb[5].ap()[:, 0:CH],
                       ones_row.ap(), brow_r,
                       wait=[rows_done, bank_free[5]],
                       start=True, stop=True, inc=TS)
        prevC["rowsT"] = tb2
        for o in range(8):
            ocap = oc.ap()[:, o * CH:(o + 1) * CH]
            sch.call("vector", "tensor_tensor", ocap, ocap,
                     pb[4].ap()[:, 0:CH], wait=[tb2, prevC["stores"]],
                     op=ALU.mult)
            sch.call("vector", "tensor_tensor", ocap, ocap,
                     pb[5].ap()[:, 0:CH], op=ALU.add)
            vx2 = sch.call("vector", "tensor_tensor", ocap, ocap,
                           xc.ap()[:, o * CH:(o + 1) * CH], op=ALU.add,
                           inc=VS)
        store(scr_x2[b, :, c * CH:(c + 1) * CH]
              .rearrange("(o p) j -> p o j", p=128),
              oc.ap(), wait=[vx2])
        bank_free[4] = bank_free[5] = (VS, VS.n)
        prevC["vdone"] = (VS, VS.n)
        prevC["tdone"] = (TS, TS.n)
        prevC["stores"] = (ST, ST.n)
        fence("gpsimd", ST)

    c_scope.close()
    mk_scope.close()
    C_done_V = (VS, VS.n)
    C_done_S = (SS, SS.n)
    C_done_T = (TS, TS.n)
    C_done_ST = (ST, ST.n)

    f0_scope = ExitStack()
    wt0 = salloc(f0_scope, "wt0", [128, 2 * 512])
    absb = salloc(f0_scope, "absb", [128, 1024])
    ones128 = salloc(f0_scope, "ones128", [128, 128])
    colacc = salloc(f0_scope, "colacc", [128, 8])
    tcol = salloc(f0_scope, "tcol", [128, 1])
    mcol = salloc(f0_scope, "mcol", [128, 1])
    q32a = salloc(f0_scope, "q32a", [128, 1024])
    q32b = salloc(f0_scope, "q32b", [128, 1024])
    wqb = salloc(f0_scope, "wqb", [128, 1024], BF16)
    wq_c = salloc(f0_scope, "wq_c", [128, 2 * 3])

    mats = [(d_gWT, 8, 8, DM * 4096), (d_uWT, 8, 8, DM * 4096),
            (d_dWT, 32, 2, DM * 4096)]
    wq_kind = "ExternalOutput" if dbg else "Internal"
    scr_wq = [nc.dram_tensor("scr_wgq", [DM, 4096], BF16, kind=wq_kind).ap(),
              nc.dram_tensor("scr_wuq", [DM, 4096], BF16, kind=wq_kind).ap(),
              nc.dram_tensor("scr_wdq", [4096, DM], BF16, kind=wq_kind).ap()]

    prevF0 = {"s": C_done_S, "v": C_done_V, "t": C_done_T, "v2": None}
    sch.call("vector", "memset", ones128.ap(), 1.0, wait=[C_done_T], inc=VS)
    for mi, (dsrc, nk, ncc, nel) in enumerate(mats):
        vz = sch.call("vector", "memset", colacc.ap(), 0.0,
                      wait=[prevF0["t"]], inc=VS)
        sab_prev = None
        vac = None
        for k in range(nk):
            for cc in range(ncc // 2):
                fence("sync", LD)
                wl = load(wt0.ap(),
                          dsrc[k * 128:(k + 1) * 128,
                               cc * 1024:(cc + 1) * 1024],
                          wait=[prevF0["s"], prevF0["t"], prevF0["v2"],
                                vac, sab_prev])
                for hh in range(2):
                    sab = sch.call("scalar", "activation",
                                   absb.ap()[:, hh * 512:(hh + 1) * 512],
                                   wt0.ap()[:, hh * 512:(hh + 1) * 512],
                                   AF.Abs, wait=[wl, vac], inc=SS)
                    sab_prev = sab
                    sch.call("vector", "tensor_reduce", tcol.ap(),
                             absb.ap()[:, hh * 512:(hh + 1) * 512],
                             AX.X, ALU.add, wait=[sab])
                    vac = sch.call("vector", "tensor_add",
                                   colacc.ap()[:, 0:1],
                                   colacc.ap()[:, 0:1], tcol.ap(), inc=VS)
        tbt = sch.call("tensor", "matmul", pb[3].ap()[:, 0:8], ones128.ap(),
                       colacc.ap(), wait=[vac, bank_free[3]],
                       start=True, stop=True, inc=TS)
        smc = sch.call("scalar", "copy", mcol.ap(), pb[3].ap()[:, 0:1],
                       wait=[tbt], inc=SS)
        bank_free[3] = smc
        sch.call("vector", "tensor_scalar", mcol.ap(), mcol.ap(), 1.0 / nel,
                 None, wait=[smc], op0=ALU.mult)
        vmc = sch.call("vector", "tensor_scalar", mcol.ap(), mcol.ap(), 1e-5,
                       None, op0=ALU.max, inc=VS)
        sch.call("vector", "tensor_copy", cells.ap()[:, mi:mi + 1],
                 mcol.ap()[0:1, 0:1])
        sch.call("vector", "tensor_scalar", wq_c.ap()[:, 2 * mi:2 * mi + 1],
                 mcol.ap(), 0.5, None, op0=ALU.mult)
        vcn = sch.call("vector", "tensor_scalar",
                       wq_c.ap()[:, 2 * mi + 1:2 * mi + 2],
                       wq_c.ap()[:, 2 * mi:2 * mi + 1], -0.5 / 0.5, None,
                       op0=ALU.mult, inc=VS)
        vq_prev = vcn
        st_prev = None
        for k in range(nk):
            for cc in range(ncc // 2):
                fence("sync", LD)
                wl = load(wt0.ap(),
                          dsrc[k * 128:(k + 1) * 128,
                               cc * 1024:(cc + 1) * 1024], wait=[vq_prev])
                sch.call("vector", "tensor_scalar", q32a.ap(), wt0.ap(),
                         wq_c.ap()[:, 2 * mi:2 * mi + 1], None,
                         wait=[wl, vcn, st_prev], op0=ALU.is_gt)
                sch.call("vector", "tensor_scalar", q32b.ap(), wt0.ap(),
                         wq_c.ap()[:, 2 * mi + 1:2 * mi + 2], None,
                         op0=ALU.is_lt)
                vq_prev = sch.call("vector", "tensor_tensor", wqb.ap(),
                                   q32a.ap(), q32b.ap(), op=ALU.subtract,
                                   inc=VS)
                st_prev = store(scr_wq[mi][k * 128:(k + 1) * 128,
                                           cc * 1024:(cc + 1) * 1024],
                                wqb.ap(), wait=[vq_prev])
                fence("gpsimd", ST)
        prevF0["s"] = (SS, SS.n)
        prevF0["t"] = (TS, TS.n)
        prevF0["v2"] = vq_prev
    f0_scope.close()
    F0_done_V = (VS, VS.n)
    F0_done_ST = (ST, ST.n)

    f_scope = ExitStack()
    TB = 2 * CH
    x2blk = salloc(f_scope, "x2blk", [128, 8 * TB])
    xh3 = salloc(f_scope, "xh3", [128, 8 * TB])
    xq_b = salloc(f_scope, "xq_b", [128, 8 * TB], BF16)
    hp = salloc(f_scope, "hp", [128, 32 * TB])
    hq_b = salloc(f_scope, "hq_b", [128, 32 * TB], BF16)
    fc = salloc(f_scope, "fc", [128, 8 * TB])
    wgq_sb = salloc(f_scope, "wgq_sb", [128, 2 * 8 * 128], BF16)
    wuq_sb = salloc(f_scope, "wuq_sb", [128, 2 * 8 * 128], BF16)
    wdq_sb = salloc(f_scope, "wdq_sb", [128, 2 * 32 * 128], BF16)
    gdt = salloc(f_scope, "gdt", [128, TB])
    sgt = salloc(f_scope, "sgt", [128, TB])
    i32b = salloc(f_scope, "i32b", [128, TB], I32)
    rmx = salloc(f_scope, "rmx", [128, TB])
    sq_f = salloc(f_scope, "sq_f", [128, TB])
    bcf = salloc(f_scope, "bcf", [128, 4 * TB])
    f_rows = salloc(f_scope, "f_rows", [1, 5 * TB])

    BCR, BCB, BCSX, BCIG = range(4)
    BCSH, BCDR = 0, 1

    def bcf_s(i):
        return bcf.ap()[:, i * TB:(i + 1) * TB]

    mean_f = f_rows.ap()[:, 0 * TB:1 * TB]
    var_f = f_rows.ap()[:, 1 * TB:2 * TB]
    rstd_f = f_rows.ap()[:, 2 * TB:3 * TB]
    brow_f = f_rows.ap()[:, 3 * TB:4 * TB]
    aux_f = f_rows.ap()[:, 4 * TB:5 * TB]

    colT = salloc(f_scope, "colT", [128, 8])

    def part_tree_max(src_ap, out_row, wait=()):
        w = list(wait) + [tok_ident]
        tokv = None
        for cb in range(TB // 128):
            bk = pb[4 + (cb % 2)]
            tt = sch.call("tensor", "transpose", bk.ap()[:, 0:128],
                          src_ap[:, cb * 128:(cb + 1) * 128], ident.ap(),
                          wait=w + [bank_free[4 + (cb % 2)]], inc=TS)
            tokv = sch.call("vector", "tensor_reduce", colT.ap()[:, cb:cb + 1],
                            bk.ap()[:, 0:128], AX.X, ALU.max,
                            wait=[tt], inc=VS)
            bank_free[4 + (cb % 2)] = tokv
        toks = None
        for cb in range(TB // 128):
            bk = pb[4 + (cb % 2)]
            tt = sch.call("tensor", "transpose", bk.ap()[0:1, 0:128],
                          colT.ap()[:, cb:cb + 1], ident.ap(),
                          wait=[tokv, bank_free[4 + (cb % 2)]], inc=TS)
            toks = sch.call("scalar", "copy",
                            out_row[:, cb * 128:(cb + 1) * 128],
                            bk.ap()[0:1, 0:128], wait=[tt], inc=SS)
            bank_free[4 + (cb % 2)] = toks
        return toks

    def ln_rows(tok_sum, tok_sumsq, pbs, pbq, prev_rowuser):
        sch.call("vector", "tensor_scalar", mean_f, pbs, 1.0 / DM, None,
                 wait=[tok_sum, tok_sumsq, prev_rowuser], op0=ALU.mult)
        sch.call("vector", "tensor_scalar", var_f, pbq, 1.0 / DM, None,
                 op0=ALU.mult)
        sch.call("vector", "tensor_tensor", aux_f, mean_f, mean_f,
                 op=ALU.mult)
        sch.call("vector", "tensor_tensor", var_f, var_f, aux_f,
                 op=ALU.subtract)
        vrd = sch.call("vector", "tensor_scalar", var_f, var_f, EPS, None,
                       op0=ALU.add, inc=VS)
        sch.call("scalar", "activation", rstd_f, var_f, AF.Ln, wait=[vrd])
        sl = sch.call("scalar", "activation", rstd_f, rstd_f, AF.Exp,
                      scale=-0.5, inc=SS)
        sch.call("vector", "tensor_tensor", brow_f, mean_f, rstd_f,
                 wait=[sl], op=ALU.mult)
        return sch.call("vector", "tensor_scalar", brow_f, brow_f, -1.0,
                        None, op0=ALU.mult, inc=VS), vrd

    def bcast2(r1, r2, wait=()):
        t1b = sch.call("tensor", "matmul", pb[4].ap()[:, 0:TB], ones_row.ap(),
                       r1[0], wait=list(wait) + [bank_free[4]],
                       start=True, stop=True, inc=TS)
        s1b = sch.call("scalar", "copy", bcf_s(r1[1]), pb[4].ap()[:, 0:TB],
                       wait=[t1b, r1[2]], inc=SS)
        bank_free[4] = s1b
        if r2 is None:
            return s1b
        t2b = sch.call("tensor", "matmul", pb[5].ap()[:, 0:TB], ones_row.ap(),
                       r2[0], wait=list(wait) + [bank_free[5]],
                       start=True, stop=True, inc=TS)
        s2b = sch.call("scalar", "copy", bcf_s(r2[1]), pb[5].ap()[:, 0:TB],
                       wait=[t2b, r2[2]], inc=SS)
        bank_free[5] = s2b
        return s2b

    prevF = {"v": F0_done_V, "t": (TS, TS.n), "s": (SS, SS.n),
             "stores": (ST, ST.n), "stats": bank_free[3], "rowuser": None}

    for j in range(NCH):
        fence("sync", LD)
        for bb in range(B):
            load(x2blk.ap().rearrange("p (o z) -> p o z", z=TB)
                 [:, :, bb * CH:(bb + 1) * CH],
                 scr_x2[bb, :, j * CH:(j + 1) * CH]
                 .rearrange("(o p) j -> p o j", p=128),
                 wait=[C_done_ST, prevF["v"], prevF["t"]])
        xl = (LD, LD.n)

        sqp = None
        tok_sum = None
        for o in range(8):
            xo = x2blk.ap()[:, o * TB:(o + 1) * TB]
            stok = sch.call("scalar", "activation", sq_f.ap(), xo, AF.Square,
                            wait=[xl, prevF["stats"], sqp], inc=SS)
            sch.call("tensor", "matmul", pb[3].ap()[0:1, 0:TB], ones_col.ap(),
                     xo, wait=[xl, prevF["stats"]],
                     start=(o == 0), stop=(o == 7))
            if o == 7:
                tok_sum = sch.mark("tensor", TS)
            sqp = sch.call("tensor", "matmul", pb[6].ap()[0:1, 0:TB],
                           ones_col.ap(), sq_f.ap(), wait=[stok],
                           start=(o == 0), stop=(o == 7), inc=TS)
        rows_done, _ = ln_rows(tok_sum, sqp, pb[3].ap()[0:1, 0:TB],
                               pb[6].ap()[0:1, 0:TB], prevF["rowuser"])
        sb = bcast2((rstd_f, BCR, None), (brow_f, BCB, None),
                    wait=[rows_done])
        vx = None
        for o in range(8):
            xo = x2blk.ap()[:, o * TB:(o + 1) * TB]
            ho = xh3.ap()[:, o * TB:(o + 1) * TB]
            sch.call("vector", "tensor_tensor", ho, xo, bcf_s(BCR),
                     wait=[sb], op=ALU.mult)
            vt = sch.call("vector", "tensor_tensor", ho, ho, bcf_s(BCB),
                          op=ALU.add, inc=VS)
            sab = sch.call("scalar", "activation", sq_f.ap(), ho, AF.Abs,
                           wait=[vt], inc=SS)
            if o == 0:
                sch.call("vector", "tensor_copy", rmx.ap(), sq_f.ap(),
                         wait=[sab])
            else:
                sch.call("vector", "tensor_tensor", rmx.ap(), rmx.ap(),
                         sq_f.ap(), wait=[sab], op=ALU.max)
        vx = sch.mark("vector", VS)
        ptm = part_tree_max(rmx.ap(), mean_f, wait=[vx])
        sch.call("vector", "tensor_scalar", mean_f, mean_f, 1e-5, None,
                 wait=[ptm], op0=ALU.max)
        sch.call("vector", "reciprocal", var_f, mean_f)
        sch.call("vector", "tensor_scalar", rstd_f, var_f, 127.0, None,
                 op0=ALU.mult)
        sch.call("vector", "tensor_scalar", brow_f, mean_f,
                 cells.ap()[:, 0:1], None, op0=ALU.mult)
        vig = sch.call("vector", "tensor_scalar", brow_f, brow_f,
                       1.0 / 127.0, None, op0=ALU.mult, inc=VS)
        sb2 = bcast2((rstd_f, BCSX, None), (brow_f, BCIG, None), wait=[vig])
        vq = None
        for o in range(8):
            ho = xh3.ap()[:, o * TB:(o + 1) * TB]
            sch.call("vector", "tensor_tensor", ho, ho, bcf_s(BCSX),
                     wait=[sb2], op=ALU.mult)
            sch.call("vector", "tensor_copy", i32b.ap(), ho)
            vq = sch.call("vector", "tensor_copy",
                          xq_b.ap()[:, o * TB:(o + 1) * TB], i32b.ap(),
                          inc=VS)
        if dbg and j == 0:
            for o in range(8):
                cv = sch.call("vector", "tensor_copy", sq_f.ap(),
                              xq_b.ap()[:, o * TB:(o + 1) * TB],
                              wait=[vq, (ST, ST.n)], inc=VS)
                store(d_xq[:, o * TB:(o + 1) * TB], sq_f.ap(), wait=[cv])
            fence("gpsimd", ST)
        vhm = None
        wld = {}
        tsu = {}
        for et in range(33):
            if et < 32:
                par = et % 2
                fence("sync", LD)
                load(wgq_sb.ap()[:, par * 1024:(par + 1) * 1024],
                     scr_wq[0][:, et * 128:(et + 1) * 128]
                     .rearrange("(k p) j -> p k j", p=128),
                     wait=[F0_done_ST, tsu.get(et - 2)])
                load(wuq_sb.ap()[:, par * 1024:(par + 1) * 1024],
                     scr_wq[1][:, et * 128:(et + 1) * 128]
                     .rearrange("(k p) j -> p k j", p=128))
                wld[et] = (LD, LD.n)
            if et == 0:
                continue
            e = et - 1
            par = e % 2
            bg = 0 if par == 0 else 7
            bu = 1 if par == 0 else 2
            for k in range(8):
                sch.call("tensor", "matmul", pb[bg].ap()[:, 0:TB],
                         wgq_sb.ap()[:, (par * 8 + k) * 128:
                                     (par * 8 + k + 1) * 128],
                         xq_b.ap()[:, k * TB:(k + 1) * TB],
                         wait=[wld[e], vq, bank_free[bg]],
                         start=(k == 0), stop=(k == 7))
            tg = sch.mark("tensor", TS)
            for k in range(8):
                sch.call("tensor", "matmul", pb[bu].ap()[:, 0:TB],
                         wuq_sb.ap()[:, (par * 8 + k) * 128:
                                     (par * 8 + k + 1) * 128],
                         xq_b.ap()[:, k * TB:(k + 1) * TB],
                         wait=[wld[e], vq, bank_free[bu]],
                         start=(k == 0), stop=(k == 7))
            tu = sch.mark("tensor", TS)
            tsu[e] = tu
            vg = sch.call("vector", "tensor_tensor", gdt.ap(),
                          pb[bg].ap()[:, 0:TB], bcf_s(BCIG),
                          wait=[tg, sb2], op=ALU.mult, inc=VS)
            bank_free[bg] = vg
            sg = sch.call("scalar", "activation", sgt.ap(), gdt.ap(),
                          AF.Silu, wait=[vg], inc=SS)
            he = hp.ap()[:, e * TB:(e + 1) * TB]
            vh = sch.call("vector", "tensor_tensor", he, sgt.ap(),
                          pb[1 if par == 0 else 2].ap()[:, 0:TB],
                          wait=[sg, tu], op=ALU.mult, inc=VS)
            bank_free[bu] = vh
            sabh = sch.call("scalar", "activation", sq_f.ap(), he, AF.Abs,
                            wait=[vh], inc=SS)
            if e == 0:
                vhm = sch.call("vector", "tensor_copy", rmx.ap(), sq_f.ap(),
                               wait=[sabh], inc=VS)
            else:
                vhm = sch.call("vector", "tensor_tensor", rmx.ap(), rmx.ap(),
                               sq_f.ap(), wait=[sabh], op=ALU.max, inc=VS)
        if dbg and j == 0:
            for e in range(32):
                store(d_hp[:, e * TB:(e + 1) * TB],
                      hp.ap()[:, e * TB:(e + 1) * TB], wait=[vhm])
            fence("gpsimd", ST)
            sch.call("vector", "nop", wait=[(ST, ST.n)])
        ptm2 = part_tree_max(rmx.ap(), var_f, wait=[vhm])
        sch.call("vector", "tensor_scalar", aux_f, var_f,
                 cells.ap()[:, 1:2], None, wait=[ptm2], op0=ALU.mult)
        sch.call("vector", "tensor_tensor", aux_f, aux_f, mean_f,
                 op=ALU.mult)
        sch.call("vector", "tensor_scalar", aux_f, aux_f, 1.0 / 127.0, None,
                 op0=ALU.mult)
        sch.call("vector", "tensor_scalar", aux_f, aux_f, 1e-5, None,
                 op0=ALU.max)
        sch.call("vector", "reciprocal", var_f, aux_f)
        sch.call("vector", "tensor_tensor", var_f, var_f, mean_f,
                 op=ALU.mult)
        sch.call("vector", "tensor_scalar", var_f, var_f,
                 cells.ap()[:, 1:2], None, op0=ALU.mult)
        sch.call("vector", "tensor_scalar", aux_f, aux_f,
                 cells.ap()[:, 2:3], None, op0=ALU.mult)
        vdr = sch.call("vector", "tensor_scalar", aux_f, aux_f, 1.0 / 127.0,
                       None, op0=ALU.mult, inc=VS)
        sb3 = bcast2((var_f, BCSH, None), (aux_f, BCDR, None), wait=[vdr])
        vq2 = None
        for e in range(32):
            he = hp.ap()[:, e * TB:(e + 1) * TB]
            sch.call("vector", "tensor_tensor", he, he, bcf_s(BCSH),
                     wait=[sb3], op=ALU.mult)
            sch.call("vector", "tensor_copy", i32b.ap(), he)
            vq2 = sch.call("vector", "tensor_copy",
                           hq_b.ap()[:, e * TB:(e + 1) * TB], i32b.ap(),
                           inc=VS)
        wld2 = {}
        tsd = {}
        for ot in range(9):
            if ot < 8:
                par = ot % 2
                fence("sync", LD)
                load(wdq_sb.ap()[:, par * 4096:(par + 1) * 4096],
                     scr_wq[2][:, ot * 128:(ot + 1) * 128]
                     .rearrange("(k p) j -> p k j", p=128),
                     wait=[F0_done_ST, tsd.get(ot - 2)])
                wld2[ot] = (LD, LD.n)
            if ot == 0:
                continue
            o = ot - 1
            par = o % 2
            for k in range(32):
                sch.call("tensor", "matmul", pb[2].ap()[:, 0:TB],
                         wdq_sb.ap()[:, (par * 32 + k) * 128:
                                     (par * 32 + k + 1) * 128],
                         hq_b.ap()[:, k * TB:(k + 1) * TB],
                         wait=[wld2[o], vq2, bank_free[2]],
                         start=(k == 0), stop=(k == 31))
            td = sch.mark("tensor", TS)
            tsd[o] = td
            fo = fc.ap()[:, o * TB:(o + 1) * TB]
            vf = sch.call("vector", "tensor_tensor", fo,
                          pb[2].ap()[:, 0:TB], bcf_s(BCDR),
                          wait=[td, sb3], op=ALU.mult, inc=VS)
            bank_free[2] = vf
        if dbg and j == 0:
            for o in range(8):
                store(d_fc[:, o * TB:(o + 1) * TB],
                      fc.ap()[:, o * TB:(o + 1) * TB], wait=[vf])
            fence("gpsimd", ST)
            sch.call("vector", "nop", wait=[(ST, ST.n)])
        sqp = None
        tok_sum = None
        for o in range(8):
            fo = fc.ap()[:, o * TB:(o + 1) * TB]
            stok = sch.call("scalar", "activation", sq_f.ap(), fo, AF.Square,
                            wait=[vf, sqp], inc=SS)
            sch.call("tensor", "matmul", pb[3].ap()[0:1, 0:TB], ones_col.ap(),
                     fo, wait=[vf, bank_free[3]],
                     start=(o == 0), stop=(o == 7))
            if o == 7:
                tok_sum = sch.mark("tensor", TS)
            sqp = sch.call("tensor", "matmul", pb[6].ap()[0:1, 0:TB],
                           ones_col.ap(), sq_f.ap(), wait=[stok],
                           start=(o == 0), stop=(o == 7), inc=TS)
        rows_done, vrd4 = ln_rows(tok_sum, sqp, pb[3].ap()[0:1, 0:TB],
                                  pb[6].ap()[0:1, 0:TB], None)
        bank_free[3] = vrd4
        sb4 = bcast2((rstd_f, BCSX, None), (brow_f, BCIG, None),
                     wait=[rows_done])
        for o in range(8):
            fo = fc.ap()[:, o * TB:(o + 1) * TB]
            xo = x2blk.ap()[:, o * TB:(o + 1) * TB]
            sch.call("vector", "tensor_tensor", fo, fo, bcf_s(BCSX),
                     wait=[sb4, prevF["stores"]], op=ALU.mult)
            sch.call("vector", "tensor_tensor", fo, fo, bcf_s(BCIG),
                     op=ALU.add)
            vx3 = sch.call("vector", "tensor_tensor", fo, fo, xo,
                           op=ALU.add, inc=VS)
        for bb in range(B):
            store(d_out[bb, :, j * CH:(j + 1) * CH]
                  .rearrange("(o p) j -> p o j", p=128),
                  fc.ap().rearrange("p (o z) -> p o z", z=TB)
                  [:, :, bb * CH:(bb + 1) * CH], wait=[vx3])
        prevF["v"] = (VS, VS.n)
        prevF["t"] = (TS, TS.n)
        prevF["stats"] = vrd4
        prevF["rowuser"] = sb4
        prevF["stores"] = (ST, ST.n)
        fence("gpsimd", ST)

    f_scope.close()
    F_done_V = (VS, VS.n)
    F_done_S = (SS, SS.n)
    F_done_T = (TS, TS.n)

    sch.call("gpsimd", "nop", wait=[(ST, ST.n), (LD, LD.n), C_done_V,
                                    C_done_S, C_done_T])

    sch.finalize()
    stack.close()
    return nc


def _prep_inputs(inputs):
    def T(a):
        return np.ascontiguousarray(np.asarray(a, dtype=np.float32).T)

    def C(a):
        return np.ascontiguousarray(np.asarray(a, dtype=np.float32))

    def V(a):
        return np.ascontiguousarray(
            np.asarray(a, dtype=np.float32).reshape(-1, 1))

    return {
        "inWT": T(inputs["L0_attn_in_proj_w"]),
        "convW": C(inputs["L0_attn_conv_w"]),
        "convB": V(inputs["L0_attn_conv_b"]),
        "xprojWT": T(inputs["L0_attn_x_proj_w"]),
        "Alog": C(inputs["L0_attn_A_log"]),
        "Dvec": V(inputs["L0_attn_D"]),
        "dtowT": T(inputs["L0_attn_dt_out_w"]),
        "dtoB": V(inputs["L0_attn_dt_out_b"]),
        "opWT": T(inputs["L0_attn_out_proj_w"]),
        "minWT": T(inputs["L0_attn_m_in_w"]),
        "mconvW": C(inputs["L0_attn_m_conv_w"]),
        "mconvB": V(inputs["L0_attn_m_conv_b"]),
        "mxprojWT": T(inputs["L0_attn_m_x_proj_w"]),
        "mdtWT": T(inputs["L0_attn_m_dt_w"]),
        "mdtB": V(inputs["L0_attn_m_dt_b"]),
        "mAlog": C(inputs["L0_attn_m_A_log"]),
        "mDvec": V(inputs["L0_attn_m_D"]),
        "moutWT": T(inputs["L0_attn_m_out_w"]),
        "gWT": T(inputs["L0_ffn_w_gate"]),
        "uWT": T(inputs["L0_ffn_w_up"]),
        "dWT": T(inputs["L0_ffn_w_down"]),
    }


def kernel(**inputs):
    ncores = 8
    x = np.asarray(inputs["L0_attn_hidden_states"], dtype=np.float32)
    common = _prep_inputs(inputs)
    nc = build_nc()
    in_maps = []
    for i in range(ncores):
        m = dict(common)
        m["x_fm"] = np.ascontiguousarray(x[2 * i:2 * i + 2].transpose(0, 2, 1))
        in_maps.append(m)
    res = run_bass_kernel_spmd(nc, in_maps, core_ids=list(range(ncores)))
    outs = [np.asarray(r["out_fm"]).transpose(0, 2, 1) for r in res.results]
    return np.ascontiguousarray(np.concatenate(outs, axis=0))


# version 12
# speedup vs baseline: 1.0043x; 1.0043x over previous
import numpy as np
from contextlib import ExitStack

import concourse.bass as bass
import concourse.mybir as mybir
from concourse.bass_utils import run_bass_kernel_spmd


F32 = mybir.dt.float32
BF16 = mybir.dt.bfloat16
I32 = mybir.dt.int32
AF = mybir.ActivationFunctionType
ALU = mybir.AluOpType
AX = mybir.AxisListType

B = 2
L = 2048
DM = 1024
DI = 2048
NST = 16
DTR = 64
KC = 4
MDI, MN, MDTR, MKC = 64, 4, 4, 2
CH = 256
NCH = L // CH
NT = DI // 128
EPS = 1e-5

ENGINES = ("sync", "tensor", "vector", "scalar", "gpsimd")


class Sem:
    __slots__ = ("h", "n", "owner", "name", "is_dma")

    def __init__(self, h, owner, name, is_dma=False):
        self.h, self.n, self.owner, self.name = h, 0, owner, name
        self.is_dma = is_dma


class Sched:

    def __init__(self, nc, stack):
        self.nc = nc
        self.ops = {e: [] for e in ENGINES}
        self.waited = {e: {} for e in ENGINES}
        self.stack = stack

    def sem(self, owner, name, is_dma=False):
        h = self.stack.enter_context(self.nc.semaphore(name))
        return Sem(h, owner, name, is_dma)

    def call(self, _eng, _opname, *args, wait=(), inc=None, amt=1,
             raw=False, **kwargs):
        wl = []
        for tok in wait:
            if tok is None:
                continue
            s, v = tok
            if s.owner == _eng and not s.is_dma:
                continue
            prev = self.waited[_eng].get(id(s), 0)
            if v > prev:
                self.waited[_eng][id(s)] = v
                wl.append((s.h, v))
        if inc is not None:
            inc.n += amt
        tok_out = (inc, inc.n) if inc is not None else None
        import sys as _sys
        _ln = _sys._getframe(1).f_lineno
        self.ops[_eng].append((wl, _opname, args, kwargs,
                               inc.h if inc is not None else None, amt, _ln,
                               raw))
        return tok_out

    def mark(self, eng, sem, amt=1):
        wl, opname, args, kwargs, inch, a, _ln, raw = self.ops[eng][-1]
        if inch is not None:
            assert inch is sem.h, "last op already incs a different sem"
            return (sem, sem.n)
        sem.n += amt
        self.ops[eng][-1] = (wl, opname, args, kwargs, sem.h, amt, _ln, raw)
        return (sem, sem.n)

    def finalize(self):
        with self.nc.Block() as block:
            for name in ENGINES:
                ops = self.ops[name]

                def body(engine, ops=ops, name=name):
                    compute = name in ("vector", "scalar", "gpsimd")

                    def small_out(a):
                        try:
                            sh = a[0].shape
                            n = 1
                            for s in sh[1:]:
                                n *= s
                            return n <= 32
                        except Exception:
                            return False

                    for wl, opname, args, kwargs, inch, amt, _ln, raw in ops:
                        for h, v in wl:
                            engine.wait_ge(h, v)
                        try:
                            ins = getattr(engine, opname)(*args, **kwargs)
                        except Exception as e:
                            raise RuntimeError(
                                f"emit failed: {name}.{opname} from "
                                f"kernel.py:{_ln}: {e}") from e
                        if compute and inch is None and not raw and \
                                opname not in ("dma_start", "nop", "memset") \
                                and (opname == "tensor_reduce"
                                     or small_out(args)):
                            engine.drain(fusable=True)
                        if inch is not None:
                            if compute and opname != "dma_start" and not raw:
                                engine.drain(fusable=True).then_inc(inch, amt)
                            else:
                                ins.then_inc(inch, amt)
                getattr(block, name)(body)


def build_nc(dbg=False, phases="ABCF"):
    nc = bass.Bass("TRN2", target_bir_lowering=False,
                   detect_race_conditions=False)
    stack = ExitStack()

    def din(name, shape):
        return nc.dram_tensor(name, list(shape), F32, kind="ExternalInput").ap()

    dx = din("x_fm", (B, DM, L))
    d_inWT = din("inWT", (DM, 2 * DI))
    d_convW = din("convW", (DI, KC))
    d_convB = din("convB", (DI, 1))
    d_xprojWT = din("xprojWT", (DI, DTR + 2 * NST))
    d_Alog = din("Alog", (DI, NST))
    d_D = din("Dvec", (DI, 1))
    d_dtowT = din("dtowT", (DTR, DI))
    d_dtoB = din("dtoB", (DI, 1))
    d_opWT = din("opWT", (DI, DM))
    d_minWT = din("minWT", (DTR, 2 * MDI))
    d_mconvW = din("mconvW", (MDI, MKC))
    d_mconvB = din("mconvB", (MDI, 1))
    d_mxprojWT = din("mxprojWT", (MDI, MDTR + 2 * MN))
    d_mdtWT = din("mdtWT", (MDTR, MDI))
    d_mdtB = din("mdtB", (MDI, 1))
    d_mAlog = din("mAlog", (MDI, MN))
    d_mD = din("mDvec", (MDI, 1))
    d_moutWT = din("moutWT", (MDI, DTR))
    d_gWT = din("gWT", (DM, 4096))
    d_uWT = din("uWT", (DM, 4096))
    d_dWT = din("dWT", (4096, DM))

    d_out = nc.dram_tensor("out_fm", [B, DM, L], F32, kind="ExternalOutput").ap()

    scr_kind = "ExternalOutput" if dbg else "Internal"
    scr_u = nc.dram_tensor("scr_u", [B, DI, L], F32, kind=scr_kind).ap()
    scr_sz = nc.dram_tensor("scr_sz", [B, DI, L], F32, kind=scr_kind).ap()
    scr_yg = nc.dram_tensor("scr_yg", [B, DI, L], F32, kind=scr_kind).ap()
    scr_x2 = nc.dram_tensor("scr_x2", [B, DM, L], F32,
                            kind="ExternalInput" if phases == "F"
                            else scr_kind).ap()
    if dbg:
        d_xdbl = nc.dram_tensor("dbg_xdbl", [96, B * L], F32,
                                kind="ExternalOutput").ap()
        d_dtm = nc.dram_tensor("dbg_dtm", [DTR, B * L], F32,
                               kind="ExternalOutput").ap()
        d_xq = nc.dram_tensor("dbg_xq", [128, 8 * 512], F32,
                              kind="ExternalOutput").ap()
        d_hp = nc.dram_tensor("dbg_hp", [128, 32 * 512], F32,
                              kind="ExternalOutput").ap()
        d_fc = nc.dram_tensor("dbg_fc", [128, 8 * 512], F32,
                              kind="ExternalOutput").ap()

    sch = Sched(nc, stack)
    LD = sch.sem("sync", "ld", is_dma=True)
    ST = sch.sem("gpsimd", "st", is_dma=True)
    TS = sch.sem("tensor", "tsm")
    VS = sch.sem("vector", "vsm")
    SS = sch.sem("scalar", "ssm")
    GS = sch.sem("gpsimd", "gsm")

    def fence(eng, semq):
        sch.call(eng, "nop", wait=[(semq, semq.n)])

    alloc = nc.alloc_sbuf_tensor

    def salloc(scope, name, shape, dtype=F32):
        return scope.enter_context(nc.sbuf_tensor(name, shape, dtype))

    ones_col = alloc("ones_col", [128, 1], F32)
    ones_row = alloc("ones_row", [1, 128], F32)
    ident = alloc("ident", [128, 128], F32)
    tmp_init = alloc("tmp_init", [128, 256], F32)
    cells = alloc("cells", [1, 16], F32)
    mk_scope = ExitStack()
    a_neg = salloc(mk_scope, "a_neg", [128, NT * NST])
    convw_c = salloc(mk_scope, "convw_c", [128, NT * KC])
    convb_c = salloc(mk_scope, "convb_c", [128, NT])
    dvec_c = salloc(mk_scope, "dvec_c", [128, NT])
    dtob_c = salloc(mk_scope, "dtob_c", [128, NT])
    m_aneg = salloc(mk_scope, "m_aneg", [128, MN])
    m_convw = salloc(mk_scope, "m_convw", [128, MKC])
    m_convb = salloc(mk_scope, "m_convb", [128, 1])
    m_dvec = salloc(mk_scope, "m_dvec", [128, 1])
    m_dtb = salloc(mk_scope, "m_dtb", [128, 1])
    x_dbl = salloc(mk_scope, "x_dbl", [96, B * L])
    dtm_sb = salloc(mk_scope, "dtm_sb", [DTR, B * L])
    states = salloc(mk_scope, "states", [128, B * NT * NST])
    w_dtow = salloc(mk_scope, "w_dtow", [DTR, DI])
    w_minW = salloc(mk_scope, "w_minW", [DTR, 2 * MDI])
    w_mxproj = salloc(mk_scope, "w_mxproj", [MDI, MDTR + 2 * MN])
    w_mdt = salloc(mk_scope, "w_mdt", [MDTR, MDI])
    w_mout = salloc(mk_scope, "w_mout", [MDI, DTR])
    w_xproj = [salloc(mk_scope, f"w_xproj{k}", [128, 96]) for k in range(NT)]
    rows = salloc(mk_scope, "rows", [1, 2560])

    pb = [nc.alloc_psum_tensor(f"pb{i}", [128, 512], F32) for i in range(8)]
    bank_free = [None] * 8

    def load(dst_ap, src_ap, wait=()):
        return sch.call("sync", "dma_start", dst_ap, src_ap,
                        wait=wait, inc=LD, amt=16)

    def store(dst_ap, src_ap, wait=()):
        return sch.call("gpsimd", "dma_start", dst_ap, src_ap,
                        wait=wait, inc=ST, amt=16)

    for t in range(NT):
        load(tmp_init.ap()[:, t * NST:(t + 1) * NST],
             d_Alog[t * 128:(t + 1) * 128, :])
    tok_alog = (LD, LD.n)
    fence("sync", LD)
    for t in range(NT):
        load(convw_c.ap()[:, t * KC:(t + 1) * KC],
             d_convW[t * 128:(t + 1) * 128, :])
        load(convb_c.ap()[:, t:t + 1], d_convB[t * 128:(t + 1) * 128, :])
        load(dvec_c.ap()[:, t:t + 1], d_D[t * 128:(t + 1) * 128, :])
        load(dtob_c.ap()[:, t:t + 1], d_dtoB[t * 128:(t + 1) * 128, :])
    load(m_aneg.ap()[0:MDI, :], d_mAlog[:, :])
    load(m_convw.ap()[0:MDI, :], d_mconvW[:, :])
    load(m_convb.ap()[0:MDI, :], d_mconvB[:, :])
    load(m_dvec.ap()[0:MDI, :], d_mD[:, :])
    load(m_dtb.ap()[0:MDI, :], d_mdtB[:, :])
    load(w_dtow.ap(), d_dtowT)
    load(w_minW.ap(), d_minWT)
    load(w_mxproj.ap(), d_mxprojWT)
    load(w_mdt.ap(), d_mdtWT)
    load(w_mout.ap(), d_moutWT)
    for k in range(NT):
        load(w_xproj[k].ap(), d_xprojWT[k * 128:(k + 1) * 128, :])
    tok_small = (LD, LD.n)
    fence("sync", LD)

    t1 = sch.call("scalar", "activation", a_neg.ap(), tmp_init.ap(), AF.Exp,
                  wait=[tok_alog], inc=SS)
    t2 = sch.call("scalar", "activation", m_aneg.ap()[0:MDI, :],
                  m_aneg.ap()[0:MDI, :], AF.Exp, wait=[tok_small], inc=SS)
    sch.call("vector", "tensor_scalar", a_neg.ap(), a_neg.ap(), -1.0, None,
             wait=[t1], op0=ALU.mult)
    sch.call("vector", "tensor_scalar", m_aneg.ap()[0:MDI, :],
             m_aneg.ap()[0:MDI, :], -1.0, None, wait=[t2], op0=ALU.mult)
    sch.call("vector", "memset", ones_col.ap(), 1.0)
    sch.call("vector", "memset", ones_row.ap(), 1.0)
    tok_ones = sch.call("vector", "memset", states.ap(), 0.0, inc=VS)
    gi = sch.call("gpsimd", "iota", ident.ap(), [[0, 128]],
                  channel_multiplier=1,
                  allow_small_or_imprecise_dtypes=True, wait=[t1])
    gi = sch.call("gpsimd", "iota", tmp_init.ap()[:, 0:128], [[1, 128]],
                  channel_multiplier=0,
                  allow_small_or_imprecise_dtypes=True, inc=GS)
    tok_ident = sch.call("vector", "tensor_tensor", ident.ap(), ident.ap(),
                         tmp_init.ap()[:, 0:128], wait=[gi],
                         op=ALU.is_equal, inc=VS)

    run_abc = "A" in phases
    a_scope = ExitStack()
    CHA = 512
    NCHA = L // CHA
    wa_in = salloc(a_scope, "wa_in", [128, 8 * 2048])
    xa = salloc(a_scope, "xa", [128, 2 * 8 * CHA])
    acta = salloc(a_scope, "acta", [128, NT * CHA])
    hist = salloc(a_scope, "hist", [128, B * NT * 4])
    sqa = salloc(a_scope, "sqa", [128, CHA])
    csum = salloc(a_scope, "csum", [1, 2048])

    mean_r = rows.ap()[:, 0:CHA]
    var_r = rows.ap()[:, CHA:2 * CHA]
    m2_r = rows.ap()[:, 2 * CHA:3 * CHA]
    rstd_r = rows.ap()[:, 3 * CHA:4 * CHA]
    brow_r = rows.ap()[:, 4 * CHA:5 * CHA]

    itersA = [(b, c) for c in range(NCHA) for b in range(B)]
    iters = [(b, c) for c in range(NCH) for b in range(B)]
    xld = {}
    prev = {"stores": None, "tdone": None, "rows_S": None, "stats_rd": None}

    sch.call("vector", "memset", hist.ap(), 0.0, inc=VS)

    for half in (range(2) if run_abc else []):
        fence("sync", LD)
        for k in range(8):
            load(wa_in.ap()[:, k * 2048:(k + 1) * 2048],
                 d_inWT[k * 128:(k + 1) * 128, half * 2048:(half + 1) * 2048],
                 wait=[prev["tdone"]])
        tok_win = (LD, LD.n)

        tok_csum = None
        for cc in range(4):
            for k in range(8):
                sch.call("tensor", "matmul", pb[0].ap()[0:1, 0:512],
                         ones_col.ap(),
                         wa_in.ap()[:, k * 2048 + cc * 512:
                                    k * 2048 + (cc + 1) * 512],
                         wait=[tok_win, bank_free[0], tok_ones],
                         start=(k == 0), stop=(k == 7))
            tcs = sch.mark("tensor", TS)
            bank_free[0] = sch.call("scalar", "copy",
                                    csum.ap()[:, cc * 512:(cc + 1) * 512],
                                    pb[0].ap()[0:1, 0:512], wait=[tcs], inc=SS)
        tok_csum = bank_free[0]
        prev["stats_rd"] = tok_csum

        def load_x(it):
            b, c = itersA[it]
            buf = it % 2
            fence("sync", LD)
            load(xa.ap()[:, buf * 8 * CHA:(buf + 1) * 8 * CHA],
                 dx[b, :, c * CHA:(c + 1) * CHA]
                 .rearrange("(k p) j -> p k j", p=128),
                 wait=[prev["tdone"]])
            xld[it] = (LD, LD.n)

        load_x(0)

        for it, (b, c) in enumerate(itersA):
            buf = it % 2
            xt = [xa.ap()[:, (buf * 8 + k) * CHA:(buf * 8 + k + 1) * CHA]
                  for k in range(8)]
            if it + 1 < len(itersA):
                load_x(it + 1)

            sq_prev = None
            tok_pb0 = None
            for k in range(8):
                stok = sch.call("scalar", "activation", sqa.ap(), xt[k],
                                AF.Square,
                                wait=[xld[it], prev["stats_rd"], sq_prev],
                                inc=SS)
                sch.call("tensor", "matmul", pb[0].ap()[0:1, 0:CHA],
                         ones_col.ap(), xt[k],
                         wait=[xld[it], prev["stats_rd"], tok_ones],
                         start=(k == 0), stop=(k == 7))
                if k == 7:
                    tok_pb0 = sch.mark("tensor", TS)
                sq_prev = sch.call("tensor", "matmul", pb[1].ap()[0:1, 0:CHA],
                                   ones_col.ap(), sqa.ap(), wait=[stok],
                                   start=(k == 0), stop=(k == 7), inc=TS)
            tstat = sq_prev

            sch.call("vector", "tensor_scalar", mean_r, pb[0].ap()[0:1, 0:CHA],
                     1.0 / DM, None,
                     wait=[tstat, tok_pb0, prev["tdone"], prev["rows_S"]],
                     op0=ALU.mult)
            sch.call("vector", "tensor_scalar", var_r, pb[1].ap()[0:1, 0:CHA],
                     1.0 / DM, None, op0=ALU.mult)
            sch.call("vector", "tensor_tensor", m2_r, mean_r, mean_r,
                     op=ALU.mult)
            sch.call("vector", "tensor_tensor", var_r, var_r, m2_r,
                     op=ALU.subtract)
            stats_rd = sch.call("vector", "tensor_scalar", var_r, var_r, EPS,
                                None, op0=ALU.add, inc=VS)
            prev["stats_rd"] = stats_rd
            sl = sch.call("scalar", "activation", rstd_r, var_r, AF.Ln,
                          wait=[stats_rd])
            sl = sch.call("scalar", "activation", rstd_r, rstd_r, AF.Exp,
                          scale=-0.5, inc=SS)
            sch.call("vector", "tensor_tensor", brow_r, mean_r, rstd_r,
                     wait=[sl], op=ALU.mult)
            rows_done = sch.call("vector", "tensor_scalar", brow_r, brow_r,
                                 -1.0, None, op0=ALU.mult, inc=VS)
            prev["rows_S"] = sl
            tbr = sch.call("tensor", "matmul", pb[2].ap()[:, 0:CHA],
                           ones_row.ap(), rstd_r,
                           wait=[rows_done, tok_ident, bank_free[2]],
                           start=True, stop=True, inc=TS)
            xh_tok = None
            for k in range(8):
                xh_tok = sch.call("vector", "tensor_tensor", xt[k], xt[k],
                                  pb[2].ap()[:, 0:CHA], wait=[tbr],
                                  op=ALU.mult, inc=VS)
            bank_free[2] = xh_tok

            for t in range(NT):
                bi = 3 + (t % 2)
                for k in range(8):
                    sch.call("tensor", "matmul", pb[bi].ap()[:, 0:CHA],
                             wa_in.ap()[:, k * 2048 + t * 128:
                                        k * 2048 + (t + 1) * 128],
                             xt[k],
                             wait=[xh_tok, tok_win, bank_free[bi]],
                             start=(k == 0), stop=False)
                tmm = sch.call("tensor", "matmul", pb[bi].ap()[:, 0:CHA],
                               csum.ap()[:, t * 128:(t + 1) * 128], brow_r,
                               wait=[tok_csum, rows_done],
                               start=False, stop=True, inc=TS)
                aap = acta.ap()[:, t * CHA:(t + 1) * CHA]
                if half == 0:
                    hap = hist.ap()[:, (b * NT + t) * 4:(b * NT + t) * 4 + 4]
                    wcol = [convw_c.ap()[:, t * KC + j:t * KC + j + 1]
                            for j in range(KC)]
                    sch.call("vector", "tensor_scalar", aap,
                             pb[bi].ap()[:, 0:CHA], wcol[3], None,
                             wait=[tmm, prev["stores"], prev["tdone"]],
                             op0=ALU.mult)
                    for j in range(3):
                        sh = 3 - j
                        sch.call("vector", "scalar_tensor_tensor",
                                 aap[:, sh:CHA], pb[bi].ap()[:, 0:CHA - sh],
                                 wcol[j], aap[:, sh:CHA],
                                 op0=ALU.mult, op1=ALU.add)
                        sch.call("vector", "scalar_tensor_tensor",
                                 aap[:, 0:sh], hap[:, 1 + j:1 + j + sh],
                                 wcol[j], aap[:, 0:sh],
                                 op0=ALU.mult, op1=ALU.add)
                    vconv = sch.call("vector", "tensor_copy", hap[:, 1:4],
                                     pb[bi].ap()[:, CHA - 3:CHA], inc=VS)
                    bank_free[bi] = vconv
                    su = sch.call("scalar", "activation", aap, aap, AF.Silu,
                                  bias=convb_c.ap()[:, t:t + 1],
                                  wait=[vconv], inc=SS)
                    sch.call("tensor", "matmul", pb[5].ap()[0:96, 0:CHA],
                             w_xproj[t].ap(), aap,
                             wait=[su, tok_small, bank_free[5]],
                             start=(t == 0), stop=(t == NT - 1))
                    a_st_tok = su
                else:
                    stok = sch.call("scalar", "activation", aap,
                                    pb[bi].ap()[:, 0:CHA], AF.Silu,
                                    wait=[tmm, prev["stores"],
                                          prev["tdone"]], inc=SS)
                    bank_free[bi] = stok
                    a_st_tok = stok
            dstA = scr_u if half == 0 else scr_sz
            store(dstA[b, :, c * CHA:(c + 1) * CHA]
                  .rearrange("(k p) j -> p k j", p=128),
                  acta.ap(), wait=[a_st_tok])
            txp = sch.mark("tensor", TS)
            if half == 0:
                bank_free[5] = sch.call(
                    "scalar", "copy",
                    x_dbl.ap()[:, b * L + c * CHA:b * L + (c + 1) * CHA],
                    pb[5].ap()[0:96, 0:CHA], wait=[txp], inc=SS)
            prev["stores"] = (ST, ST.n)
            prev["tdone"] = txp
            fence("gpsimd", ST)

    a_scope.close()
    A_done_S = (SS, SS.n)
    A_done_V = (VS, VS.n)
    A_done_T = (TS, TS.n)
    A_done_ST = (ST, ST.n)

    bi_scope = ExitStack()
    xzm = salloc(bi_scope, "xzm", [128, L])
    xum = salloc(bi_scope, "xum", [64, L])
    szm = salloc(bi_scope, "szm", [64, L])
    xdm = salloc(bi_scope, "xdm", [12, L])
    xbcm = salloc(bi_scope, "xbcm", [1, L])
    dmm = salloc(bi_scope, "dmm", [64, L])
    dtms = salloc(bi_scope, "dtms", [64, L])
    tma = salloc(bi_scope, "tma", [64, L])
    tmb = salloc(bi_scope, "tmb", [64, L])
    w_m = salloc(bi_scope, "w_m", [64, L])
    bcm = salloc(bi_scope, "bcm", [64, L])
    sm = salloc(bi_scope, "sm", [64, L])
    ym = salloc(bi_scope, "ym", [64, L])

    def mm512(dst_sb, lhsT, rhs_base_ap, n_cc, psum_rows, wait=(), kname=None):
        toks = []
        for cc in range(n_cc):
            tmmi = sch.call("tensor", "matmul",
                            pb[6].ap()[0:psum_rows, 0:512], lhsT,
                            rhs_base_ap[:, cc * 512:(cc + 1) * 512],
                            wait=list(wait) + [bank_free[6]],
                            start=True, stop=True, inc=TS)
            bank_free[6] = sch.call("scalar", "copy",
                                    dst_sb[:, cc * 512:(cc + 1) * 512],
                                    pb[6].ap()[0:psum_rows, 0:512],
                                    wait=[tmmi], inc=SS)
        return bank_free[6]

    tok_dtm = None
    for b in (range(B) if run_abc else []):
        xdb = x_dbl.ap()[0:64, b * L:(b + 1) * L]
        tk = mm512(xzm.ap(), w_minW.ap(), xdb, 4, 128,
                   wait=[A_done_S, tok_small])
        x_m = xzm.ap()[0:64, :]
        z_m = xzm.ap()[64:128, :]
        w0 = m_convw.ap()[0:64, 0:1]
        w1 = m_convw.ap()[0:64, 1:2]
        sch.call("vector", "tensor_scalar", xum.ap(), x_m, w1, None,
                 wait=[tk], op0=ALU.mult)
        vtk = sch.call("vector", "scalar_tensor_tensor", xum.ap()[:, 1:L],
                       x_m[:, 0:L - 1], w0, xum.ap()[:, 1:L],
                       op0=ALU.mult, op1=ALU.add, inc=VS)
        stk = sch.call("scalar", "activation", xum.ap(), xum.ap(), AF.Silu,
                       bias=m_convb.ap()[0:64, :], wait=[vtk], inc=SS)
        szk = sch.call("scalar", "activation", szm.ap(), z_m, AF.Silu,
                       wait=[tk], inc=SS)
        tk = mm512(xdm.ap(), w_mxproj.ap(), xum.ap(), 4, 12, wait=[stk])
        tok_xdm = tk
        tk = mm512(dmm.ap(), w_mdt.ap(), xdm.ap()[0:4, :], 4, 64, wait=[tk])
        sch.call("scalar", "activation", tma.ap(), dmm.ap(), AF.Abs,
                 bias=m_dtb.ap()[0:64, :], wait=[tk])
        sch.call("scalar", "activation", tma.ap(), tma.ap(), AF.Exp,
                 scale=-1.0)
        sch.call("scalar", "add", tma.ap(), tma.ap(), 1.0)
        sch.call("scalar", "activation", tma.ap(), tma.ap(), AF.Ln)
        ssp = sch.call("scalar", "activation", tmb.ap(), dmm.ap(), AF.Relu,
                       bias=m_dtb.ap()[0:64, :], inc=SS)
        vdt = sch.call("vector", "tensor_add", dtms.ap(), tma.ap(), tmb.ap(),
                       wait=[ssp])
        vw = sch.call("vector", "tensor_mul", w_m.ap(), dtms.ap(), xum.ap(),
                      inc=VS)
        vym = sch.call("vector", "tensor_scalar", ym.ap(), xum.ap(),
                       m_dvec.ap()[0:64, :], None, op0=ALU.mult, inc=VS)
        vsc = None
        vym_prev = vym
        tkb_prev = None
        for n in range(MN):
            sda = sch.call("scalar", "activation", tma.ap(), dtms.ap(),
                           AF.Exp, scale=m_aneg.ap()[0:64, n:n + 1],
                           wait=[vw, vsc], inc=SS)
            fence("sync", LD)
            gb = load(xbcm.ap(), xdm.ap()[4 + n:5 + n, :],
                      wait=[tok_xdm, tkb_prev])
            tkb = mm512(bcm.ap(), ones_row.ap()[0:1, 0:64],
                        xbcm.ap(), 4, 64, wait=[vw, vym_prev, gb])
            vdb = sch.call("vector", "tensor_mul", tmb.ap(), w_m.ap(),
                           bcm.ap(), wait=[tkb])
            vsc = sch.call("vector", "tensor_tensor_scan", sm.ap(), tma.ap(),
                           tmb.ap(), 0.0, wait=[sda],
                           op0=ALU.mult, op1=ALU.add, inc=VS)
            fence("sync", LD)
            gc = load(xbcm.ap(), xdm.ap()[8 + n:9 + n, :],
                      wait=[tok_xdm, tkb])
            tkb_prev = tkc = mm512(bcm.ap(), ones_row.ap()[0:1, 0:64],
                                   xbcm.ap(), 4, 64, wait=[vsc, gc])
            sch.call("vector", "tensor_mul", sm.ap(), sm.ap(), bcm.ap(),
                     wait=[tkc])
            vym_prev = vym = sch.call("vector", "tensor_add", ym.ap(),
                                      ym.ap(), sm.ap(), inc=VS)
        vyg = sch.call("vector", "tensor_mul", w_m.ap(), ym.ap(), szm.ap(),
                       wait=[szk], inc=VS)
        tok_dtm = mm512(dtm_sb.ap()[:, b * L:(b + 1) * L], w_mout.ap(),
                        w_m.ap(), 4, 64, wait=[vyg])
    bi_scope.close()

    bm_scope = ExitStack()
    ub = salloc(bm_scope, "ub", [128, NT * CH])
    bc_sb = salloc(bm_scope, "bc_sb", [128, 2 * NST * CH])
    s3 = salloc(bm_scope, "s3", [128, NST, CH])
    da3 = salloc(bm_scope, "da3", [128, 2, NST, CH])
    dbu_b = salloc(bm_scope, "dbu_b", [128, CH])
    dt4 = salloc(bm_scope, "dt4", [128, 2, CH])
    w4 = salloc(bm_scope, "w4", [128, 4, CH])
    t14 = salloc(bm_scope, "t14", [128, 2, CH])
    t24 = salloc(bm_scope, "t24", [128, 2, CH])
    yg_sb = salloc(bm_scope, "yg_sb", [128, 2 * CH])
    xbc = salloc(bm_scope, "xbc", [1, 2 * NST * CH])

    ut_tok = {}
    ztoks = {}
    da_t = {}
    pd_free = [A_done_V, A_done_V]
    py_free = [A_done_V, A_done_V]
    yg_st = [None, None]
    last_z = (VS, VS.n)
    BI = len(iters)

    def u_slice(t):
        return ub.ap()[:, t * CH:(t + 1) * CH]

    def emit_u_group(it, t0, wait_tok):
        b, c = iters[it]
        fence("sync", LD)
        load(ub.ap()[:, t0 * CH:(t0 + 4) * CH],
             scr_u[b, t0 * 128:(t0 + 4) * 128, c * CH:(c + 1) * CH]
             .rearrange("(k p) j -> p k j", p=128),
             wait=[A_done_ST, wait_tok])
        tok = (LD, LD.n)
        for t in range(t0, t0 + 4):
            ut_tok[(it, t)] = tok

    for g4 in range(4):
        emit_u_group(0, 4 * g4, None)
    fence("sync", LD)
    b0, c0 = iters[0]
    for n in range(2 * NST):
        load(xbc.ap()[:, n * CH:(n + 1) * CH],
             x_dbl.ap()[64 + n:65 + n, b0 * L + c0 * CH:b0 * L + (c0 + 1) * CH],
             wait=[A_done_S])
    gld = {0: (LD, LD.n)}
    tok_bc = {}
    pend = []

    def flush_pend(tok, nxt_it):
        while pend:
            pit, ps, ygap = pend.pop(0)
            pb_, pc_ = iters[pit]
            yg_st[ps % 2] = store(
                scr_yg[pb_, ps * 128:(ps + 1) * 128,
                       pc_ * CH:(pc_ + 1) * CH], ygap, wait=[tok])
            if nxt_it is not None and ps % 4 == 3:
                emit_u_group(nxt_it, ps - 3, tok)

    def emit_scans(it, s):
        b, c = iters[it]
        par = s % 2
        g = it * NT + s
        dab = da3.ap()[:, par, :, :]
        sti = states.ap()[:, (b * NT + s) * NST:(b * NT + s + 1) * NST]
        for n in range(NST):
            sch.call("vector", "tensor_mul", dbu_b.ap(),
                     w4.ap()[:, s % 4, :],
                     bc_sb.ap()[:, n * CH:(n + 1) * CH],
                     wait=[da_t[g], tok_bc[it]])
            sch.call("vector", "tensor_tensor_scan", s3.ap()[:, n, :],
                     dab[:, n, :], dbu_b.ap(), sti[:, n:n + 1],
                     op0=ALU.mult, op1=ALU.add)
        sch.call("vector", "tensor_copy", sti, s3.ap()[:, :, CH - 1],
                 raw=True)
        for n in range(NST):
            sch.call("vector", "tensor_mul", s3.ap()[:, n, :],
                     s3.ap()[:, n, :],
                     bc_sb.ap()[:, (NST + n) * CH:(NST + n + 1) * CH])
        ztoks[g] = tok_z = sch.mark("vector", VS)
        nxt = it + 1 if it + 1 < BI else None
        flush_pend(tok_z, nxt)
        py = pb[2 + par]
        for n in range(NST):
            sch.call("tensor", "matmul", py.ap()[:, 0:CH], ident.ap(),
                     s3.ap()[:, n, :], wait=[tok_z, py_free[par]],
                     start=(n == 0), stop=(n == NST - 1))
        tok_y = sch.mark("tensor", TS)
        ygap = yg_sb.ap()[:, par * CH:(par + 1) * CH]
        vev = sch.call("vector", "scalar_tensor_tensor", ygap, u_slice(s),
                       dvec_c.ap()[:, s:s + 1], py.ap()[:, 0:CH],
                       wait=[tok_y, yg_st[par]],
                       op0=ALU.mult, op1=ALU.add, inc=VS, raw=True)
        py_free[par] = vev
        pend.append((it, s, ygap))

    for it, (b, c) in (enumerate(iters) if run_abc else []):
        bcdone = None
        for n in range(2 * NST):
            bi = 4 + (n % 2)
            tmmi = sch.call("tensor", "matmul", pb[bi].ap()[:, 0:CH],
                            ones_row.ap(),
                            xbc.ap()[:, n * CH:(n + 1) * CH],
                            wait=[gld[it], bank_free[bi], tok_ident],
                            start=True, stop=True, inc=TS)
            bank_free[bi] = sch.call("scalar", "copy",
                                     bc_sb.ap()[:, n * CH:(n + 1) * CH],
                                     pb[bi].ap()[:, 0:CH],
                                     wait=[tmmi, last_z], inc=SS)
            bcdone = tmmi
        tok_bc[it] = bank_free[5]
        if it + 1 < BI:
            bn, cn = iters[it + 1]
            fence("sync", LD)
            for n in range(2 * NST):
                load(xbc.ap()[:, n * CH:(n + 1) * CH],
                     x_dbl.ap()[64 + n:65 + n,
                                bn * L + cn * CH:bn * L + (cn + 1) * CH],
                     wait=[bcdone])
            gld[it + 1] = (LD, LD.n)

        sp_t = {}

        def emit_dsp(t):
            par = t % 2
            g = it * NT + t
            pd = pb[par]
            tok_pd = sch.call("tensor", "matmul", pd.ap()[:, 0:CH],
                              w_dtow.ap()[:, t * 128:(t + 1) * 128],
                              dtm_sb.ap()[:, b * L + c * CH:
                                          b * L + (c + 1) * CH],
                              wait=[tok_dtm, pd_free[par]],
                              start=True, stop=True, inc=TS)
            bcol = dtob_c.ap()[:, t:t + 1]
            t1 = t14.ap()[:, par, :]
            t2 = t24.ap()[:, par, :]
            sch.call("scalar", "activation", t1, pd.ap()[:, 0:CH],
                     AF.Abs, bias=bcol, wait=[tok_pd, ztoks.get(g - 4)])
            sch.call("scalar", "activation", t1, t1, AF.Exp, scale=-1.0)
            sch.call("scalar", "add", t1, t1, 1.0)
            sch.call("scalar", "activation", t1, t1, AF.Ln)
            sp_t[t] = sch.call("scalar", "activation", t2, pd.ap()[:, 0:CH],
                               AF.Relu, bias=bcol, inc=SS)
            pd_free[par] = sp_t[t]

        emit_dsp(0)
        for t in range(NT):
            par = t % 2
            p4 = t % 4
            g = it * NT + t
            if t + 1 < NT:
                emit_dsp(t + 1)
            sch.call("vector", "tensor_add", dt4.ap()[:, par, :],
                     t14.ap()[:, par, :], t24.ap()[:, par, :],
                     wait=[sp_t[t], ut_tok[(it, t)], da_t.get(g - 2)])
            wtok = sch.call("vector", "tensor_mul", w4.ap()[:, p4, :],
                            dt4.ap()[:, par, :], u_slice(t),
                            inc=VS if t < 2 else None)
            if t >= 2:
                emit_scans(it, t - 2)
            dab = da3.ap()[:, par, :, :]
            for n in range(NST):
                sch.call("scalar", "activation", dab[:, n, :],
                         dt4.ap()[:, par, :], AF.Exp,
                         scale=a_neg.ap()[:, t * NST + n:t * NST + n + 1],
                         wait=[ztoks.get(g - 2), wtok])
            da_t[g] = sch.mark("scalar", SS)
        emit_scans(it, NT - 2)
        emit_scans(it, NT - 1)
        ztoks[it * NT + NT] = fin = sch.call("vector", "engine_nop",
                                             inc=VS)
        flush_pend(fin, it + 1 if it + 1 < BI else None)
        last_z = fin
        fence("gpsimd", ST)

    bm_scope.close()
    B_done_V = (VS, VS.n)
    B_done_S = (SS, SS.n)
    B_done_T = (TS, TS.n)
    B_done_ST = (ST, ST.n)

    if dbg and run_abc:
        fence("gpsimd", ST)
        store(d_xdbl, x_dbl.ap(), wait=[A_done_S])
        store(d_dtm, dtm_sb.ap(), wait=[tok_dtm])
        fence("gpsimd", ST)

    c_scope = ExitStack()
    mean_r = rows.ap()[:, 0:CH]
    var_r = rows.ap()[:, CH:2 * CH]
    m2_r = rows.ap()[:, 2 * CH:3 * CH]
    rstd_r = rows.ap()[:, 3 * CH:4 * CH]
    brow_r = rows.ap()[:, 4 * CH:5 * CH]
    wop = salloc(c_scope, "wop", [128, NT * DM])
    ygc = salloc(c_scope, "ygc", [128, NT * CH])
    szc = salloc(c_scope, "szc", [128, NT * CH])
    xc = salloc(c_scope, "xc", [128, 8 * CH])
    oc = salloc(c_scope, "oc", [128, 8 * CH])
    sqc = salloc(c_scope, "sqc", [128, CH])

    fence("sync", LD)
    for k in range(NT):
        load(wop.ap()[:, k * DM:(k + 1) * DM],
             d_opWT[k * 128:(k + 1) * 128, :], wait=[B_done_T])
    tok_wop = (LD, LD.n)

    prevC = {"vdone": B_done_V, "tdone": B_done_T, "sdone": B_done_S,
             "stores": B_done_ST, "rows_S": None, "stats_rd": None,
             "rowsT": None}

    for it, (b, c) in (enumerate(iters) if run_abc else []):
        fence("sync", LD)
        load(ygc.ap(),
             scr_yg[b, :, c * CH:(c + 1) * CH]
             .rearrange("(k p) j -> p k j", p=128),
             wait=[B_done_ST, prevC["vdone"], prevC["tdone"]])
        load(szc.ap(),
             scr_sz[b, :, c * CH:(c + 1) * CH]
             .rearrange("(k p) j -> p k j", p=128))
        load(xc.ap(),
             dx[b, :, c * CH:(c + 1) * CH]
             .rearrange("(k p) j -> p k j", p=128))
        cld = (LD, LD.n)

        for t in range(NT):
            sch.call("vector", "tensor_mul", ygc.ap()[:, t * CH:(t + 1) * CH],
                     ygc.ap()[:, t * CH:(t + 1) * CH],
                     szc.ap()[:, t * CH:(t + 1) * CH],
                     wait=[cld, prevC["tdone"]])
        tok_yf = sch.mark("vector", VS)

        sq_prev = None
        tok_pb2 = None
        for o in range(8):
            bi = o % 2
            for k in range(NT):
                sch.call("tensor", "matmul", pb[bi].ap()[:, 0:CH],
                         wop.ap()[:, k * DM + o * 128:k * DM + (o + 1) * 128],
                         ygc.ap()[:, k * CH:(k + 1) * CH],
                         wait=[tok_yf, tok_wop, bank_free[bi]],
                         start=(k == 0), stop=(k == NT - 1))
            tmm = sch.mark("tensor", TS)
            ocap = oc.ap()[:, o * CH:(o + 1) * CH]
            bank_free[bi] = sch.call("scalar", "copy", ocap,
                                     pb[bi].ap()[:, 0:CH],
                                     wait=[tmm, prevC["vdone"]], inc=SS)
            stok = sch.call("scalar", "activation", sqc.ap(), ocap, AF.Square,
                            wait=[sq_prev], inc=SS)
            sch.call("tensor", "matmul", pb[2].ap()[0:1, 0:CH], ones_col.ap(),
                     ocap, wait=[bank_free[bi], prevC["stats_rd"]],
                     start=(o == 0), stop=(o == 7))
            if o == 7:
                tok_pb2 = sch.mark("tensor", TS)
            sq_prev = sch.call("tensor", "matmul", pb[3].ap()[0:1, 0:CH],
                               ones_col.ap(), sqc.ap(),
                               wait=[stok, prevC["stats_rd"]],
                               start=(o == 0), stop=(o == 7), inc=TS)
        sch.call("vector", "tensor_scalar", mean_r, pb[2].ap()[0:1, 0:CH],
                 1.0 / DM, None, wait=[tok_pb2, sq_prev, prevC["rowsT"],
                                       prevC["rows_S"]], op0=ALU.mult)
        sch.call("vector", "tensor_scalar", var_r, pb[3].ap()[0:1, 0:CH],
                 1.0 / DM, None, op0=ALU.mult)
        sch.call("vector", "tensor_tensor", m2_r, mean_r, mean_r, op=ALU.mult)
        sch.call("vector", "tensor_tensor", var_r, var_r, m2_r,
                 op=ALU.subtract)
        stats_rd = sch.call("vector", "tensor_scalar", var_r, var_r, EPS,
                            None, op0=ALU.add, inc=VS)
        prevC["stats_rd"] = stats_rd
        sl = sch.call("scalar", "activation", rstd_r, var_r, AF.Ln,
                      wait=[stats_rd])
        sl = sch.call("scalar", "activation", rstd_r, rstd_r, AF.Exp,
                      scale=-0.5, inc=SS)
        sch.call("vector", "tensor_tensor", brow_r, mean_r, rstd_r,
                 wait=[sl], op=ALU.mult)
        rows_done = sch.call("vector", "tensor_scalar", brow_r, brow_r, -1.0,
                             None, op0=ALU.mult, inc=VS)
        prevC["rows_S"] = sl
        tb1 = sch.call("tensor", "matmul", pb[4].ap()[:, 0:CH],
                       ones_row.ap(), rstd_r,
                       wait=[rows_done, bank_free[4]],
                       start=True, stop=True)
        tb2 = sch.call("tensor", "matmul", pb[5].ap()[:, 0:CH],
                       ones_row.ap(), brow_r,
                       wait=[rows_done, bank_free[5]],
                       start=True, stop=True, inc=TS)
        prevC["rowsT"] = tb2
        for o in range(8):
            ocap = oc.ap()[:, o * CH:(o + 1) * CH]
            sch.call("vector", "tensor_tensor", ocap, ocap,
                     pb[4].ap()[:, 0:CH], wait=[tb2, prevC["stores"]],
                     op=ALU.mult)
            sch.call("vector", "tensor_tensor", ocap, ocap,
                     pb[5].ap()[:, 0:CH], op=ALU.add)
            vx2 = sch.call("vector", "tensor_tensor", ocap, ocap,
                           xc.ap()[:, o * CH:(o + 1) * CH], op=ALU.add,
                           inc=VS)
        store(scr_x2[b, :, c * CH:(c + 1) * CH]
              .rearrange("(o p) j -> p o j", p=128),
              oc.ap(), wait=[vx2])
        bank_free[4] = bank_free[5] = (VS, VS.n)
        prevC["vdone"] = (VS, VS.n)
        prevC["tdone"] = (TS, TS.n)
        prevC["stores"] = (ST, ST.n)
        fence("gpsimd", ST)

    c_scope.close()
    mk_scope.close()
    C_done_V = (VS, VS.n)
    C_done_S = (SS, SS.n)
    C_done_T = (TS, TS.n)
    C_done_ST = (ST, ST.n)

    f0_scope = ExitStack()
    wt0 = salloc(f0_scope, "wt0", [128, 2 * 512])
    absb = salloc(f0_scope, "absb", [128, 1024])
    ones128 = salloc(f0_scope, "ones128", [128, 128])
    colacc = salloc(f0_scope, "colacc", [128, 8])
    tcol = salloc(f0_scope, "tcol", [128, 1])
    mcol = salloc(f0_scope, "mcol", [128, 1])
    q32a = salloc(f0_scope, "q32a", [128, 1024])
    q32b = salloc(f0_scope, "q32b", [128, 1024])
    wqb = salloc(f0_scope, "wqb", [128, 1024], BF16)
    wq_c = salloc(f0_scope, "wq_c", [128, 2 * 3])

    mats = [(d_gWT, 8, 8, DM * 4096), (d_uWT, 8, 8, DM * 4096),
            (d_dWT, 32, 2, DM * 4096)]
    wq_kind = "ExternalOutput" if dbg else "Internal"
    scr_wq = [nc.dram_tensor("scr_wgq", [DM, 4096], BF16, kind=wq_kind).ap(),
              nc.dram_tensor("scr_wuq", [DM, 4096], BF16, kind=wq_kind).ap(),
              nc.dram_tensor("scr_wdq", [4096, DM], BF16, kind=wq_kind).ap()]

    prevF0 = {"s": C_done_S, "v": C_done_V, "t": C_done_T, "v2": None}
    sch.call("vector", "memset", ones128.ap(), 1.0, wait=[C_done_T], inc=VS)
    for mi, (dsrc, nk, ncc, nel) in enumerate(mats):
        vz = sch.call("vector", "memset", colacc.ap(), 0.0,
                      wait=[prevF0["t"]], inc=VS)
        sab_prev = None
        vac = None
        for k in range(nk):
            for cc in range(ncc // 2):
                fence("sync", LD)
                wl = load(wt0.ap(),
                          dsrc[k * 128:(k + 1) * 128,
                               cc * 1024:(cc + 1) * 1024],
                          wait=[prevF0["s"], prevF0["t"], prevF0["v2"],
                                vac, sab_prev])
                for hh in range(2):
                    sab = sch.call("scalar", "activation",
                                   absb.ap()[:, hh * 512:(hh + 1) * 512],
                                   wt0.ap()[:, hh * 512:(hh + 1) * 512],
                                   AF.Abs, wait=[wl, vac], inc=SS)
                    sab_prev = sab
                    sch.call("vector", "tensor_reduce", tcol.ap(),
                             absb.ap()[:, hh * 512:(hh + 1) * 512],
                             AX.X, ALU.add, wait=[sab])
                    vac = sch.call("vector", "tensor_add",
                                   colacc.ap()[:, 0:1],
                                   colacc.ap()[:, 0:1], tcol.ap(), inc=VS)
        tbt = sch.call("tensor", "matmul", pb[3].ap()[:, 0:8], ones128.ap(),
                       colacc.ap(), wait=[vac, bank_free[3]],
                       start=True, stop=True, inc=TS)
        smc = sch.call("scalar", "copy", mcol.ap(), pb[3].ap()[:, 0:1],
                       wait=[tbt], inc=SS)
        bank_free[3] = smc
        sch.call("vector", "tensor_scalar", mcol.ap(), mcol.ap(), 1.0 / nel,
                 None, wait=[smc], op0=ALU.mult)
        vmc = sch.call("vector", "tensor_scalar", mcol.ap(), mcol.ap(), 1e-5,
                       None, op0=ALU.max, inc=VS)
        sch.call("vector", "tensor_copy", cells.ap()[:, mi:mi + 1],
                 mcol.ap()[0:1, 0:1])
        sch.call("vector", "tensor_scalar", wq_c.ap()[:, 2 * mi:2 * mi + 1],
                 mcol.ap(), 0.5, None, op0=ALU.mult)
        vcn = sch.call("vector", "tensor_scalar",
                       wq_c.ap()[:, 2 * mi + 1:2 * mi + 2],
                       wq_c.ap()[:, 2 * mi:2 * mi + 1], -0.5 / 0.5, None,
                       op0=ALU.mult, inc=VS)
        vq_prev = vcn
        st_prev = None
        for k in range(nk):
            for cc in range(ncc // 2):
                fence("sync", LD)
                wl = load(wt0.ap(),
                          dsrc[k * 128:(k + 1) * 128,
                               cc * 1024:(cc + 1) * 1024], wait=[vq_prev])
                sch.call("vector", "tensor_scalar", q32a.ap(), wt0.ap(),
                         wq_c.ap()[:, 2 * mi:2 * mi + 1], None,
                         wait=[wl, vcn, st_prev], op0=ALU.is_gt)
                sch.call("vector", "tensor_scalar", q32b.ap(), wt0.ap(),
                         wq_c.ap()[:, 2 * mi + 1:2 * mi + 2], None,
                         op0=ALU.is_lt)
                vq_prev = sch.call("vector", "tensor_tensor", wqb.ap(),
                                   q32a.ap(), q32b.ap(), op=ALU.subtract,
                                   inc=VS)
                st_prev = store(scr_wq[mi][k * 128:(k + 1) * 128,
                                           cc * 1024:(cc + 1) * 1024],
                                wqb.ap(), wait=[vq_prev])
                fence("gpsimd", ST)
        prevF0["s"] = (SS, SS.n)
        prevF0["t"] = (TS, TS.n)
        prevF0["v2"] = vq_prev
    f0_scope.close()
    F0_done_V = (VS, VS.n)
    F0_done_ST = (ST, ST.n)

    f_scope = ExitStack()
    TB = 2 * CH
    x2blk = salloc(f_scope, "x2blk", [128, 8 * TB])
    xh3 = salloc(f_scope, "xh3", [128, 8 * TB])
    xq_b = salloc(f_scope, "xq_b", [128, 8 * TB], BF16)
    hp = salloc(f_scope, "hp", [128, 32 * TB])
    hq_b = salloc(f_scope, "hq_b", [128, 32 * TB], BF16)
    fc = salloc(f_scope, "fc", [128, 8 * TB])
    wgq_sb = salloc(f_scope, "wgq_sb", [128, 2 * 8 * 128], BF16)
    wuq_sb = salloc(f_scope, "wuq_sb", [128, 2 * 8 * 128], BF16)
    wdq_sb = salloc(f_scope, "wdq_sb", [128, 2 * 32 * 128], BF16)
    gdt = salloc(f_scope, "gdt", [128, TB])
    sgt = salloc(f_scope, "sgt", [128, TB])
    i32b = salloc(f_scope, "i32b", [128, TB], I32)
    rmx = salloc(f_scope, "rmx", [128, TB])
    sq_f = salloc(f_scope, "sq_f", [128, TB])
    bcf = salloc(f_scope, "bcf", [128, 4 * TB])
    f_rows = salloc(f_scope, "f_rows", [1, 5 * TB])

    BCR, BCB, BCSX, BCIG = range(4)
    BCSH, BCDR = 0, 1

    def bcf_s(i):
        return bcf.ap()[:, i * TB:(i + 1) * TB]

    mean_f = f_rows.ap()[:, 0 * TB:1 * TB]
    var_f = f_rows.ap()[:, 1 * TB:2 * TB]
    rstd_f = f_rows.ap()[:, 2 * TB:3 * TB]
    brow_f = f_rows.ap()[:, 3 * TB:4 * TB]
    aux_f = f_rows.ap()[:, 4 * TB:5 * TB]

    colT = salloc(f_scope, "colT", [128, 8])

    def part_tree_max(src_ap, out_row, wait=()):
        w = list(wait) + [tok_ident]
        tokv = None
        for cb in range(TB // 128):
            bk = pb[4 + (cb % 2)]
            tt = sch.call("tensor", "transpose", bk.ap()[:, 0:128],
                          src_ap[:, cb * 128:(cb + 1) * 128], ident.ap(),
                          wait=w + [bank_free[4 + (cb % 2)]], inc=TS)
            tokv = sch.call("vector", "tensor_reduce", colT.ap()[:, cb:cb + 1],
                            bk.ap()[:, 0:128], AX.X, ALU.max,
                            wait=[tt], inc=VS)
            bank_free[4 + (cb % 2)] = tokv
        toks = None
        for cb in range(TB // 128):
            bk = pb[4 + (cb % 2)]
            tt = sch.call("tensor", "transpose", bk.ap()[0:1, 0:128],
                          colT.ap()[:, cb:cb + 1], ident.ap(),
                          wait=[tokv, bank_free[4 + (cb % 2)]], inc=TS)
            toks = sch.call("scalar", "copy",
                            out_row[:, cb * 128:(cb + 1) * 128],
                            bk.ap()[0:1, 0:128], wait=[tt], inc=SS)
            bank_free[4 + (cb % 2)] = toks
        return toks

    def ln_rows(tok_sum, tok_sumsq, pbs, pbq, prev_rowuser):
        sch.call("vector", "tensor_scalar", mean_f, pbs, 1.0 / DM, None,
                 wait=[tok_sum, tok_sumsq, prev_rowuser], op0=ALU.mult)
        sch.call("vector", "tensor_scalar", var_f, pbq, 1.0 / DM, None,
                 op0=ALU.mult)
        sch.call("vector", "tensor_tensor", aux_f, mean_f, mean_f,
                 op=ALU.mult)
        sch.call("vector", "tensor_tensor", var_f, var_f, aux_f,
                 op=ALU.subtract)
        vrd = sch.call("vector", "tensor_scalar", var_f, var_f, EPS, None,
                       op0=ALU.add, inc=VS)
        sch.call("scalar", "activation", rstd_f, var_f, AF.Ln, wait=[vrd])
        sl = sch.call("scalar", "activation", rstd_f, rstd_f, AF.Exp,
                      scale=-0.5, inc=SS)
        sch.call("vector", "tensor_tensor", brow_f, mean_f, rstd_f,
                 wait=[sl], op=ALU.mult)
        return sch.call("vector", "tensor_scalar", brow_f, brow_f, -1.0,
                        None, op0=ALU.mult, inc=VS), vrd

    def bcast2(r1, r2, wait=()):
        t1b = sch.call("tensor", "matmul", pb[4].ap()[:, 0:TB], ones_row.ap(),
                       r1[0], wait=list(wait) + [bank_free[4]],
                       start=True, stop=True, inc=TS)
        s1b = sch.call("scalar", "copy", bcf_s(r1[1]), pb[4].ap()[:, 0:TB],
                       wait=[t1b, r1[2]], inc=SS)
        bank_free[4] = s1b
        if r2 is None:
            return s1b
        t2b = sch.call("tensor", "matmul", pb[5].ap()[:, 0:TB], ones_row.ap(),
                       r2[0], wait=list(wait) + [bank_free[5]],
                       start=True, stop=True, inc=TS)
        s2b = sch.call("scalar", "copy", bcf_s(r2[1]), pb[5].ap()[:, 0:TB],
                       wait=[t2b, r2[2]], inc=SS)
        bank_free[5] = s2b
        return s2b

    prevF = {"v": F0_done_V, "t": (TS, TS.n), "s": (SS, SS.n),
             "stores": (ST, ST.n), "stats": bank_free[3], "rowuser": None}

    for j in range(NCH):
        fence("sync", LD)
        for bb in range(B):
            load(x2blk.ap().rearrange("p (o z) -> p o z", z=TB)
                 [:, :, bb * CH:(bb + 1) * CH],
                 scr_x2[bb, :, j * CH:(j + 1) * CH]
                 .rearrange("(o p) j -> p o j", p=128),
                 wait=[C_done_ST, prevF["v"], prevF["t"]])
        xl = (LD, LD.n)

        sqp = None
        tok_sum = None
        for o in range(8):
            xo = x2blk.ap()[:, o * TB:(o + 1) * TB]
            stok = sch.call("scalar", "activation", sq_f.ap(), xo, AF.Square,
                            wait=[xl, prevF["stats"], sqp], inc=SS)
            sch.call("tensor", "matmul", pb[3].ap()[0:1, 0:TB], ones_col.ap(),
                     xo, wait=[xl, prevF["stats"]],
                     start=(o == 0), stop=(o == 7))
            if o == 7:
                tok_sum = sch.mark("tensor", TS)
            sqp = sch.call("tensor", "matmul", pb[6].ap()[0:1, 0:TB],
                           ones_col.ap(), sq_f.ap(), wait=[stok],
                           start=(o == 0), stop=(o == 7), inc=TS)
        rows_done, _ = ln_rows(tok_sum, sqp, pb[3].ap()[0:1, 0:TB],
                               pb[6].ap()[0:1, 0:TB], prevF["rowuser"])
        sb = bcast2((rstd_f, BCR, None), (brow_f, BCB, None),
                    wait=[rows_done])
        vx = None
        for o in range(8):
            xo = x2blk.ap()[:, o * TB:(o + 1) * TB]
            ho = xh3.ap()[:, o * TB:(o + 1) * TB]
            sch.call("vector", "tensor_tensor", ho, xo, bcf_s(BCR),
                     wait=[sb], op=ALU.mult)
            vt = sch.call("vector", "tensor_tensor", ho, ho, bcf_s(BCB),
                          op=ALU.add, inc=VS)
            sab = sch.call("scalar", "activation", sq_f.ap(), ho, AF.Abs,
                           wait=[vt], inc=SS)
            if o == 0:
                sch.call("vector", "tensor_copy", rmx.ap(), sq_f.ap(),
                         wait=[sab])
            else:
                sch.call("vector", "tensor_tensor", rmx.ap(), rmx.ap(),
                         sq_f.ap(), wait=[sab], op=ALU.max)
        vx = sch.mark("vector", VS)
        ptm = part_tree_max(rmx.ap(), mean_f, wait=[vx])
        sch.call("vector", "tensor_scalar", mean_f, mean_f, 1e-5, None,
                 wait=[ptm], op0=ALU.max)
        sch.call("vector", "reciprocal", var_f, mean_f)
        sch.call("vector", "tensor_scalar", rstd_f, var_f, 127.0, None,
                 op0=ALU.mult)
        sch.call("vector", "tensor_scalar", brow_f, mean_f,
                 cells.ap()[:, 0:1], None, op0=ALU.mult)
        vig = sch.call("vector", "tensor_scalar", brow_f, brow_f,
                       1.0 / 127.0, None, op0=ALU.mult, inc=VS)
        sb2 = bcast2((rstd_f, BCSX, None), (brow_f, BCIG, None), wait=[vig])
        vq = None
        for o in range(8):
            ho = xh3.ap()[:, o * TB:(o + 1) * TB]
            sch.call("vector", "tensor_tensor", ho, ho, bcf_s(BCSX),
                     wait=[sb2], op=ALU.mult)
            sch.call("vector", "tensor_copy", i32b.ap(), ho)
            vq = sch.call("vector", "tensor_copy",
                          xq_b.ap()[:, o * TB:(o + 1) * TB], i32b.ap(),
                          inc=VS)
        if dbg and j == 0:
            for o in range(8):
                cv = sch.call("vector", "tensor_copy", sq_f.ap(),
                              xq_b.ap()[:, o * TB:(o + 1) * TB],
                              wait=[vq, (ST, ST.n)], inc=VS)
                store(d_xq[:, o * TB:(o + 1) * TB], sq_f.ap(), wait=[cv])
            fence("gpsimd", ST)
        vhm = None
        wld = {}
        tsu = {}
        for et in range(33):
            if et < 32:
                par = et % 2
                fence("sync", LD)
                load(wgq_sb.ap()[:, par * 1024:(par + 1) * 1024],
                     scr_wq[0][:, et * 128:(et + 1) * 128]
                     .rearrange("(k p) j -> p k j", p=128),
                     wait=[F0_done_ST, tsu.get(et - 2)])
                load(wuq_sb.ap()[:, par * 1024:(par + 1) * 1024],
                     scr_wq[1][:, et * 128:(et + 1) * 128]
                     .rearrange("(k p) j -> p k j", p=128))
                wld[et] = (LD, LD.n)
            if et == 0:
                continue
            e = et - 1
            par = e % 2
            bg = 0 if par == 0 else 7
            bu = 1 if par == 0 else 2
            for k in range(8):
                sch.call("tensor", "matmul", pb[bg].ap()[:, 0:TB],
                         wgq_sb.ap()[:, (par * 8 + k) * 128:
                                     (par * 8 + k + 1) * 128],
                         xq_b.ap()[:, k * TB:(k + 1) * TB],
                         wait=[wld[e], vq, bank_free[bg]],
                         start=(k == 0), stop=(k == 7))
            tg = sch.mark("tensor", TS)
            for k in range(8):
                sch.call("tensor", "matmul", pb[bu].ap()[:, 0:TB],
                         wuq_sb.ap()[:, (par * 8 + k) * 128:
                                     (par * 8 + k + 1) * 128],
                         xq_b.ap()[:, k * TB:(k + 1) * TB],
                         wait=[wld[e], vq, bank_free[bu]],
                         start=(k == 0), stop=(k == 7))
            tu = sch.mark("tensor", TS)
            tsu[e] = tu
            vg = sch.call("vector", "tensor_tensor", gdt.ap(),
                          pb[bg].ap()[:, 0:TB], bcf_s(BCIG),
                          wait=[tg, sb2], op=ALU.mult, inc=VS)
            bank_free[bg] = vg
            sg = sch.call("scalar", "activation", sgt.ap(), gdt.ap(),
                          AF.Silu, wait=[vg], inc=SS)
            he = hp.ap()[:, e * TB:(e + 1) * TB]
            vh = sch.call("vector", "tensor_tensor", he, sgt.ap(),
                          pb[1 if par == 0 else 2].ap()[:, 0:TB],
                          wait=[sg, tu], op=ALU.mult, inc=VS)
            bank_free[bu] = vh
            sabh = sch.call("scalar", "activation", sq_f.ap(), he, AF.Abs,
                            wait=[vh], inc=SS)
            if e == 0:
                vhm = sch.call("vector", "tensor_copy", rmx.ap(), sq_f.ap(),
                               wait=[sabh], inc=VS)
            else:
                vhm = sch.call("vector", "tensor_tensor", rmx.ap(), rmx.ap(),
                               sq_f.ap(), wait=[sabh], op=ALU.max, inc=VS)
        if dbg and j == 0:
            for e in range(32):
                store(d_hp[:, e * TB:(e + 1) * TB],
                      hp.ap()[:, e * TB:(e + 1) * TB], wait=[vhm])
            fence("gpsimd", ST)
            sch.call("vector", "nop", wait=[(ST, ST.n)])
        ptm2 = part_tree_max(rmx.ap(), var_f, wait=[vhm])
        sch.call("vector", "tensor_scalar", aux_f, var_f,
                 cells.ap()[:, 1:2], None, wait=[ptm2], op0=ALU.mult)
        sch.call("vector", "tensor_tensor", aux_f, aux_f, mean_f,
                 op=ALU.mult)
        sch.call("vector", "tensor_scalar", aux_f, aux_f, 1.0 / 127.0, None,
                 op0=ALU.mult)
        sch.call("vector", "tensor_scalar", aux_f, aux_f, 1e-5, None,
                 op0=ALU.max)
        sch.call("vector", "reciprocal", var_f, aux_f)
        sch.call("vector", "tensor_tensor", var_f, var_f, mean_f,
                 op=ALU.mult)
        sch.call("vector", "tensor_scalar", var_f, var_f,
                 cells.ap()[:, 1:2], None, op0=ALU.mult)
        sch.call("vector", "tensor_scalar", aux_f, aux_f,
                 cells.ap()[:, 2:3], None, op0=ALU.mult)
        vdr = sch.call("vector", "tensor_scalar", aux_f, aux_f, 1.0 / 127.0,
                       None, op0=ALU.mult, inc=VS)
        sb3 = bcast2((var_f, BCSH, None), (aux_f, BCDR, None), wait=[vdr])
        vq2 = None
        for e in range(32):
            he = hp.ap()[:, e * TB:(e + 1) * TB]
            sch.call("vector", "tensor_tensor", he, he, bcf_s(BCSH),
                     wait=[sb3], op=ALU.mult)
            sch.call("vector", "tensor_copy", i32b.ap(), he)
            vq2 = sch.call("vector", "tensor_copy",
                           hq_b.ap()[:, e * TB:(e + 1) * TB], i32b.ap(),
                           inc=VS)
        wld2 = {}
        tsd = {}
        for ot in range(9):
            if ot < 8:
                par = ot % 2
                fence("sync", LD)
                load(wdq_sb.ap()[:, par * 4096:(par + 1) * 4096],
                     scr_wq[2][:, ot * 128:(ot + 1) * 128]
                     .rearrange("(k p) j -> p k j", p=128),
                     wait=[F0_done_ST, tsd.get(ot - 2)])
                wld2[ot] = (LD, LD.n)
            if ot == 0:
                continue
            o = ot - 1
            par = o % 2
            for k in range(32):
                sch.call("tensor", "matmul", pb[2].ap()[:, 0:TB],
                         wdq_sb.ap()[:, (par * 32 + k) * 128:
                                     (par * 32 + k + 1) * 128],
                         hq_b.ap()[:, k * TB:(k + 1) * TB],
                         wait=[wld2[o], vq2, bank_free[2]],
                         start=(k == 0), stop=(k == 31))
            td = sch.mark("tensor", TS)
            tsd[o] = td
            fo = fc.ap()[:, o * TB:(o + 1) * TB]
            vf = sch.call("vector", "tensor_tensor", fo,
                          pb[2].ap()[:, 0:TB], bcf_s(BCDR),
                          wait=[td, sb3], op=ALU.mult, inc=VS)
            bank_free[2] = vf
        if dbg and j == 0:
            for o in range(8):
                store(d_fc[:, o * TB:(o + 1) * TB],
                      fc.ap()[:, o * TB:(o + 1) * TB], wait=[vf])
            fence("gpsimd", ST)
            sch.call("vector", "nop", wait=[(ST, ST.n)])
        sqp = None
        tok_sum = None
        for o in range(8):
            fo = fc.ap()[:, o * TB:(o + 1) * TB]
            stok = sch.call("scalar", "activation", sq_f.ap(), fo, AF.Square,
                            wait=[vf, sqp], inc=SS)
            sch.call("tensor", "matmul", pb[3].ap()[0:1, 0:TB], ones_col.ap(),
                     fo, wait=[vf, bank_free[3]],
                     start=(o == 0), stop=(o == 7))
            if o == 7:
                tok_sum = sch.mark("tensor", TS)
            sqp = sch.call("tensor", "matmul", pb[6].ap()[0:1, 0:TB],
                           ones_col.ap(), sq_f.ap(), wait=[stok],
                           start=(o == 0), stop=(o == 7), inc=TS)
        rows_done, vrd4 = ln_rows(tok_sum, sqp, pb[3].ap()[0:1, 0:TB],
                                  pb[6].ap()[0:1, 0:TB], None)
        bank_free[3] = vrd4
        sb4 = bcast2((rstd_f, BCSX, None), (brow_f, BCIG, None),
                     wait=[rows_done])
        for o in range(8):
            fo = fc.ap()[:, o * TB:(o + 1) * TB]
            xo = x2blk.ap()[:, o * TB:(o + 1) * TB]
            sch.call("vector", "tensor_tensor", fo, fo, bcf_s(BCSX),
                     wait=[sb4, prevF["stores"]], op=ALU.mult)
            sch.call("vector", "tensor_tensor", fo, fo, bcf_s(BCIG),
                     op=ALU.add)
            vx3 = sch.call("vector", "tensor_tensor", fo, fo, xo,
                           op=ALU.add, inc=VS)
        for bb in range(B):
            store(d_out[bb, :, j * CH:(j + 1) * CH]
                  .rearrange("(o p) j -> p o j", p=128),
                  fc.ap().rearrange("p (o z) -> p o z", z=TB)
                  [:, :, bb * CH:(bb + 1) * CH], wait=[vx3])
        prevF["v"] = (VS, VS.n)
        prevF["t"] = (TS, TS.n)
        prevF["stats"] = vrd4
        prevF["rowuser"] = sb4
        prevF["stores"] = (ST, ST.n)
        fence("gpsimd", ST)

    f_scope.close()
    F_done_V = (VS, VS.n)
    F_done_S = (SS, SS.n)
    F_done_T = (TS, TS.n)

    sch.call("gpsimd", "nop", wait=[(ST, ST.n), (LD, LD.n), C_done_V,
                                    C_done_S, C_done_T])

    sch.finalize()
    stack.close()
    return nc


def _prep_inputs(inputs):
    def T(a):
        return np.ascontiguousarray(np.asarray(a, dtype=np.float32).T)

    def C(a):
        return np.ascontiguousarray(np.asarray(a, dtype=np.float32))

    def V(a):
        return np.ascontiguousarray(
            np.asarray(a, dtype=np.float32).reshape(-1, 1))

    return {
        "inWT": T(inputs["L0_attn_in_proj_w"]),
        "convW": C(inputs["L0_attn_conv_w"]),
        "convB": V(inputs["L0_attn_conv_b"]),
        "xprojWT": T(inputs["L0_attn_x_proj_w"]),
        "Alog": C(inputs["L0_attn_A_log"]),
        "Dvec": V(inputs["L0_attn_D"]),
        "dtowT": T(inputs["L0_attn_dt_out_w"]),
        "dtoB": V(inputs["L0_attn_dt_out_b"]),
        "opWT": T(inputs["L0_attn_out_proj_w"]),
        "minWT": T(inputs["L0_attn_m_in_w"]),
        "mconvW": C(inputs["L0_attn_m_conv_w"]),
        "mconvB": V(inputs["L0_attn_m_conv_b"]),
        "mxprojWT": T(inputs["L0_attn_m_x_proj_w"]),
        "mdtWT": T(inputs["L0_attn_m_dt_w"]),
        "mdtB": V(inputs["L0_attn_m_dt_b"]),
        "mAlog": C(inputs["L0_attn_m_A_log"]),
        "mDvec": V(inputs["L0_attn_m_D"]),
        "moutWT": T(inputs["L0_attn_m_out_w"]),
        "gWT": T(inputs["L0_ffn_w_gate"]),
        "uWT": T(inputs["L0_ffn_w_up"]),
        "dWT": T(inputs["L0_ffn_w_down"]),
    }


def kernel(**inputs):
    ncores = 8
    x = np.asarray(inputs["L0_attn_hidden_states"], dtype=np.float32)
    common = _prep_inputs(inputs)
    nc = build_nc()
    in_maps = []
    for i in range(ncores):
        m = dict(common)
        m["x_fm"] = np.ascontiguousarray(x[2 * i:2 * i + 2].transpose(0, 2, 1))
        in_maps.append(m)
    res = run_bass_kernel_spmd(nc, in_maps, core_ids=list(range(ncores)))
    outs = [np.asarray(r["out_fm"]).transpose(0, 2, 1) for r in res.results]
    return np.ascontiguousarray(np.concatenate(outs, axis=0))


# version 15
# speedup vs baseline: 1.0164x; 1.0120x over previous
import numpy as np
from contextlib import ExitStack

import concourse.bass as bass
import concourse.mybir as mybir
from concourse.bass_utils import run_bass_kernel_spmd


F32 = mybir.dt.float32
BF16 = mybir.dt.bfloat16
I32 = mybir.dt.int32
AF = mybir.ActivationFunctionType
ALU = mybir.AluOpType
AX = mybir.AxisListType

B = 2
L = 2048
DM = 1024
DI = 2048
NST = 16
DTR = 64
KC = 4
MDI, MN, MDTR, MKC = 64, 4, 4, 2
CH = 256
NCH = L // CH
NT = DI // 128
EPS = 1e-5

ENGINES = ("sync", "tensor", "vector", "scalar", "gpsimd")


class Sem:
    __slots__ = ("h", "n", "owner", "name", "is_dma")

    def __init__(self, h, owner, name, is_dma=False):
        self.h, self.n, self.owner, self.name = h, 0, owner, name
        self.is_dma = is_dma


class Sched:

    def __init__(self, nc, stack):
        self.nc = nc
        self.ops = {e: [] for e in ENGINES}
        self.waited = {e: {} for e in ENGINES}
        self.stack = stack

    def sem(self, owner, name, is_dma=False):
        h = self.stack.enter_context(self.nc.semaphore(name))
        return Sem(h, owner, name, is_dma)

    def call(self, _eng, _opname, *args, wait=(), inc=None, amt=1,
             raw=False, **kwargs):
        wl = []
        for tok in wait:
            if tok is None:
                continue
            s, v = tok
            if s.owner == _eng and not s.is_dma:
                continue
            prev = self.waited[_eng].get(id(s), 0)
            if v > prev:
                self.waited[_eng][id(s)] = v
                wl.append((s.h, v))
        if inc is not None:
            inc.n += amt
        tok_out = (inc, inc.n) if inc is not None else None
        import sys as _sys
        _ln = _sys._getframe(1).f_lineno
        self.ops[_eng].append((wl, _opname, args, kwargs,
                               inc.h if inc is not None else None, amt, _ln,
                               raw))
        return tok_out

    def mark(self, eng, sem, amt=1):
        wl, opname, args, kwargs, inch, a, _ln, raw = self.ops[eng][-1]
        if inch is not None:
            assert inch is sem.h, "last op already incs a different sem"
            return (sem, sem.n)
        sem.n += amt
        self.ops[eng][-1] = (wl, opname, args, kwargs, sem.h, amt, _ln, raw)
        return (sem, sem.n)

    def finalize(self):
        with self.nc.Block() as block:
            for name in ENGINES:
                ops = self.ops[name]

                def body(engine, ops=ops, name=name):
                    compute = name in ("vector", "scalar", "gpsimd")

                    def small_out(a):
                        try:
                            sh = a[0].shape
                            n = 1
                            for s in sh[1:]:
                                n *= s
                            return n <= 32
                        except Exception:
                            return False

                    for wl, opname, args, kwargs, inch, amt, _ln, raw in ops:
                        for h, v in wl:
                            engine.wait_ge(h, v)
                        try:
                            ins = getattr(engine, opname)(*args, **kwargs)
                        except Exception as e:
                            raise RuntimeError(
                                f"emit failed: {name}.{opname} from "
                                f"kernel.py:{_ln}: {e}") from e
                        if compute and inch is None and not raw and \
                                opname not in ("dma_start", "nop", "memset") \
                                and (opname == "tensor_reduce"
                                     or small_out(args)):
                            engine.drain(fusable=True)
                        if inch is not None:
                            if compute and opname != "dma_start" and not raw:
                                engine.drain(fusable=True).then_inc(inch, amt)
                            else:
                                ins.then_inc(inch, amt)
                getattr(block, name)(body)


def build_nc(dbg=False, phases="ABCF"):
    nc = bass.Bass("TRN2", target_bir_lowering=False,
                   detect_race_conditions=False)
    stack = ExitStack()

    def din(name, shape):
        return nc.dram_tensor(name, list(shape), F32, kind="ExternalInput").ap()

    dx = din("x_fm", (B, DM, L))
    d_inWT = din("inWT", (DM, 2 * DI))
    d_convW = din("convW", (DI, KC))
    d_convB = din("convB", (DI, 1))
    d_xprojWT = din("xprojWT", (DI, DTR + 2 * NST))
    d_Alog = din("Alog", (DI, NST))
    d_D = din("Dvec", (DI, 1))
    d_dtowT = din("dtowT", (DTR, DI))
    d_dtoB = din("dtoB", (DI, 1))
    d_opWT = din("opWT", (DI, DM))
    d_minWT = din("minWT", (DTR, 2 * MDI))
    d_mconvW = din("mconvW", (MDI, MKC))
    d_mconvB = din("mconvB", (MDI, 1))
    d_mxprojWT = din("mxprojWT", (MDI, MDTR + 2 * MN))
    d_mdtWT = din("mdtWT", (MDTR, MDI))
    d_mdtB = din("mdtB", (MDI, 1))
    d_mAlog = din("mAlog", (MDI, MN))
    d_mD = din("mDvec", (MDI, 1))
    d_moutWT = din("moutWT", (MDI, DTR))
    d_gWT = din("gWT", (DM, 4096))
    d_uWT = din("uWT", (DM, 4096))
    d_dWT = din("dWT", (4096, DM))

    d_out = nc.dram_tensor("out_fm", [B, DM, L], F32, kind="ExternalOutput").ap()

    scr_kind = "ExternalOutput" if dbg else "Internal"
    scr_u = nc.dram_tensor("scr_u", [B, DI, L], F32, kind=scr_kind).ap()
    scr_sz = nc.dram_tensor("scr_sz", [B, DI, L], F32, kind=scr_kind).ap()
    scr_yg = nc.dram_tensor("scr_yg", [B, DI, L], F32, kind=scr_kind).ap()
    scr_xbc = nc.dram_tensor("scr_xbc", [32, B * L], F32, kind=scr_kind).ap()
    scr_dtm = nc.dram_tensor("scr_dtm", [DTR, B * L], F32,
                             kind=scr_kind).ap()
    scr_x2 = nc.dram_tensor("scr_x2", [B, DM, L], F32,
                            kind="ExternalInput" if phases == "F"
                            else scr_kind).ap()
    if dbg:
        d_xdbl = nc.dram_tensor("dbg_xdbl", [96, B * L], F32,
                                kind="ExternalOutput").ap()
        d_dtm = nc.dram_tensor("dbg_dtm", [DTR, B * L], F32,
                               kind="ExternalOutput").ap()
        d_xq = nc.dram_tensor("dbg_xq", [128, 8 * 512], F32,
                              kind="ExternalOutput").ap()
        d_hp = nc.dram_tensor("dbg_hp", [128, 32 * 512], F32,
                              kind="ExternalOutput").ap()
        d_fc = nc.dram_tensor("dbg_fc", [128, 8 * 512], F32,
                              kind="ExternalOutput").ap()

    sch = Sched(nc, stack)
    LD = sch.sem("sync", "ld", is_dma=True)
    ST = sch.sem("gpsimd", "st", is_dma=True)
    TS = sch.sem("tensor", "tsm")
    VS = sch.sem("vector", "vsm")
    SS = sch.sem("scalar", "ssm")
    GS = sch.sem("gpsimd", "gsm")

    def fence(eng, semq):
        sch.call(eng, "nop", wait=[(semq, semq.n)])

    alloc = nc.alloc_sbuf_tensor

    def salloc(scope, name, shape, dtype=F32):
        return scope.enter_context(nc.sbuf_tensor(name, shape, dtype))

    ones_col = alloc("ones_col", [128, 1], F32)
    ones_row = alloc("ones_row", [1, 128], F32)
    ident = alloc("ident", [128, 128], F32)
    tmp_init = alloc("tmp_init", [128, 256], F32)
    cells = alloc("cells", [1, 16], F32)
    mk_scope = ExitStack()
    a_neg = salloc(mk_scope, "a_neg", [128, NT * NST])
    convw_c = salloc(mk_scope, "convw_c", [128, NT * KC])
    convb_c = salloc(mk_scope, "convb_c", [128, NT])
    dvec_c = salloc(mk_scope, "dvec_c", [128, NT])
    dtob_c = salloc(mk_scope, "dtob_c", [128, NT])
    m_aneg = salloc(mk_scope, "m_aneg", [128, MN])
    m_convw = salloc(mk_scope, "m_convw", [128, MKC])
    m_convb = salloc(mk_scope, "m_convb", [128, 1])
    m_dvec = salloc(mk_scope, "m_dvec", [128, 1])
    m_dtb = salloc(mk_scope, "m_dtb", [128, 1])
    states = salloc(mk_scope, "states", [128, B * NT * NST])
    w_dtow = salloc(mk_scope, "w_dtow", [DTR, DI])
    w_minW = salloc(mk_scope, "w_minW", [DTR, 2 * MDI])
    w_mxproj = salloc(mk_scope, "w_mxproj", [MDI, MDTR + 2 * MN])
    w_mdt = salloc(mk_scope, "w_mdt", [MDTR, MDI])
    w_mout = salloc(mk_scope, "w_mout", [MDI, DTR])
    w_xproj = [salloc(mk_scope, f"w_xproj{k}", [128, 96]) for k in range(NT)]
    rows = salloc(mk_scope, "rows", [1, 2560])
    xd_scope = ExitStack()
    x_dbl = salloc(xd_scope, "x_dbl", [96, B * L])
    dtm_sb = salloc(xd_scope, "dtm_sb", [DTR, B * L])

    pb = [nc.alloc_psum_tensor(f"pb{i}", [128, 512], F32) for i in range(8)]
    bank_free = [None] * 8

    def load(dst_ap, src_ap, wait=()):
        return sch.call("sync", "dma_start", dst_ap, src_ap,
                        wait=wait, inc=LD, amt=16)

    def store(dst_ap, src_ap, wait=()):
        return sch.call("gpsimd", "dma_start", dst_ap, src_ap,
                        wait=wait, inc=ST, amt=16)

    for t in range(NT):
        load(tmp_init.ap()[:, t * NST:(t + 1) * NST],
             d_Alog[t * 128:(t + 1) * 128, :])
    tok_alog = (LD, LD.n)
    fence("sync", LD)
    for t in range(NT):
        load(convw_c.ap()[:, t * KC:(t + 1) * KC],
             d_convW[t * 128:(t + 1) * 128, :])
        load(convb_c.ap()[:, t:t + 1], d_convB[t * 128:(t + 1) * 128, :])
        load(dvec_c.ap()[:, t:t + 1], d_D[t * 128:(t + 1) * 128, :])
        load(dtob_c.ap()[:, t:t + 1], d_dtoB[t * 128:(t + 1) * 128, :])
    load(m_aneg.ap()[0:MDI, :], d_mAlog[:, :])
    load(m_convw.ap()[0:MDI, :], d_mconvW[:, :])
    load(m_convb.ap()[0:MDI, :], d_mconvB[:, :])
    load(m_dvec.ap()[0:MDI, :], d_mD[:, :])
    load(m_dtb.ap()[0:MDI, :], d_mdtB[:, :])
    load(w_dtow.ap(), d_dtowT)
    load(w_minW.ap(), d_minWT)
    load(w_mxproj.ap(), d_mxprojWT)
    load(w_mdt.ap(), d_mdtWT)
    load(w_mout.ap(), d_moutWT)
    for k in range(NT):
        load(w_xproj[k].ap(), d_xprojWT[k * 128:(k + 1) * 128, :])
    tok_small = (LD, LD.n)
    fence("sync", LD)

    t1 = sch.call("scalar", "activation", a_neg.ap(), tmp_init.ap(), AF.Exp,
                  wait=[tok_alog], inc=SS)
    t2 = sch.call("scalar", "activation", m_aneg.ap()[0:MDI, :],
                  m_aneg.ap()[0:MDI, :], AF.Exp, wait=[tok_small], inc=SS)
    sch.call("vector", "tensor_scalar", a_neg.ap(), a_neg.ap(), -1.0, None,
             wait=[t1], op0=ALU.mult)
    sch.call("vector", "tensor_scalar", m_aneg.ap()[0:MDI, :],
             m_aneg.ap()[0:MDI, :], -1.0, None, wait=[t2], op0=ALU.mult)
    sch.call("vector", "memset", ones_col.ap(), 1.0)
    sch.call("vector", "memset", ones_row.ap(), 1.0)
    tok_ones = sch.call("vector", "memset", states.ap(), 0.0, inc=VS)
    gi = sch.call("gpsimd", "iota", ident.ap(), [[0, 128]],
                  channel_multiplier=1,
                  allow_small_or_imprecise_dtypes=True, wait=[t1])
    gi = sch.call("gpsimd", "iota", tmp_init.ap()[:, 0:128], [[1, 128]],
                  channel_multiplier=0,
                  allow_small_or_imprecise_dtypes=True, inc=GS)
    tok_ident = sch.call("vector", "tensor_tensor", ident.ap(), ident.ap(),
                         tmp_init.ap()[:, 0:128], wait=[gi],
                         op=ALU.is_equal, inc=VS)

    run_abc = "A" in phases
    a_scope = ExitStack()
    CHA = 512
    NCHA = L // CHA
    wa_in = salloc(a_scope, "wa_in", [128, 8 * 2048])
    xa = salloc(a_scope, "xa", [128, 2 * 8 * CHA])
    acta = salloc(a_scope, "acta", [128, NT * CHA])
    hist = salloc(a_scope, "hist", [128, B * NT * 4])
    sqa = salloc(a_scope, "sqa", [128, CHA])
    csum = salloc(a_scope, "csum", [1, 2048])

    mean_r = rows.ap()[:, 0:CHA]
    var_r = rows.ap()[:, CHA:2 * CHA]
    m2_r = rows.ap()[:, 2 * CHA:3 * CHA]
    rstd_r = rows.ap()[:, 3 * CHA:4 * CHA]
    brow_r = rows.ap()[:, 4 * CHA:5 * CHA]

    itersA = [(b, c) for c in range(NCHA) for b in range(B)]
    iters = [(b, c) for c in range(NCH) for b in range(B)]
    xld = {}
    prev = {"stores": None, "tdone": None, "rows_S": None, "stats_rd": None}

    sch.call("vector", "memset", hist.ap(), 0.0, inc=VS)

    for half in (range(2) if run_abc else []):
        fence("sync", LD)
        for k in range(8):
            load(wa_in.ap()[:, k * 2048:(k + 1) * 2048],
                 d_inWT[k * 128:(k + 1) * 128, half * 2048:(half + 1) * 2048],
                 wait=[prev["tdone"]])
        tok_win = (LD, LD.n)

        tok_csum = None
        for cc in range(4):
            for k in range(8):
                sch.call("tensor", "matmul", pb[0].ap()[0:1, 0:512],
                         ones_col.ap(),
                         wa_in.ap()[:, k * 2048 + cc * 512:
                                    k * 2048 + (cc + 1) * 512],
                         wait=[tok_win, bank_free[0], tok_ones],
                         start=(k == 0), stop=(k == 7))
            tcs = sch.mark("tensor", TS)
            bank_free[0] = sch.call("scalar", "copy",
                                    csum.ap()[:, cc * 512:(cc + 1) * 512],
                                    pb[0].ap()[0:1, 0:512], wait=[tcs], inc=SS)
        tok_csum = bank_free[0]
        prev["stats_rd"] = tok_csum

        def load_x(it):
            b, c = itersA[it]
            buf = it % 2
            fence("sync", LD)
            load(xa.ap()[:, buf * 8 * CHA:(buf + 1) * 8 * CHA],
                 dx[b, :, c * CHA:(c + 1) * CHA]
                 .rearrange("(k p) j -> p k j", p=128),
                 wait=[prev["tdone"]])
            xld[it] = (LD, LD.n)

        load_x(0)

        for it, (b, c) in enumerate(itersA):
            buf = it % 2
            xt = [xa.ap()[:, (buf * 8 + k) * CHA:(buf * 8 + k + 1) * CHA]
                  for k in range(8)]
            if it + 1 < len(itersA):
                load_x(it + 1)

            sq_prev = None
            tok_pb0 = None
            for k in range(8):
                stok = sch.call("scalar", "activation", sqa.ap(), xt[k],
                                AF.Square,
                                wait=[xld[it], prev["stats_rd"], sq_prev],
                                inc=SS)
                sch.call("tensor", "matmul", pb[0].ap()[0:1, 0:CHA],
                         ones_col.ap(), xt[k],
                         wait=[xld[it], prev["stats_rd"], tok_ones],
                         start=(k == 0), stop=(k == 7))
                if k == 7:
                    tok_pb0 = sch.mark("tensor", TS)
                sq_prev = sch.call("tensor", "matmul", pb[1].ap()[0:1, 0:CHA],
                                   ones_col.ap(), sqa.ap(), wait=[stok],
                                   start=(k == 0), stop=(k == 7), inc=TS)
            tstat = sq_prev

            sch.call("vector", "tensor_scalar", mean_r, pb[0].ap()[0:1, 0:CHA],
                     1.0 / DM, None,
                     wait=[tstat, tok_pb0, prev["tdone"], prev["rows_S"]],
                     op0=ALU.mult)
            sch.call("vector", "tensor_scalar", var_r, pb[1].ap()[0:1, 0:CHA],
                     1.0 / DM, None, op0=ALU.mult)
            sch.call("vector", "tensor_tensor", m2_r, mean_r, mean_r,
                     op=ALU.mult)
            sch.call("vector", "tensor_tensor", var_r, var_r, m2_r,
                     op=ALU.subtract)
            stats_rd = sch.call("vector", "tensor_scalar", var_r, var_r, EPS,
                                None, op0=ALU.add, inc=VS)
            prev["stats_rd"] = stats_rd
            sl = sch.call("scalar", "activation", rstd_r, var_r, AF.Ln,
                          wait=[stats_rd])
            sl = sch.call("scalar", "activation", rstd_r, rstd_r, AF.Exp,
                          scale=-0.5, inc=SS)
            sch.call("vector", "tensor_tensor", brow_r, mean_r, rstd_r,
                     wait=[sl], op=ALU.mult)
            rows_done = sch.call("vector", "tensor_scalar", brow_r, brow_r,
                                 -1.0, None, op0=ALU.mult, inc=VS)
            prev["rows_S"] = sl
            tbr = sch.call("tensor", "matmul", pb[2].ap()[:, 0:CHA],
                           ones_row.ap(), rstd_r,
                           wait=[rows_done, tok_ident, bank_free[2]],
                           start=True, stop=True, inc=TS)
            xh_tok = None
            for k in range(8):
                xh_tok = sch.call("vector", "tensor_tensor", xt[k], xt[k],
                                  pb[2].ap()[:, 0:CHA], wait=[tbr],
                                  op=ALU.mult, inc=VS)
            bank_free[2] = xh_tok

            for t in range(NT):
                bi = 3 + (t % 2)
                for k in range(8):
                    sch.call("tensor", "matmul", pb[bi].ap()[:, 0:CHA],
                             wa_in.ap()[:, k * 2048 + t * 128:
                                        k * 2048 + (t + 1) * 128],
                             xt[k],
                             wait=[xh_tok, tok_win, bank_free[bi]],
                             start=(k == 0), stop=False)
                tmm = sch.call("tensor", "matmul", pb[bi].ap()[:, 0:CHA],
                               csum.ap()[:, t * 128:(t + 1) * 128], brow_r,
                               wait=[tok_csum, rows_done],
                               start=False, stop=True, inc=TS)
                aap = acta.ap()[:, t * CHA:(t + 1) * CHA]
                if half == 0:
                    hap = hist.ap()[:, (b * NT + t) * 4:(b * NT + t) * 4 + 4]
                    wcol = [convw_c.ap()[:, t * KC + j:t * KC + j + 1]
                            for j in range(KC)]
                    sch.call("vector", "tensor_scalar", aap,
                             pb[bi].ap()[:, 0:CHA], wcol[3], None,
                             wait=[tmm, prev["stores"], prev["tdone"]],
                             op0=ALU.mult)
                    for j in range(3):
                        sh = 3 - j
                        sch.call("vector", "scalar_tensor_tensor",
                                 aap[:, sh:CHA], pb[bi].ap()[:, 0:CHA - sh],
                                 wcol[j], aap[:, sh:CHA],
                                 op0=ALU.mult, op1=ALU.add)
                        sch.call("vector", "scalar_tensor_tensor",
                                 aap[:, 0:sh], hap[:, 1 + j:1 + j + sh],
                                 wcol[j], aap[:, 0:sh],
                                 op0=ALU.mult, op1=ALU.add)
                    vconv = sch.call("vector", "tensor_copy", hap[:, 1:4],
                                     pb[bi].ap()[:, CHA - 3:CHA], inc=VS)
                    bank_free[bi] = vconv
                    su = sch.call("scalar", "activation", aap, aap, AF.Silu,
                                  bias=convb_c.ap()[:, t:t + 1],
                                  wait=[vconv], inc=SS)
                    sch.call("tensor", "matmul", pb[5].ap()[0:96, 0:CHA],
                             w_xproj[t].ap(), aap,
                             wait=[su, tok_small, bank_free[5]],
                             start=(t == 0), stop=(t == NT - 1))
                    a_st_tok = su
                else:
                    stok = sch.call("scalar", "activation", aap,
                                    pb[bi].ap()[:, 0:CHA], AF.Silu,
                                    wait=[tmm, prev["stores"],
                                          prev["tdone"]], inc=SS)
                    bank_free[bi] = stok
                    a_st_tok = stok
            dstA = scr_u if half == 0 else scr_sz
            store(dstA[b, :, c * CHA:(c + 1) * CHA]
                  .rearrange("(k p) j -> p k j", p=128),
                  acta.ap(), wait=[a_st_tok])
            txp = sch.mark("tensor", TS)
            if half == 0:
                bank_free[5] = sch.call(
                    "scalar", "copy",
                    x_dbl.ap()[:, b * L + c * CHA:b * L + (c + 1) * CHA],
                    pb[5].ap()[0:96, 0:CHA], wait=[txp], inc=SS)
            prev["stores"] = (ST, ST.n)
            prev["tdone"] = txp
            fence("gpsimd", ST)

    a_scope.close()
    A_done_S = (SS, SS.n)
    A_done_V = (VS, VS.n)
    A_done_T = (TS, TS.n)
    A_done_ST = (ST, ST.n)

    bi_scope = ExitStack()
    xzm = salloc(bi_scope, "xzm", [128, L])
    xum = salloc(bi_scope, "xum", [64, L])
    szm = salloc(bi_scope, "szm", [64, L])
    xdm = salloc(bi_scope, "xdm", [12, L])
    xbcm = salloc(bi_scope, "xbcm", [1, L])
    dmm = salloc(bi_scope, "dmm", [64, L])
    dtms = salloc(bi_scope, "dtms", [64, L])
    tma = salloc(bi_scope, "tma", [64, L])
    tmb = salloc(bi_scope, "tmb", [64, L])
    w_m = salloc(bi_scope, "w_m", [64, L])
    bcm = salloc(bi_scope, "bcm", [64, L])
    sm = salloc(bi_scope, "sm", [64, L])
    ym = salloc(bi_scope, "ym", [64, L])

    def mm512(dst_sb, lhsT, rhs_base_ap, n_cc, psum_rows, wait=(), kname=None):
        toks = []
        for cc in range(n_cc):
            tmmi = sch.call("tensor", "matmul",
                            pb[6].ap()[0:psum_rows, 0:512], lhsT,
                            rhs_base_ap[:, cc * 512:(cc + 1) * 512],
                            wait=list(wait) + [bank_free[6]],
                            start=True, stop=True, inc=TS)
            bank_free[6] = sch.call("scalar", "copy",
                                    dst_sb[:, cc * 512:(cc + 1) * 512],
                                    pb[6].ap()[0:psum_rows, 0:512],
                                    wait=[tmmi], inc=SS)
        return bank_free[6]

    tok_dtm = None
    for b in (range(B) if run_abc else []):
        xdb = x_dbl.ap()[0:64, b * L:(b + 1) * L]
        tk = mm512(xzm.ap(), w_minW.ap(), xdb, 4, 128,
                   wait=[A_done_S, tok_small])
        x_m = xzm.ap()[0:64, :]
        z_m = xzm.ap()[64:128, :]
        w0 = m_convw.ap()[0:64, 0:1]
        w1 = m_convw.ap()[0:64, 1:2]
        sch.call("vector", "tensor_scalar", xum.ap(), x_m, w1, None,
                 wait=[tk], op0=ALU.mult)
        vtk = sch.call("vector", "scalar_tensor_tensor", xum.ap()[:, 1:L],
                       x_m[:, 0:L - 1], w0, xum.ap()[:, 1:L],
                       op0=ALU.mult, op1=ALU.add, inc=VS)
        stk = sch.call("scalar", "activation", xum.ap(), xum.ap(), AF.Silu,
                       bias=m_convb.ap()[0:64, :], wait=[vtk], inc=SS)
        szk = sch.call("scalar", "activation", szm.ap(), z_m, AF.Silu,
                       wait=[tk], inc=SS)
        tk = mm512(xdm.ap(), w_mxproj.ap(), xum.ap(), 4, 12, wait=[stk])
        tok_xdm = tk
        tk = mm512(dmm.ap(), w_mdt.ap(), xdm.ap()[0:4, :], 4, 64, wait=[tk])
        sch.call("scalar", "activation", tma.ap(), dmm.ap(), AF.Abs,
                 bias=m_dtb.ap()[0:64, :], wait=[tk])
        sch.call("scalar", "activation", tma.ap(), tma.ap(), AF.Exp,
                 scale=-1.0)
        sch.call("scalar", "add", tma.ap(), tma.ap(), 1.0)
        sch.call("scalar", "activation", tma.ap(), tma.ap(), AF.Ln)
        ssp = sch.call("scalar", "activation", tmb.ap(), dmm.ap(), AF.Relu,
                       bias=m_dtb.ap()[0:64, :], inc=SS)
        vdt = sch.call("vector", "tensor_add", dtms.ap(), tma.ap(), tmb.ap(),
                       wait=[ssp])
        vw = sch.call("vector", "tensor_mul", w_m.ap(), dtms.ap(), xum.ap(),
                      inc=VS)
        vym = sch.call("vector", "tensor_scalar", ym.ap(), xum.ap(),
                       m_dvec.ap()[0:64, :], None, op0=ALU.mult, inc=VS)
        vsc = None
        vym_prev = vym
        tkb_prev = None
        for n in range(MN):
            sda = sch.call("scalar", "activation", tma.ap(), dtms.ap(),
                           AF.Exp, scale=m_aneg.ap()[0:64, n:n + 1],
                           wait=[vw, vsc], inc=SS)
            fence("sync", LD)
            gb = load(xbcm.ap(), xdm.ap()[4 + n:5 + n, :],
                      wait=[tok_xdm, tkb_prev])
            tkb = mm512(bcm.ap(), ones_row.ap()[0:1, 0:64],
                        xbcm.ap(), 4, 64, wait=[vw, vym_prev, gb])
            vdb = sch.call("vector", "tensor_mul", tmb.ap(), w_m.ap(),
                           bcm.ap(), wait=[tkb])
            vsc = sch.call("vector", "tensor_tensor_scan", sm.ap(), tma.ap(),
                           tmb.ap(), 0.0, wait=[sda],
                           op0=ALU.mult, op1=ALU.add, inc=VS)
            fence("sync", LD)
            gc = load(xbcm.ap(), xdm.ap()[8 + n:9 + n, :],
                      wait=[tok_xdm, tkb])
            tkb_prev = tkc = mm512(bcm.ap(), ones_row.ap()[0:1, 0:64],
                                   xbcm.ap(), 4, 64, wait=[vsc, gc])
            sch.call("vector", "tensor_mul", sm.ap(), sm.ap(), bcm.ap(),
                     wait=[tkc])
            vym_prev = vym = sch.call("vector", "tensor_add", ym.ap(),
                                      ym.ap(), sm.ap(), inc=VS)
        vyg = sch.call("vector", "tensor_mul", w_m.ap(), ym.ap(), szm.ap(),
                       wait=[szk], inc=VS)
        tok_dtm = mm512(dtm_sb.ap()[:, b * L:(b + 1) * L], w_mout.ap(),
                        w_m.ap(), 4, 64, wait=[vyg])
        store(scr_dtm[:, b * L:(b + 1) * L],
              dtm_sb.ap()[:, b * L:(b + 1) * L], wait=[tok_dtm])
    if run_abc:
        store(scr_xbc, x_dbl.ap()[64:96, :], wait=[A_done_S])
    if dbg and run_abc:
        fence("gpsimd", ST)
        store(d_xdbl, x_dbl.ap(), wait=[A_done_S])
        store(d_dtm, dtm_sb.ap(), wait=[tok_dtm])
    bi_scope.close()
    xd_scope.close()
    fence("gpsimd", ST)
    spill_tok = (ST, ST.n)

    bm_scope = ExitStack()
    ub = salloc(bm_scope, "ub", [128, NT * CH])
    bc_sb = salloc(bm_scope, "bc_sb", [128, 2, 2 * NST * CH])
    s3 = salloc(bm_scope, "s3", [128, NST, CH])
    da3 = salloc(bm_scope, "da3", [128, 2, NST, CH])
    dbu_b = salloc(bm_scope, "dbu_b", [128, CH])
    dt4 = salloc(bm_scope, "dt4", [128, 2, CH])
    w4 = salloc(bm_scope, "w4", [128, 4, CH])
    t14 = salloc(bm_scope, "t14", [128, 2, CH])
    t24 = salloc(bm_scope, "t24", [128, 2, CH])
    yg_sb = salloc(bm_scope, "yg_sb", [128, 2 * CH])
    xbc = salloc(bm_scope, "xbc", [1, 2 * NST * CH])
    dtm_c = salloc(bm_scope, "dtm_c", [DTR, 2, CH])

    ut_tok = {}
    ztoks = {}
    da_t = {}
    pd_free = [A_done_V, A_done_V]
    py_free = [A_done_V, A_done_V]
    yg_st = [None, None]
    last_z = (VS, VS.n)
    BI = len(iters)

    def u_slice(t):
        return ub.ap()[:, t * CH:(t + 1) * CH]

    def emit_u_group(it, t0, wait_tok):
        b, c = iters[it]
        fence("sync", LD)
        load(ub.ap()[:, t0 * CH:(t0 + 4) * CH],
             scr_u[b, t0 * 128:(t0 + 4) * 128, c * CH:(c + 1) * CH]
             .rearrange("(k p) j -> p k j", p=128),
             wait=[spill_tok, wait_tok])
        tok = (LD, LD.n)
        for t in range(t0, t0 + 4):
            ut_tok[(it, t)] = tok

    for g4 in range(4):
        emit_u_group(0, 4 * g4, None)
    gld = {}
    tok_bc = {}
    dtm_tok = {}
    iter_pd = {}

    def emit_ldbc(it):
        bn, cn = iters[it]
        war = tok_bc.get(it - 1, spill_tok)
        off = bn * L + cn * CH
        fence("sync", LD)
        for n in range(2 * NST):
            load(xbc.ap()[:, n * CH:(n + 1) * CH],
                 scr_xbc[n:n + 1, off:off + CH], wait=[war])
        gld[it] = (LD, LD.n)

    def emit_lddtm(it):
        bn, cn = iters[it]
        fence("sync", LD)
        load(dtm_c.ap()[:, it % 2, :],
             scr_dtm[:, bn * L + cn * CH:bn * L + (cn + 1) * CH],
             wait=[iter_pd.get(it - 2, spill_tok)])
        dtm_tok[it] = (LD, LD.n)

    def emit_bc(it):
        for n in range(2 * NST):
            bi = 4 + (n % 2)
            tmmi = sch.call("tensor", "matmul", pb[bi].ap()[:, 0:CH],
                            ones_row.ap(),
                            xbc.ap()[:, n * CH:(n + 1) * CH],
                            wait=[gld[it], bank_free[bi], tok_ident],
                            start=True, stop=True, inc=TS)
            bank_free[bi] = sch.call("scalar", "copy",
                                     bc_sb.ap()[:, it % 2,
                                                n * CH:(n + 1) * CH],
                                     pb[bi].ap()[:, 0:CH],
                                     wait=[tmmi, last_z, spill_tok],
                                     inc=SS)
        tok_bc[it] = bank_free[5]

    if run_abc:
        emit_ldbc(0)
        emit_lddtm(0)
        emit_bc(0)
        if BI > 1:
            emit_ldbc(1)
            emit_lddtm(1)
    pend = []

    def flush_pend(tok, nxt_it):
        while pend:
            pit, ps, ygap = pend.pop(0)
            pb_, pc_ = iters[pit]
            yg_st[ps % 2] = store(
                scr_yg[pb_, ps * 128:(ps + 1) * 128,
                       pc_ * CH:(pc_ + 1) * CH], ygap, wait=[tok])
            if nxt_it is not None and ps % 4 == 3:
                emit_u_group(nxt_it, ps - 3, tok)

    def emit_scans(it, s):
        b, c = iters[it]
        par = s % 2
        g = it * NT + s
        dab = da3.ap()[:, par, :, :]
        sti = states.ap()[:, (b * NT + s) * NST:(b * NT + s + 1) * NST]
        for n in range(NST):
            sch.call("vector", "tensor_mul", dbu_b.ap(),
                     w4.ap()[:, s % 4, :],
                     bc_sb.ap()[:, it % 2, n * CH:(n + 1) * CH],
                     wait=[da_t[g], tok_bc[it]])
            sch.call("vector", "tensor_tensor_scan", s3.ap()[:, n, :],
                     dab[:, n, :], dbu_b.ap(), sti[:, n:n + 1],
                     op0=ALU.mult, op1=ALU.add)
        sch.call("vector", "tensor_copy", sti, s3.ap()[:, :, CH - 1],
                 raw=True)
        for n in range(NST):
            sch.call("vector", "tensor_mul", s3.ap()[:, n, :],
                     s3.ap()[:, n, :],
                     bc_sb.ap()[:, it % 2,
                                (NST + n) * CH:(NST + n + 1) * CH])
        ztoks[g] = tok_z = sch.mark("vector", VS)
        nxt = it + 1 if it + 1 < BI else None
        flush_pend(tok_z, nxt)
        py = pb[2 + par]
        for n in range(NST):
            sch.call("tensor", "matmul", py.ap()[:, 0:CH], ident.ap(),
                     s3.ap()[:, n, :], wait=[tok_z, py_free[par]],
                     start=(n == 0), stop=(n == NST - 1))
        tok_y = sch.mark("tensor", TS)
        ygap = yg_sb.ap()[:, par * CH:(par + 1) * CH]
        vev = sch.call("vector", "scalar_tensor_tensor", ygap, u_slice(s),
                       dvec_c.ap()[:, s:s + 1], py.ap()[:, 0:CH],
                       wait=[tok_y, yg_st[par]],
                       op0=ALU.mult, op1=ALU.add, inc=VS, raw=True)
        py_free[par] = vev
        pend.append((it, s, ygap))

    for it, (b, c) in (enumerate(iters) if run_abc else []):
        sp_t = {}

        def emit_dsp(t):
            par = t % 2
            g = it * NT + t
            pd = pb[par]
            tok_pd = sch.call("tensor", "matmul", pd.ap()[:, 0:CH],
                              w_dtow.ap()[:, t * 128:(t + 1) * 128],
                              dtm_c.ap()[:, it % 2, :],
                              wait=[dtm_tok[it], pd_free[par]],
                              start=True, stop=True, inc=TS)
            iter_pd[it] = tok_pd
            bcol = dtob_c.ap()[:, t:t + 1]
            t1 = t14.ap()[:, par, :]
            t2 = t24.ap()[:, par, :]
            sch.call("scalar", "activation", t1, pd.ap()[:, 0:CH],
                     AF.Abs, bias=bcol, wait=[tok_pd, ztoks.get(g - 4)])
            sch.call("scalar", "activation", t1, t1, AF.Exp, scale=-1.0)
            sch.call("scalar", "add", t1, t1, 1.0)
            sch.call("scalar", "activation", t1, t1, AF.Ln)
            sp_t[t] = sch.call("scalar", "activation", t2, pd.ap()[:, 0:CH],
                               AF.Relu, bias=bcol, inc=SS)
            pd_free[par] = sp_t[t]

        emit_dsp(0)
        for t in range(NT):
            par = t % 2
            p4 = t % 4
            g = it * NT + t
            if t + 1 < NT:
                emit_dsp(t + 1)
            if t == NT // 2:
                if it + 1 < BI:
                    emit_bc(it + 1)
                if it + 2 < BI:
                    emit_ldbc(it + 2)
            sch.call("vector", "tensor_add", dt4.ap()[:, par, :],
                     t14.ap()[:, par, :], t24.ap()[:, par, :],
                     wait=[sp_t[t], ut_tok[(it, t)], da_t.get(g - 2)])
            wtok = sch.call("vector", "tensor_mul", w4.ap()[:, p4, :],
                            dt4.ap()[:, par, :], u_slice(t),
                            inc=VS if t < 2 else None)
            if t >= 2:
                emit_scans(it, t - 2)
            dab = da3.ap()[:, par, :, :]
            for n in range(NST):
                sch.call("scalar", "activation", dab[:, n, :],
                         dt4.ap()[:, par, :], AF.Exp,
                         scale=a_neg.ap()[:, t * NST + n:t * NST + n + 1],
                         wait=[ztoks.get(g - 2), wtok])
            da_t[g] = sch.mark("scalar", SS)
        emit_scans(it, NT - 2)
        emit_scans(it, NT - 1)
        if it + 2 < BI:
            emit_lddtm(it + 2)
        ztoks[it * NT + NT] = fin = sch.call("vector", "engine_nop",
                                             inc=VS)
        flush_pend(fin, it + 1 if it + 1 < BI else None)
        last_z = fin
        fence("gpsimd", ST)

    bm_scope.close()
    B_done_V = (VS, VS.n)
    B_done_S = (SS, SS.n)
    B_done_T = (TS, TS.n)
    B_done_ST = (ST, ST.n)

    c_scope = ExitStack()
    mean_r = rows.ap()[:, 0:CH]
    var_r = rows.ap()[:, CH:2 * CH]
    m2_r = rows.ap()[:, 2 * CH:3 * CH]
    rstd_r = rows.ap()[:, 3 * CH:4 * CH]
    brow_r = rows.ap()[:, 4 * CH:5 * CH]
    wop = salloc(c_scope, "wop", [128, NT * DM])
    ygc = salloc(c_scope, "ygc", [128, NT * CH])
    szc = salloc(c_scope, "szc", [128, NT * CH])
    xc = salloc(c_scope, "xc", [128, 8 * CH])
    oc = salloc(c_scope, "oc", [128, 8 * CH])
    sqc = salloc(c_scope, "sqc", [128, CH])

    fence("sync", LD)
    for k in range(NT):
        load(wop.ap()[:, k * DM:(k + 1) * DM],
             d_opWT[k * 128:(k + 1) * 128, :], wait=[B_done_T])
    tok_wop = (LD, LD.n)

    prevC = {"vdone": B_done_V, "tdone": B_done_T, "sdone": B_done_S,
             "stores": B_done_ST, "rows_S": None, "stats_rd": None,
             "rowsT": None}

    for it, (b, c) in (enumerate(iters) if run_abc else []):
        fence("sync", LD)
        load(ygc.ap(),
             scr_yg[b, :, c * CH:(c + 1) * CH]
             .rearrange("(k p) j -> p k j", p=128),
             wait=[B_done_ST, prevC["vdone"], prevC["tdone"]])
        load(szc.ap(),
             scr_sz[b, :, c * CH:(c + 1) * CH]
             .rearrange("(k p) j -> p k j", p=128))
        load(xc.ap(),
             dx[b, :, c * CH:(c + 1) * CH]
             .rearrange("(k p) j -> p k j", p=128))
        cld = (LD, LD.n)

        for t in range(NT):
            sch.call("vector", "tensor_mul", ygc.ap()[:, t * CH:(t + 1) * CH],
                     ygc.ap()[:, t * CH:(t + 1) * CH],
                     szc.ap()[:, t * CH:(t + 1) * CH],
                     wait=[cld, prevC["tdone"]])
        tok_yf = sch.mark("vector", VS)

        sq_prev = None
        tok_pb2 = None
        for o in range(8):
            bi = o % 2
            for k in range(NT):
                sch.call("tensor", "matmul", pb[bi].ap()[:, 0:CH],
                         wop.ap()[:, k * DM + o * 128:k * DM + (o + 1) * 128],
                         ygc.ap()[:, k * CH:(k + 1) * CH],
                         wait=[tok_yf, tok_wop, bank_free[bi]],
                         start=(k == 0), stop=(k == NT - 1))
            tmm = sch.mark("tensor", TS)
            ocap = oc.ap()[:, o * CH:(o + 1) * CH]
            bank_free[bi] = sch.call("scalar", "copy", ocap,
                                     pb[bi].ap()[:, 0:CH],
                                     wait=[tmm, prevC["vdone"]], inc=SS)
            stok = sch.call("scalar", "activation", sqc.ap(), ocap, AF.Square,
                            wait=[sq_prev], inc=SS)
            sch.call("tensor", "matmul", pb[2].ap()[0:1, 0:CH], ones_col.ap(),
                     ocap, wait=[bank_free[bi], prevC["stats_rd"]],
                     start=(o == 0), stop=(o == 7))
            if o == 7:
                tok_pb2 = sch.mark("tensor", TS)
            sq_prev = sch.call("tensor", "matmul", pb[3].ap()[0:1, 0:CH],
                               ones_col.ap(), sqc.ap(),
                               wait=[stok, prevC["stats_rd"]],
                               start=(o == 0), stop=(o == 7), inc=TS)
        sch.call("vector", "tensor_scalar", mean_r, pb[2].ap()[0:1, 0:CH],
                 1.0 / DM, None, wait=[tok_pb2, sq_prev, prevC["rowsT"],
                                       prevC["rows_S"]], op0=ALU.mult)
        sch.call("vector", "tensor_scalar", var_r, pb[3].ap()[0:1, 0:CH],
                 1.0 / DM, None, op0=ALU.mult)
        sch.call("vector", "tensor_tensor", m2_r, mean_r, mean_r, op=ALU.mult)
        sch.call("vector", "tensor_tensor", var_r, var_r, m2_r,
                 op=ALU.subtract)
        stats_rd = sch.call("vector", "tensor_scalar", var_r, var_r, EPS,
                            None, op0=ALU.add, inc=VS)
        prevC["stats_rd"] = stats_rd
        sl = sch.call("scalar", "activation", rstd_r, var_r, AF.Ln,
                      wait=[stats_rd])
        sl = sch.call("scalar", "activation", rstd_r, rstd_r, AF.Exp,
                      scale=-0.5, inc=SS)
        sch.call("vector", "tensor_tensor", brow_r, mean_r, rstd_r,
                 wait=[sl], op=ALU.mult)
        rows_done = sch.call("vector", "tensor_scalar", brow_r, brow_r, -1.0,
                             None, op0=ALU.mult, inc=VS)
        prevC["rows_S"] = sl
        tb1 = sch.call("tensor", "matmul", pb[4].ap()[:, 0:CH],
                       ones_row.ap(), rstd_r,
                       wait=[rows_done, bank_free[4]],
                       start=True, stop=True)
        tb2 = sch.call("tensor", "matmul", pb[5].ap()[:, 0:CH],
                       ones_row.ap(), brow_r,
                       wait=[rows_done, bank_free[5]],
                       start=True, stop=True, inc=TS)
        prevC["rowsT"] = tb2
        for o in range(8):
            ocap = oc.ap()[:, o * CH:(o + 1) * CH]
            sch.call("vector", "tensor_tensor", ocap, ocap,
                     pb[4].ap()[:, 0:CH], wait=[tb2, prevC["stores"]],
                     op=ALU.mult)
            sch.call("vector", "tensor_tensor", ocap, ocap,
                     pb[5].ap()[:, 0:CH], op=ALU.add)
            vx2 = sch.call("vector", "tensor_tensor", ocap, ocap,
                           xc.ap()[:, o * CH:(o + 1) * CH], op=ALU.add,
                           inc=VS)
        store(scr_x2[b, :, c * CH:(c + 1) * CH]
              .rearrange("(o p) j -> p o j", p=128),
              oc.ap(), wait=[vx2])
        bank_free[4] = bank_free[5] = (VS, VS.n)
        prevC["vdone"] = (VS, VS.n)
        prevC["tdone"] = (TS, TS.n)
        prevC["stores"] = (ST, ST.n)
        fence("gpsimd", ST)

    c_scope.close()
    mk_scope.close()
    C_done_V = (VS, VS.n)
    C_done_S = (SS, SS.n)
    C_done_T = (TS, TS.n)
    C_done_ST = (ST, ST.n)

    f0_scope = ExitStack()
    wt0 = salloc(f0_scope, "wt0", [128, 2 * 512])
    absb = salloc(f0_scope, "absb", [128, 1024])
    ones128 = salloc(f0_scope, "ones128", [128, 128])
    colacc = salloc(f0_scope, "colacc", [128, 8])
    tcol = salloc(f0_scope, "tcol", [128, 1])
    mcol = salloc(f0_scope, "mcol", [128, 1])
    q32a = salloc(f0_scope, "q32a", [128, 1024])
    q32b = salloc(f0_scope, "q32b", [128, 1024])
    wqb = salloc(f0_scope, "wqb", [128, 1024], BF16)
    wq_c = salloc(f0_scope, "wq_c", [128, 2 * 3])

    mats = [(d_gWT, 8, 8, DM * 4096), (d_uWT, 8, 8, DM * 4096),
            (d_dWT, 32, 2, DM * 4096)]
    wq_kind = "ExternalOutput" if dbg else "Internal"
    scr_wq = [nc.dram_tensor("scr_wgq", [DM, 4096], BF16, kind=wq_kind).ap(),
              nc.dram_tensor("scr_wuq", [DM, 4096], BF16, kind=wq_kind).ap(),
              nc.dram_tensor("scr_wdq", [4096, DM], BF16, kind=wq_kind).ap()]

    prevF0 = {"s": C_done_S, "v": C_done_V, "t": C_done_T, "v2": None}
    sch.call("vector", "memset", ones128.ap(), 1.0, wait=[C_done_T], inc=VS)
    for mi, (dsrc, nk, ncc, nel) in enumerate(mats):
        vz = sch.call("vector", "memset", colacc.ap(), 0.0,
                      wait=[prevF0["t"]], inc=VS)
        sab_prev = None
        vac = None
        for k in range(nk):
            for cc in range(ncc // 2):
                fence("sync", LD)
                wl = load(wt0.ap(),
                          dsrc[k * 128:(k + 1) * 128,
                               cc * 1024:(cc + 1) * 1024],
                          wait=[prevF0["s"], prevF0["t"], prevF0["v2"],
                                vac, sab_prev])
                for hh in range(2):
                    sab = sch.call("scalar", "activation",
                                   absb.ap()[:, hh * 512:(hh + 1) * 512],
                                   wt0.ap()[:, hh * 512:(hh + 1) * 512],
                                   AF.Abs, wait=[wl, vac], inc=SS)
                    sab_prev = sab
                    sch.call("vector", "tensor_reduce", tcol.ap(),
                             absb.ap()[:, hh * 512:(hh + 1) * 512],
                             AX.X, ALU.add, wait=[sab])
                    vac = sch.call("vector", "tensor_add",
                                   colacc.ap()[:, 0:1],
                                   colacc.ap()[:, 0:1], tcol.ap(), inc=VS)
        tbt = sch.call("tensor", "matmul", pb[3].ap()[:, 0:8], ones128.ap(),
                       colacc.ap(), wait=[vac, bank_free[3]],
                       start=True, stop=True, inc=TS)
        smc = sch.call("scalar", "copy", mcol.ap(), pb[3].ap()[:, 0:1],
                       wait=[tbt], inc=SS)
        bank_free[3] = smc
        sch.call("vector", "tensor_scalar", mcol.ap(), mcol.ap(), 1.0 / nel,
                 None, wait=[smc], op0=ALU.mult)
        vmc = sch.call("vector", "tensor_scalar", mcol.ap(), mcol.ap(), 1e-5,
                       None, op0=ALU.max, inc=VS)
        sch.call("vector", "tensor_copy", cells.ap()[:, mi:mi + 1],
                 mcol.ap()[0:1, 0:1])
        sch.call("vector", "tensor_scalar", wq_c.ap()[:, 2 * mi:2 * mi + 1],
                 mcol.ap(), 0.5, None, op0=ALU.mult)
        vcn = sch.call("vector", "tensor_scalar",
                       wq_c.ap()[:, 2 * mi + 1:2 * mi + 2],
                       wq_c.ap()[:, 2 * mi:2 * mi + 1], -0.5 / 0.5, None,
                       op0=ALU.mult, inc=VS)
        vq_prev = vcn
        st_prev = None
        for k in range(nk):
            for cc in range(ncc // 2):
                fence("sync", LD)
                wl = load(wt0.ap(),
                          dsrc[k * 128:(k + 1) * 128,
                               cc * 1024:(cc + 1) * 1024], wait=[vq_prev])
                sch.call("vector", "tensor_scalar", q32a.ap(), wt0.ap(),
                         wq_c.ap()[:, 2 * mi:2 * mi + 1], None,
                         wait=[wl, vcn, st_prev], op0=ALU.is_gt)
                sch.call("vector", "tensor_scalar", q32b.ap(), wt0.ap(),
                         wq_c.ap()[:, 2 * mi + 1:2 * mi + 2], None,
                         op0=ALU.is_lt)
                vq_prev = sch.call("vector", "tensor_tensor", wqb.ap(),
                                   q32a.ap(), q32b.ap(), op=ALU.subtract,
                                   inc=VS)
                st_prev = store(scr_wq[mi][k * 128:(k + 1) * 128,
                                           cc * 1024:(cc + 1) * 1024],
                                wqb.ap(), wait=[vq_prev])
                fence("gpsimd", ST)
        prevF0["s"] = (SS, SS.n)
        prevF0["t"] = (TS, TS.n)
        prevF0["v2"] = vq_prev
    f0_scope.close()
    F0_done_V = (VS, VS.n)
    F0_done_ST = (ST, ST.n)

    f_scope = ExitStack()
    TB = 2 * CH
    x2blk = salloc(f_scope, "x2blk", [128, 8 * TB])
    xh3 = salloc(f_scope, "xh3", [128, 8 * TB])
    xq_b = salloc(f_scope, "xq_b", [128, 8 * TB], BF16)
    hp = salloc(f_scope, "hp", [128, 32 * TB])
    hq_b = salloc(f_scope, "hq_b", [128, 32 * TB], BF16)
    fc = salloc(f_scope, "fc", [128, 8 * TB])
    wgq_sb = salloc(f_scope, "wgq_sb", [128, 2 * 8 * 128], BF16)
    wuq_sb = salloc(f_scope, "wuq_sb", [128, 2 * 8 * 128], BF16)
    wdq_sb = salloc(f_scope, "wdq_sb", [128, 2 * 32 * 128], BF16)
    gdt = salloc(f_scope, "gdt", [128, TB])
    sgt = salloc(f_scope, "sgt", [128, TB])
    i32b = salloc(f_scope, "i32b", [128, TB], I32)
    rmx = salloc(f_scope, "rmx", [128, TB])
    sq_f = salloc(f_scope, "sq_f", [128, TB])
    bcf = salloc(f_scope, "bcf", [128, 4 * TB])
    f_rows = salloc(f_scope, "f_rows", [1, 5 * TB])

    BCR, BCB, BCSX, BCIG = range(4)
    BCSH, BCDR = 0, 1

    def bcf_s(i):
        return bcf.ap()[:, i * TB:(i + 1) * TB]

    mean_f = f_rows.ap()[:, 0 * TB:1 * TB]
    var_f = f_rows.ap()[:, 1 * TB:2 * TB]
    rstd_f = f_rows.ap()[:, 2 * TB:3 * TB]
    brow_f = f_rows.ap()[:, 3 * TB:4 * TB]
    aux_f = f_rows.ap()[:, 4 * TB:5 * TB]

    colT = salloc(f_scope, "colT", [128, 8])

    def part_tree_max(src_ap, out_row, wait=()):
        w = list(wait) + [tok_ident]
        tokv = None
        for cb in range(TB // 128):
            bk = pb[4 + (cb % 2)]
            tt = sch.call("tensor", "transpose", bk.ap()[:, 0:128],
                          src_ap[:, cb * 128:(cb + 1) * 128], ident.ap(),
                          wait=w + [bank_free[4 + (cb % 2)]], inc=TS)
            tokv = sch.call("vector", "tensor_reduce", colT.ap()[:, cb:cb + 1],
                            bk.ap()[:, 0:128], AX.X, ALU.max,
                            wait=[tt], inc=VS)
            bank_free[4 + (cb % 2)] = tokv
        toks = None
        for cb in range(TB // 128):
            bk = pb[4 + (cb % 2)]
            tt = sch.call("tensor", "transpose", bk.ap()[0:1, 0:128],
                          colT.ap()[:, cb:cb + 1], ident.ap(),
                          wait=[tokv, bank_free[4 + (cb % 2)]], inc=TS)
            toks = sch.call("scalar", "copy",
                            out_row[:, cb * 128:(cb + 1) * 128],
                            bk.ap()[0:1, 0:128], wait=[tt], inc=SS)
            bank_free[4 + (cb % 2)] = toks
        return toks

    def ln_rows(tok_sum, tok_sumsq, pbs, pbq, prev_rowuser):
        sch.call("vector", "tensor_scalar", mean_f, pbs, 1.0 / DM, None,
                 wait=[tok_sum, tok_sumsq, prev_rowuser], op0=ALU.mult)
        sch.call("vector", "tensor_scalar", var_f, pbq, 1.0 / DM, None,
                 op0=ALU.mult)
        sch.call("vector", "tensor_tensor", aux_f, mean_f, mean_f,
                 op=ALU.mult)
        sch.call("vector", "tensor_tensor", var_f, var_f, aux_f,
                 op=ALU.subtract)
        vrd = sch.call("vector", "tensor_scalar", var_f, var_f, EPS, None,
                       op0=ALU.add, inc=VS)
        sch.call("scalar", "activation", rstd_f, var_f, AF.Ln, wait=[vrd])
        sl = sch.call("scalar", "activation", rstd_f, rstd_f, AF.Exp,
                      scale=-0.5, inc=SS)
        sch.call("vector", "tensor_tensor", brow_f, mean_f, rstd_f,
                 wait=[sl], op=ALU.mult)
        return sch.call("vector", "tensor_scalar", brow_f, brow_f, -1.0,
                        None, op0=ALU.mult, inc=VS), vrd

    def bcast2(r1, r2, wait=()):
        t1b = sch.call("tensor", "matmul", pb[4].ap()[:, 0:TB], ones_row.ap(),
                       r1[0], wait=list(wait) + [bank_free[4]],
                       start=True, stop=True, inc=TS)
        s1b = sch.call("scalar", "copy", bcf_s(r1[1]), pb[4].ap()[:, 0:TB],
                       wait=[t1b, r1[2]], inc=SS)
        bank_free[4] = s1b
        if r2 is None:
            return s1b
        t2b = sch.call("tensor", "matmul", pb[5].ap()[:, 0:TB], ones_row.ap(),
                       r2[0], wait=list(wait) + [bank_free[5]],
                       start=True, stop=True, inc=TS)
        s2b = sch.call("scalar", "copy", bcf_s(r2[1]), pb[5].ap()[:, 0:TB],
                       wait=[t2b, r2[2]], inc=SS)
        bank_free[5] = s2b
        return s2b

    prevF = {"v": F0_done_V, "t": (TS, TS.n), "s": (SS, SS.n),
             "stores": (ST, ST.n), "stats": bank_free[3], "rowuser": None}

    for j in range(NCH):
        fence("sync", LD)
        for bb in range(B):
            load(x2blk.ap().rearrange("p (o z) -> p o z", z=TB)
                 [:, :, bb * CH:(bb + 1) * CH],
                 scr_x2[bb, :, j * CH:(j + 1) * CH]
                 .rearrange("(o p) j -> p o j", p=128),
                 wait=[C_done_ST, prevF["v"], prevF["t"]])
        xl = (LD, LD.n)

        sqp = None
        tok_sum = None
        for o in range(8):
            xo = x2blk.ap()[:, o * TB:(o + 1) * TB]
            stok = sch.call("scalar", "activation", sq_f.ap(), xo, AF.Square,
                            wait=[xl, prevF["stats"], sqp], inc=SS)
            sch.call("tensor", "matmul", pb[3].ap()[0:1, 0:TB], ones_col.ap(),
                     xo, wait=[xl, prevF["stats"]],
                     start=(o == 0), stop=(o == 7))
            if o == 7:
                tok_sum = sch.mark("tensor", TS)
            sqp = sch.call("tensor", "matmul", pb[6].ap()[0:1, 0:TB],
                           ones_col.ap(), sq_f.ap(), wait=[stok],
                           start=(o == 0), stop=(o == 7), inc=TS)
        rows_done, _ = ln_rows(tok_sum, sqp, pb[3].ap()[0:1, 0:TB],
                               pb[6].ap()[0:1, 0:TB], prevF["rowuser"])
        sb = bcast2((rstd_f, BCR, None), (brow_f, BCB, None),
                    wait=[rows_done])
        vx = None
        for o in range(8):
            xo = x2blk.ap()[:, o * TB:(o + 1) * TB]
            ho = xh3.ap()[:, o * TB:(o + 1) * TB]
            sch.call("vector", "tensor_tensor", ho, xo, bcf_s(BCR),
                     wait=[sb], op=ALU.mult)
            vt = sch.call("vector", "tensor_tensor", ho, ho, bcf_s(BCB),
                          op=ALU.add, inc=VS)
            sab = sch.call("scalar", "activation", sq_f.ap(), ho, AF.Abs,
                           wait=[vt], inc=SS)
            if o == 0:
                sch.call("vector", "tensor_copy", rmx.ap(), sq_f.ap(),
                         wait=[sab])
            else:
                sch.call("vector", "tensor_tensor", rmx.ap(), rmx.ap(),
                         sq_f.ap(), wait=[sab], op=ALU.max)
        vx = sch.mark("vector", VS)
        ptm = part_tree_max(rmx.ap(), mean_f, wait=[vx])
        sch.call("vector", "tensor_scalar", mean_f, mean_f, 1e-5, None,
                 wait=[ptm], op0=ALU.max)
        sch.call("vector", "reciprocal", var_f, mean_f)
        sch.call("vector", "tensor_scalar", rstd_f, var_f, 127.0, None,
                 op0=ALU.mult)
        sch.call("vector", "tensor_scalar", brow_f, mean_f,
                 cells.ap()[:, 0:1], None, op0=ALU.mult)
        vig = sch.call("vector", "tensor_scalar", brow_f, brow_f,
                       1.0 / 127.0, None, op0=ALU.mult, inc=VS)
        sb2 = bcast2((rstd_f, BCSX, None), (brow_f, BCIG, None), wait=[vig])
        vq = None
        for o in range(8):
            ho = xh3.ap()[:, o * TB:(o + 1) * TB]
            sch.call("vector", "tensor_tensor", ho, ho, bcf_s(BCSX),
                     wait=[sb2], op=ALU.mult)
            sch.call("vector", "tensor_copy", i32b.ap(), ho)
            vq = sch.call("vector", "tensor_copy",
                          xq_b.ap()[:, o * TB:(o + 1) * TB], i32b.ap(),
                          inc=VS)
        if dbg and j == 0:
            for o in range(8):
                cv = sch.call("vector", "tensor_copy", sq_f.ap(),
                              xq_b.ap()[:, o * TB:(o + 1) * TB],
                              wait=[vq, (ST, ST.n)], inc=VS)
                store(d_xq[:, o * TB:(o + 1) * TB], sq_f.ap(), wait=[cv])
            fence("gpsimd", ST)
        vhm = None
        wld = {}
        tsu = {}
        for et in range(33):
            if et < 32:
                par = et % 2
                fence("sync", LD)
                load(wgq_sb.ap()[:, par * 1024:(par + 1) * 1024],
                     scr_wq[0][:, et * 128:(et + 1) * 128]
                     .rearrange("(k p) j -> p k j", p=128),
                     wait=[F0_done_ST, tsu.get(et - 2)])
                load(wuq_sb.ap()[:, par * 1024:(par + 1) * 1024],
                     scr_wq[1][:, et * 128:(et + 1) * 128]
                     .rearrange("(k p) j -> p k j", p=128))
                wld[et] = (LD, LD.n)
            if et == 0:
                continue
            e = et - 1
            par = e % 2
            bg = 0 if par == 0 else 7
            bu = 1 if par == 0 else 2
            for k in range(8):
                sch.call("tensor", "matmul", pb[bg].ap()[:, 0:TB],
                         wgq_sb.ap()[:, (par * 8 + k) * 128:
                                     (par * 8 + k + 1) * 128],
                         xq_b.ap()[:, k * TB:(k + 1) * TB],
                         wait=[wld[e], vq, bank_free[bg]],
                         start=(k == 0), stop=(k == 7))
            tg = sch.mark("tensor", TS)
            for k in range(8):
                sch.call("tensor", "matmul", pb[bu].ap()[:, 0:TB],
                         wuq_sb.ap()[:, (par * 8 + k) * 128:
                                     (par * 8 + k + 1) * 128],
                         xq_b.ap()[:, k * TB:(k + 1) * TB],
                         wait=[wld[e], vq, bank_free[bu]],
                         start=(k == 0), stop=(k == 7))
            tu = sch.mark("tensor", TS)
            tsu[e] = tu
            vg = sch.call("vector", "tensor_tensor", gdt.ap(),
                          pb[bg].ap()[:, 0:TB], bcf_s(BCIG),
                          wait=[tg, sb2], op=ALU.mult, inc=VS)
            bank_free[bg] = vg
            sg = sch.call("scalar", "activation", sgt.ap(), gdt.ap(),
                          AF.Silu, wait=[vg], inc=SS)
            he = hp.ap()[:, e * TB:(e + 1) * TB]
            vh = sch.call("vector", "tensor_tensor", he, sgt.ap(),
                          pb[1 if par == 0 else 2].ap()[:, 0:TB],
                          wait=[sg, tu], op=ALU.mult, inc=VS)
            bank_free[bu] = vh
            sabh = sch.call("scalar", "activation", sq_f.ap(), he, AF.Abs,
                            wait=[vh], inc=SS)
            if e == 0:
                vhm = sch.call("vector", "tensor_copy", rmx.ap(), sq_f.ap(),
                               wait=[sabh], inc=VS)
            else:
                vhm = sch.call("vector", "tensor_tensor", rmx.ap(), rmx.ap(),
                               sq_f.ap(), wait=[sabh], op=ALU.max, inc=VS)
        if dbg and j == 0:
            for e in range(32):
                store(d_hp[:, e * TB:(e + 1) * TB],
                      hp.ap()[:, e * TB:(e + 1) * TB], wait=[vhm])
            fence("gpsimd", ST)
            sch.call("vector", "nop", wait=[(ST, ST.n)])
        ptm2 = part_tree_max(rmx.ap(), var_f, wait=[vhm])
        sch.call("vector", "tensor_scalar", aux_f, var_f,
                 cells.ap()[:, 1:2], None, wait=[ptm2], op0=ALU.mult)
        sch.call("vector", "tensor_tensor", aux_f, aux_f, mean_f,
                 op=ALU.mult)
        sch.call("vector", "tensor_scalar", aux_f, aux_f, 1.0 / 127.0, None,
                 op0=ALU.mult)
        sch.call("vector", "tensor_scalar", aux_f, aux_f, 1e-5, None,
                 op0=ALU.max)
        sch.call("vector", "reciprocal", var_f, aux_f)
        sch.call("vector", "tensor_tensor", var_f, var_f, mean_f,
                 op=ALU.mult)
        sch.call("vector", "tensor_scalar", var_f, var_f,
                 cells.ap()[:, 1:2], None, op0=ALU.mult)
        sch.call("vector", "tensor_scalar", aux_f, aux_f,
                 cells.ap()[:, 2:3], None, op0=ALU.mult)
        vdr = sch.call("vector", "tensor_scalar", aux_f, aux_f, 1.0 / 127.0,
                       None, op0=ALU.mult, inc=VS)
        sb3 = bcast2((var_f, BCSH, None), (aux_f, BCDR, None), wait=[vdr])
        vq2 = None
        for e in range(32):
            he = hp.ap()[:, e * TB:(e + 1) * TB]
            sch.call("vector", "tensor_tensor", he, he, bcf_s(BCSH),
                     wait=[sb3], op=ALU.mult)
            sch.call("vector", "tensor_copy", i32b.ap(), he)
            vq2 = sch.call("vector", "tensor_copy",
                           hq_b.ap()[:, e * TB:(e + 1) * TB], i32b.ap(),
                           inc=VS)
        wld2 = {}
        tsd = {}
        for ot in range(9):
            if ot < 8:
                par = ot % 2
                fence("sync", LD)
                load(wdq_sb.ap()[:, par * 4096:(par + 1) * 4096],
                     scr_wq[2][:, ot * 128:(ot + 1) * 128]
                     .rearrange("(k p) j -> p k j", p=128),
                     wait=[F0_done_ST, tsd.get(ot - 2)])
                wld2[ot] = (LD, LD.n)
            if ot == 0:
                continue
            o = ot - 1
            par = o % 2
            for k in range(32):
                sch.call("tensor", "matmul", pb[2].ap()[:, 0:TB],
                         wdq_sb.ap()[:, (par * 32 + k) * 128:
                                     (par * 32 + k + 1) * 128],
                         hq_b.ap()[:, k * TB:(k + 1) * TB],
                         wait=[wld2[o], vq2, bank_free[2]],
                         start=(k == 0), stop=(k == 31))
            td = sch.mark("tensor", TS)
            tsd[o] = td
            fo = fc.ap()[:, o * TB:(o + 1) * TB]
            vf = sch.call("vector", "tensor_tensor", fo,
                          pb[2].ap()[:, 0:TB], bcf_s(BCDR),
                          wait=[td, sb3], op=ALU.mult, inc=VS)
            bank_free[2] = vf
        if dbg and j == 0:
            for o in range(8):
                store(d_fc[:, o * TB:(o + 1) * TB],
                      fc.ap()[:, o * TB:(o + 1) * TB], wait=[vf])
            fence("gpsimd", ST)
            sch.call("vector", "nop", wait=[(ST, ST.n)])
        sqp = None
        tok_sum = None
        for o in range(8):
            fo = fc.ap()[:, o * TB:(o + 1) * TB]
            stok = sch.call("scalar", "activation", sq_f.ap(), fo, AF.Square,
                            wait=[vf, sqp], inc=SS)
            sch.call("tensor", "matmul", pb[3].ap()[0:1, 0:TB], ones_col.ap(),
                     fo, wait=[vf, bank_free[3]],
                     start=(o == 0), stop=(o == 7))
            if o == 7:
                tok_sum = sch.mark("tensor", TS)
            sqp = sch.call("tensor", "matmul", pb[6].ap()[0:1, 0:TB],
                           ones_col.ap(), sq_f.ap(), wait=[stok],
                           start=(o == 0), stop=(o == 7), inc=TS)
        rows_done, vrd4 = ln_rows(tok_sum, sqp, pb[3].ap()[0:1, 0:TB],
                                  pb[6].ap()[0:1, 0:TB], None)
        bank_free[3] = vrd4
        sb4 = bcast2((rstd_f, BCSX, None), (brow_f, BCIG, None),
                     wait=[rows_done])
        for o in range(8):
            fo = fc.ap()[:, o * TB:(o + 1) * TB]
            xo = x2blk.ap()[:, o * TB:(o + 1) * TB]
            sch.call("vector", "tensor_tensor", fo, fo, bcf_s(BCSX),
                     wait=[sb4, prevF["stores"]], op=ALU.mult)
            sch.call("vector", "tensor_tensor", fo, fo, bcf_s(BCIG),
                     op=ALU.add)
            vx3 = sch.call("vector", "tensor_tensor", fo, fo, xo,
                           op=ALU.add, inc=VS)
        for bb in range(B):
            store(d_out[bb, :, j * CH:(j + 1) * CH]
                  .rearrange("(o p) j -> p o j", p=128),
                  fc.ap().rearrange("p (o z) -> p o z", z=TB)
                  [:, :, bb * CH:(bb + 1) * CH], wait=[vx3])
        prevF["v"] = (VS, VS.n)
        prevF["t"] = (TS, TS.n)
        prevF["stats"] = vrd4
        prevF["rowuser"] = sb4
        prevF["stores"] = (ST, ST.n)
        fence("gpsimd", ST)

    f_scope.close()
    F_done_V = (VS, VS.n)
    F_done_S = (SS, SS.n)
    F_done_T = (TS, TS.n)

    sch.call("gpsimd", "nop", wait=[(ST, ST.n), (LD, LD.n), C_done_V,
                                    C_done_S, C_done_T])

    sch.finalize()
    stack.close()
    return nc


def _prep_inputs(inputs):
    def T(a):
        return np.ascontiguousarray(np.asarray(a, dtype=np.float32).T)

    def C(a):
        return np.ascontiguousarray(np.asarray(a, dtype=np.float32))

    def V(a):
        return np.ascontiguousarray(
            np.asarray(a, dtype=np.float32).reshape(-1, 1))

    return {
        "inWT": T(inputs["L0_attn_in_proj_w"]),
        "convW": C(inputs["L0_attn_conv_w"]),
        "convB": V(inputs["L0_attn_conv_b"]),
        "xprojWT": T(inputs["L0_attn_x_proj_w"]),
        "Alog": C(inputs["L0_attn_A_log"]),
        "Dvec": V(inputs["L0_attn_D"]),
        "dtowT": T(inputs["L0_attn_dt_out_w"]),
        "dtoB": V(inputs["L0_attn_dt_out_b"]),
        "opWT": T(inputs["L0_attn_out_proj_w"]),
        "minWT": T(inputs["L0_attn_m_in_w"]),
        "mconvW": C(inputs["L0_attn_m_conv_w"]),
        "mconvB": V(inputs["L0_attn_m_conv_b"]),
        "mxprojWT": T(inputs["L0_attn_m_x_proj_w"]),
        "mdtWT": T(inputs["L0_attn_m_dt_w"]),
        "mdtB": V(inputs["L0_attn_m_dt_b"]),
        "mAlog": C(inputs["L0_attn_m_A_log"]),
        "mDvec": V(inputs["L0_attn_m_D"]),
        "moutWT": T(inputs["L0_attn_m_out_w"]),
        "gWT": T(inputs["L0_ffn_w_gate"]),
        "uWT": T(inputs["L0_ffn_w_up"]),
        "dWT": T(inputs["L0_ffn_w_down"]),
    }


def kernel(**inputs):
    ncores = 8
    x = np.asarray(inputs["L0_attn_hidden_states"], dtype=np.float32)
    common = _prep_inputs(inputs)
    nc = build_nc()
    in_maps = []
    for i in range(ncores):
        m = dict(common)
        m["x_fm"] = np.ascontiguousarray(x[2 * i:2 * i + 2].transpose(0, 2, 1))
        in_maps.append(m)
    res = run_bass_kernel_spmd(nc, in_maps, core_ids=list(range(ncores)))
    outs = [np.asarray(r["out_fm"]).transpose(0, 2, 1) for r in res.results]
    return np.ascontiguousarray(np.concatenate(outs, axis=0))


# version 17
# speedup vs baseline: 1.1491x; 1.1305x over previous
import numpy as np
from contextlib import ExitStack

import concourse.bass as bass
import concourse.mybir as mybir
from concourse.bass_utils import run_bass_kernel_spmd


F32 = mybir.dt.float32
BF16 = mybir.dt.bfloat16
I32 = mybir.dt.int32
AF = mybir.ActivationFunctionType
ALU = mybir.AluOpType
AX = mybir.AxisListType

B = 2
L = 2048
DM = 1024
DI = 2048
NST = 16
DTR = 64
KC = 4
MDI, MN, MDTR, MKC = 64, 4, 4, 2
CH = 256
NCH = L // CH
NT = DI // 128
EPS = 1e-5

ENGINES = ("sync", "tensor", "vector", "scalar", "gpsimd")


class Sem:
    __slots__ = ("h", "n", "owner", "name", "is_dma")

    def __init__(self, h, owner, name, is_dma=False):
        self.h, self.n, self.owner, self.name = h, 0, owner, name
        self.is_dma = is_dma


class Sched:

    def __init__(self, nc, stack):
        self.nc = nc
        self.ops = {e: [] for e in ENGINES}
        self.waited = {e: {} for e in ENGINES}
        self.stack = stack

    def sem(self, owner, name, is_dma=False):
        h = self.stack.enter_context(self.nc.semaphore(name))
        return Sem(h, owner, name, is_dma)

    def call(self, _eng, _opname, *args, wait=(), inc=None, amt=1,
             raw=False, **kwargs):
        wl = []
        for tok in wait:
            if tok is None:
                continue
            s, v = tok
            if s.owner == _eng and not s.is_dma:
                continue
            prev = self.waited[_eng].get(id(s), 0)
            if v > prev:
                self.waited[_eng][id(s)] = v
                wl.append((s.h, v))
        if inc is not None:
            inc.n += amt
        tok_out = (inc, inc.n) if inc is not None else None
        import sys as _sys
        _ln = _sys._getframe(1).f_lineno
        self.ops[_eng].append((wl, _opname, args, kwargs,
                               inc.h if inc is not None else None, amt, _ln,
                               raw))
        return tok_out

    def mark(self, eng, sem, amt=1):
        wl, opname, args, kwargs, inch, a, _ln, raw = self.ops[eng][-1]
        if inch is not None:
            assert inch is sem.h, "last op already incs a different sem"
            return (sem, sem.n)
        sem.n += amt
        self.ops[eng][-1] = (wl, opname, args, kwargs, sem.h, amt, _ln, raw)
        return (sem, sem.n)

    def finalize(self):
        with self.nc.Block() as block:
            for name in ENGINES:
                ops = self.ops[name]

                def body(engine, ops=ops, name=name):
                    compute = name in ("vector", "scalar", "gpsimd")

                    def small_out(a):
                        try:
                            sh = a[0].shape
                            n = 1
                            for s in sh[1:]:
                                n *= s
                            return n <= 32
                        except Exception:
                            return False

                    for wl, opname, args, kwargs, inch, amt, _ln, raw in ops:
                        for h, v in wl:
                            engine.wait_ge(h, v)
                        try:
                            ins = getattr(engine, opname)(*args, **kwargs)
                        except Exception as e:
                            raise RuntimeError(
                                f"emit failed: {name}.{opname} from "
                                f"kernel.py:{_ln}: {e}") from e
                        if compute and inch is None and not raw and \
                                opname not in ("dma_start", "nop", "memset") \
                                and (opname == "tensor_reduce"
                                     or small_out(args)):
                            engine.drain(fusable=True)
                        if inch is not None:
                            if compute and opname != "dma_start" and not raw:
                                engine.drain(fusable=True).then_inc(inch, amt)
                            else:
                                ins.then_inc(inch, amt)
                getattr(block, name)(body)


def build_nc(dbg=False, phases="ABCF"):
    nc = bass.Bass("TRN2", target_bir_lowering=False,
                   detect_race_conditions=False)
    stack = ExitStack()

    def din(name, shape):
        return nc.dram_tensor(name, list(shape), F32, kind="ExternalInput").ap()

    dx = din("x_fm", (B, DM, L))
    d_inWT = din("inWT", (DM, 2 * DI))
    d_convW = din("convW", (DI, KC))
    d_convB = din("convB", (DI, 1))
    d_xprojWT = din("xprojWT", (DI, DTR + 2 * NST))
    d_Alog = din("Alog", (DI, NST))
    d_D = din("Dvec", (DI, 1))
    d_dtowT = din("dtowT", (DTR, DI))
    d_dtoB = din("dtoB", (DI, 1))
    d_opWT = din("opWT", (DI, DM))
    d_minWT = din("minWT", (DTR, 2 * MDI))
    d_mconvW = din("mconvW", (MDI, MKC))
    d_mconvB = din("mconvB", (MDI, 1))
    d_mxprojWT = din("mxprojWT", (MDI, MDTR + 2 * MN))
    d_mdtWT = din("mdtWT", (MDTR, MDI))
    d_mdtB = din("mdtB", (MDI, 1))
    d_mAlog = din("mAlog", (MDI, MN))
    d_mD = din("mDvec", (MDI, 1))
    d_moutWT = din("moutWT", (MDI, DTR))
    d_gWT = din("gWT", (DM, 4096))
    d_uWT = din("uWT", (DM, 4096))
    d_dWT = din("dWT", (4096, DM))

    d_out = nc.dram_tensor("out_fm", [B, DM, L], F32, kind="ExternalOutput").ap()

    scr_kind = "ExternalOutput" if dbg else "Internal"
    scr_u = nc.dram_tensor("scr_u", [B, DI, L], F32, kind=scr_kind).ap()
    scr_sz = nc.dram_tensor("scr_sz", [B, DI, L], F32, kind=scr_kind).ap()
    scr_yg = nc.dram_tensor("scr_yg", [B, DI, L], F32, kind=scr_kind).ap()
    scr_xbc = nc.dram_tensor("scr_xbc", [32, B * L], F32, kind=scr_kind).ap()
    scr_dtm = nc.dram_tensor("scr_dtm", [DTR, B * L], F32,
                             kind=scr_kind).ap()
    scr_x2 = nc.dram_tensor("scr_x2", [B, DM, L], F32,
                            kind="ExternalInput" if phases == "F"
                            else scr_kind).ap()
    if dbg:
        d_xdbl = nc.dram_tensor("dbg_xdbl", [96, B * L], F32,
                                kind="ExternalOutput").ap()
        d_dtm = nc.dram_tensor("dbg_dtm", [DTR, B * L], F32,
                               kind="ExternalOutput").ap()
        d_xq = nc.dram_tensor("dbg_xq", [128, 8 * 512], F32,
                              kind="ExternalOutput").ap()
        d_hp = nc.dram_tensor("dbg_hp", [128, 32 * 512], F32,
                              kind="ExternalOutput").ap()
        d_fc = nc.dram_tensor("dbg_fc", [128, 8 * 512], F32,
                              kind="ExternalOutput").ap()

    sch = Sched(nc, stack)
    LD = sch.sem("sync", "ld", is_dma=True)
    ST = sch.sem("gpsimd", "st", is_dma=True)
    TS = sch.sem("tensor", "tsm")
    VS = sch.sem("vector", "vsm")
    SS = sch.sem("scalar", "ssm")
    GS = sch.sem("gpsimd", "gsm")

    def fence(eng, semq):
        sch.call(eng, "nop", wait=[(semq, semq.n)])

    alloc = nc.alloc_sbuf_tensor

    def salloc(scope, name, shape, dtype=F32):
        return scope.enter_context(nc.sbuf_tensor(name, shape, dtype))

    ones_col = alloc("ones_col", [128, 1], F32)
    ones_row = alloc("ones_row", [1, 128], F32)
    ones_3 = alloc("ones_3", [65, 128], F32)
    ident = alloc("ident", [128, 128], F32)
    tmp_init = alloc("tmp_init", [128, 256], F32)
    cells = alloc("cells", [1, 16], F32)
    mk_scope = ExitStack()
    a_neg = salloc(mk_scope, "a_neg", [128, NT * NST])
    convw_c = salloc(mk_scope, "convw_c", [128, NT * KC])
    convb_c = salloc(mk_scope, "convb_c", [128, NT])
    dvec_c = salloc(mk_scope, "dvec_c", [128, NT])
    dtob_c = salloc(mk_scope, "dtob_c", [128, NT])
    m_aneg = salloc(mk_scope, "m_aneg", [128, MN])
    m_convw = salloc(mk_scope, "m_convw", [128, MKC])
    m_convb = salloc(mk_scope, "m_convb", [128, 1])
    m_dvec = salloc(mk_scope, "m_dvec", [128, 1])
    m_dtb = salloc(mk_scope, "m_dtb", [128, 1])
    states = salloc(mk_scope, "states", [128, B * NT * NST])
    w_dtow = salloc(mk_scope, "w_dtow", [DTR, DI])
    w_minW = salloc(mk_scope, "w_minW", [DTR, 2 * MDI])
    w_mxproj = salloc(mk_scope, "w_mxproj", [MDI, MDTR + 2 * MN])
    w_mdt = salloc(mk_scope, "w_mdt", [MDTR, MDI])
    w_mout = salloc(mk_scope, "w_mout", [MDI, DTR])
    w_xproj = [salloc(mk_scope, f"w_xproj{k}", [128, 96]) for k in range(NT)]
    rows = salloc(mk_scope, "rows", [1, 2560])
    xd_scope = ExitStack()
    x_dbl = salloc(xd_scope, "x_dbl", [96, B * L])
    dtm_sb = salloc(xd_scope, "dtm_sb", [DTR, B * L])

    pb = [nc.alloc_psum_tensor(f"pb{i}", [128, 512], F32) for i in range(8)]
    bank_free = [None] * 8

    def load(dst_ap, src_ap, wait=()):
        return sch.call("sync", "dma_start", dst_ap, src_ap,
                        wait=wait, inc=LD, amt=16)

    def store(dst_ap, src_ap, wait=()):
        return sch.call("gpsimd", "dma_start", dst_ap, src_ap,
                        wait=wait, inc=ST, amt=16)

    for t in range(NT):
        load(tmp_init.ap()[:, t * NST:(t + 1) * NST],
             d_Alog[t * 128:(t + 1) * 128, :])
    tok_alog = (LD, LD.n)
    fence("sync", LD)
    for t in range(NT):
        load(convw_c.ap()[:, t * KC:(t + 1) * KC],
             d_convW[t * 128:(t + 1) * 128, :])
        load(convb_c.ap()[:, t:t + 1], d_convB[t * 128:(t + 1) * 128, :])
        load(dvec_c.ap()[:, t:t + 1], d_D[t * 128:(t + 1) * 128, :])
        load(dtob_c.ap()[:, t:t + 1], d_dtoB[t * 128:(t + 1) * 128, :])
    load(m_aneg.ap()[0:MDI, :], d_mAlog[:, :])
    load(m_convw.ap()[0:MDI, :], d_mconvW[:, :])
    load(m_convb.ap()[0:MDI, :], d_mconvB[:, :])
    load(m_dvec.ap()[0:MDI, :], d_mD[:, :])
    load(m_dtb.ap()[0:MDI, :], d_mdtB[:, :])
    load(w_dtow.ap(), d_dtowT)
    load(w_minW.ap(), d_minWT)
    load(w_mxproj.ap(), d_mxprojWT)
    load(w_mdt.ap(), d_mdtWT)
    load(w_mout.ap(), d_moutWT)
    for k in range(NT):
        load(w_xproj[k].ap(), d_xprojWT[k * 128:(k + 1) * 128, :])
    tok_small = (LD, LD.n)
    fence("sync", LD)

    t1 = sch.call("scalar", "activation", a_neg.ap(), tmp_init.ap(), AF.Exp,
                  wait=[tok_alog], inc=SS)
    t2 = sch.call("scalar", "activation", m_aneg.ap()[0:MDI, :],
                  m_aneg.ap()[0:MDI, :], AF.Exp, wait=[tok_small], inc=SS)
    sch.call("vector", "tensor_scalar", a_neg.ap(), a_neg.ap(), -1.0, None,
             wait=[t1], op0=ALU.mult)
    sch.call("vector", "tensor_scalar", m_aneg.ap()[0:MDI, :],
             m_aneg.ap()[0:MDI, :], -1.0, None, wait=[t2], op0=ALU.mult)
    sch.call("vector", "memset", ones_col.ap(), 1.0)
    sch.call("vector", "memset", ones_row.ap(), 1.0)
    sch.call("vector", "memset", ones_3.ap(), 1.0)
    tok_ones = sch.call("vector", "memset", states.ap(), 0.0, inc=VS)
    gi = sch.call("gpsimd", "iota", ident.ap(), [[0, 128]],
                  channel_multiplier=1,
                  allow_small_or_imprecise_dtypes=True, wait=[t1])
    gi = sch.call("gpsimd", "iota", tmp_init.ap()[:, 0:128], [[1, 128]],
                  channel_multiplier=0,
                  allow_small_or_imprecise_dtypes=True, inc=GS)
    tok_ident = sch.call("vector", "tensor_tensor", ident.ap(), ident.ap(),
                         tmp_init.ap()[:, 0:128], wait=[gi],
                         op=ALU.is_equal, inc=VS)

    run_abc = "A" in phases
    a_scope = ExitStack()
    CHA = 512
    NCHA = L // CHA
    wa_in = salloc(a_scope, "wa_in", [128, 8 * 2048])
    xa = salloc(a_scope, "xa", [128, 2 * 8 * CHA])
    acta = salloc(a_scope, "acta", [128, NT * CHA])
    hist = salloc(a_scope, "hist", [128, B * NT * 4])
    sqa = salloc(a_scope, "sqa", [128, CHA])
    csum = salloc(a_scope, "csum", [1, 2048])

    mean_r = rows.ap()[:, 0:CHA]
    var_r = rows.ap()[:, CHA:2 * CHA]
    m2_r = rows.ap()[:, 2 * CHA:3 * CHA]
    rstd_r = rows.ap()[:, 3 * CHA:4 * CHA]
    brow_r = rows.ap()[:, 4 * CHA:5 * CHA]

    itersA = [(b, c) for c in range(NCHA) for b in range(B)]
    iters = [(b, c) for c in range(NCH) for b in range(B)]
    xld = {}
    prev = {"stores": None, "tdone": None, "rows_S": None, "stats_rd": None}

    sch.call("vector", "memset", hist.ap(), 0.0, inc=VS)

    for half in (range(2) if run_abc else []):
        fence("sync", LD)
        for k in range(8):
            load(wa_in.ap()[:, k * 2048:(k + 1) * 2048],
                 d_inWT[k * 128:(k + 1) * 128, half * 2048:(half + 1) * 2048],
                 wait=[prev["tdone"]])
        tok_win = (LD, LD.n)

        tok_csum = None
        for cc in range(4):
            for k in range(8):
                sch.call("tensor", "matmul", pb[0].ap()[0:1, 0:512],
                         ones_col.ap(),
                         wa_in.ap()[:, k * 2048 + cc * 512:
                                    k * 2048 + (cc + 1) * 512],
                         wait=[tok_win, bank_free[0], tok_ones],
                         start=(k == 0), stop=(k == 7))
            tcs = sch.mark("tensor", TS)
            bank_free[0] = sch.call("scalar", "copy",
                                    csum.ap()[:, cc * 512:(cc + 1) * 512],
                                    pb[0].ap()[0:1, 0:512], wait=[tcs], inc=SS)
        tok_csum = bank_free[0]
        prev["stats_rd"] = tok_csum

        def load_x(it):
            b, c = itersA[it]
            buf = it % 2
            fence("sync", LD)
            load(xa.ap()[:, buf * 8 * CHA:(buf + 1) * 8 * CHA],
                 dx[b, :, c * CHA:(c + 1) * CHA]
                 .rearrange("(k p) j -> p k j", p=128),
                 wait=[prev["tdone"]])
            xld[it] = (LD, LD.n)

        load_x(0)

        for it, (b, c) in enumerate(itersA):
            buf = it % 2
            xt = [xa.ap()[:, (buf * 8 + k) * CHA:(buf * 8 + k + 1) * CHA]
                  for k in range(8)]
            if it + 1 < len(itersA):
                load_x(it + 1)

            sq_prev = None
            tok_pb0 = None
            for k in range(8):
                stok = sch.call("scalar", "activation", sqa.ap(), xt[k],
                                AF.Square,
                                wait=[xld[it], prev["stats_rd"], sq_prev],
                                inc=SS)
                sch.call("tensor", "matmul", pb[0].ap()[0:1, 0:CHA],
                         ones_col.ap(), xt[k],
                         wait=[xld[it], prev["stats_rd"], tok_ones],
                         start=(k == 0), stop=(k == 7))
                if k == 7:
                    tok_pb0 = sch.mark("tensor", TS)
                sq_prev = sch.call("tensor", "matmul", pb[1].ap()[0:1, 0:CHA],
                                   ones_col.ap(), sqa.ap(), wait=[stok],
                                   start=(k == 0), stop=(k == 7), inc=TS)
            tstat = sq_prev

            sch.call("vector", "tensor_scalar", mean_r, pb[0].ap()[0:1, 0:CHA],
                     1.0 / DM, None,
                     wait=[tstat, tok_pb0, prev["tdone"], prev["rows_S"]],
                     op0=ALU.mult)
            sch.call("vector", "tensor_scalar", var_r, pb[1].ap()[0:1, 0:CHA],
                     1.0 / DM, None, op0=ALU.mult)
            sch.call("vector", "tensor_tensor", m2_r, mean_r, mean_r,
                     op=ALU.mult)
            sch.call("vector", "tensor_tensor", var_r, var_r, m2_r,
                     op=ALU.subtract)
            stats_rd = sch.call("vector", "tensor_scalar", var_r, var_r, EPS,
                                None, op0=ALU.add, inc=VS)
            prev["stats_rd"] = stats_rd
            sl = sch.call("scalar", "activation", rstd_r, var_r, AF.Ln,
                          wait=[stats_rd])
            sl = sch.call("scalar", "activation", rstd_r, rstd_r, AF.Exp,
                          scale=-0.5, inc=SS)
            sch.call("vector", "tensor_tensor", brow_r, mean_r, rstd_r,
                     wait=[sl], op=ALU.mult)
            rows_done = sch.call("vector", "tensor_scalar", brow_r, brow_r,
                                 -1.0, None, op0=ALU.mult, inc=VS)
            prev["rows_S"] = sl
            tbr = sch.call("tensor", "matmul", pb[2].ap()[:, 0:CHA],
                           ones_row.ap(), rstd_r,
                           wait=[rows_done, tok_ident, bank_free[2]],
                           start=True, stop=True, inc=TS)
            xh_tok = None
            for k in range(8):
                xh_tok = sch.call("vector", "tensor_tensor", xt[k], xt[k],
                                  pb[2].ap()[:, 0:CHA], wait=[tbr],
                                  op=ALU.mult, inc=VS)
            bank_free[2] = xh_tok

            for t in range(NT):
                bi = 3 + (t % 2)
                for k in range(8):
                    sch.call("tensor", "matmul", pb[bi].ap()[:, 0:CHA],
                             wa_in.ap()[:, k * 2048 + t * 128:
                                        k * 2048 + (t + 1) * 128],
                             xt[k],
                             wait=[xh_tok, tok_win, bank_free[bi]],
                             start=(k == 0), stop=False)
                tmm = sch.call("tensor", "matmul", pb[bi].ap()[:, 0:CHA],
                               csum.ap()[:, t * 128:(t + 1) * 128], brow_r,
                               wait=[tok_csum, rows_done],
                               start=False, stop=True, inc=TS)
                aap = acta.ap()[:, t * CHA:(t + 1) * CHA]
                if half == 0:
                    hap = hist.ap()[:, (b * NT + t) * 4:(b * NT + t) * 4 + 4]
                    wcol = [convw_c.ap()[:, t * KC + j:t * KC + j + 1]
                            for j in range(KC)]
                    sch.call("vector", "tensor_scalar", aap,
                             pb[bi].ap()[:, 0:CHA], wcol[3], None,
                             wait=[tmm, prev["stores"], prev["tdone"]],
                             op0=ALU.mult)
                    for j in range(3):
                        sh = 3 - j
                        sch.call("vector", "scalar_tensor_tensor",
                                 aap[:, sh:CHA], pb[bi].ap()[:, 0:CHA - sh],
                                 wcol[j], aap[:, sh:CHA],
                                 op0=ALU.mult, op1=ALU.add)
                        sch.call("vector", "scalar_tensor_tensor",
                                 aap[:, 0:sh], hap[:, 1 + j:1 + j + sh],
                                 wcol[j], aap[:, 0:sh],
                                 op0=ALU.mult, op1=ALU.add)
                    vconv = sch.call("vector", "tensor_copy", hap[:, 1:4],
                                     pb[bi].ap()[:, CHA - 3:CHA], inc=VS)
                    bank_free[bi] = vconv
                    su = sch.call("scalar", "activation", aap, aap, AF.Silu,
                                  bias=convb_c.ap()[:, t:t + 1],
                                  wait=[vconv], inc=SS)
                    sch.call("tensor", "matmul", pb[5].ap()[0:96, 0:CHA],
                             w_xproj[t].ap(), aap,
                             wait=[su, tok_small, bank_free[5]],
                             start=(t == 0), stop=(t == NT - 1))
                    a_st_tok = su
                else:
                    stok = sch.call("scalar", "activation", aap,
                                    pb[bi].ap()[:, 0:CHA], AF.Silu,
                                    wait=[tmm, prev["stores"],
                                          prev["tdone"]], inc=SS)
                    bank_free[bi] = stok
                    a_st_tok = stok
            dstA = scr_u if half == 0 else scr_sz
            store(dstA[b, :, c * CHA:(c + 1) * CHA]
                  .rearrange("(k p) j -> p k j", p=128),
                  acta.ap(), wait=[a_st_tok])
            txp = sch.mark("tensor", TS)
            if half == 0:
                bank_free[5] = sch.call(
                    "scalar", "copy",
                    x_dbl.ap()[:, b * L + c * CHA:b * L + (c + 1) * CHA],
                    pb[5].ap()[0:96, 0:CHA], wait=[txp], inc=SS)
            prev["stores"] = (ST, ST.n)
            prev["tdone"] = txp
            fence("gpsimd", ST)

    a_scope.close()
    A_done_S = (SS, SS.n)
    A_done_V = (VS, VS.n)
    A_done_T = (TS, TS.n)
    A_done_ST = (ST, ST.n)

    bi_scope = ExitStack()
    xzm = salloc(bi_scope, "xzm", [128, L])
    xum = salloc(bi_scope, "xum", [64, L])
    szm = salloc(bi_scope, "szm", [64, L])
    xdm = salloc(bi_scope, "xdm", [12, L])
    xbcm = salloc(bi_scope, "xbcm", [1, L])
    dmm = salloc(bi_scope, "dmm", [64, L])
    dtms = salloc(bi_scope, "dtms", [64, L])
    tma = salloc(bi_scope, "tma", [64, L])
    tmb = salloc(bi_scope, "tmb", [64, L])
    w_m = salloc(bi_scope, "w_m", [64, L])
    bcm = salloc(bi_scope, "bcm", [64, L])
    sm = salloc(bi_scope, "sm", [64, L])
    ym = salloc(bi_scope, "ym", [64, L])

    def mm512(dst_sb, lhsT, rhs_base_ap, n_cc, psum_rows, wait=(), kname=None):
        toks = []
        for cc in range(n_cc):
            tmmi = sch.call("tensor", "matmul",
                            pb[6].ap()[0:psum_rows, 0:512], lhsT,
                            rhs_base_ap[:, cc * 512:(cc + 1) * 512],
                            wait=list(wait) + [bank_free[6]],
                            start=True, stop=True, inc=TS)
            bank_free[6] = sch.call("scalar", "copy",
                                    dst_sb[:, cc * 512:(cc + 1) * 512],
                                    pb[6].ap()[0:psum_rows, 0:512],
                                    wait=[tmmi], inc=SS)
        return bank_free[6]

    tok_dtm = None
    for b in (range(B) if run_abc else []):
        xdb = x_dbl.ap()[0:64, b * L:(b + 1) * L]
        tk = mm512(xzm.ap(), w_minW.ap(), xdb, 4, 128,
                   wait=[A_done_S, tok_small])
        x_m = xzm.ap()[0:64, :]
        z_m = xzm.ap()[64:128, :]
        w0 = m_convw.ap()[0:64, 0:1]
        w1 = m_convw.ap()[0:64, 1:2]
        sch.call("vector", "tensor_scalar", xum.ap(), x_m, w1, None,
                 wait=[tk], op0=ALU.mult)
        vtk = sch.call("vector", "scalar_tensor_tensor", xum.ap()[:, 1:L],
                       x_m[:, 0:L - 1], w0, xum.ap()[:, 1:L],
                       op0=ALU.mult, op1=ALU.add, inc=VS)
        stk = sch.call("scalar", "activation", xum.ap(), xum.ap(), AF.Silu,
                       bias=m_convb.ap()[0:64, :], wait=[vtk], inc=SS)
        szk = sch.call("scalar", "activation", szm.ap(), z_m, AF.Silu,
                       wait=[tk], inc=SS)
        tk = mm512(xdm.ap(), w_mxproj.ap(), xum.ap(), 4, 12, wait=[stk])
        tok_xdm = tk
        tk = mm512(dmm.ap(), w_mdt.ap(), xdm.ap()[0:4, :], 4, 64, wait=[tk])
        sch.call("scalar", "activation", tma.ap(), dmm.ap(), AF.Abs,
                 bias=m_dtb.ap()[0:64, :], wait=[tk])
        sch.call("scalar", "activation", tma.ap(), tma.ap(), AF.Exp,
                 scale=-1.0)
        sch.call("scalar", "add", tma.ap(), tma.ap(), 1.0)
        sch.call("scalar", "activation", tma.ap(), tma.ap(), AF.Ln)
        ssp = sch.call("scalar", "activation", tmb.ap(), dmm.ap(), AF.Relu,
                       bias=m_dtb.ap()[0:64, :], inc=SS)
        vdt = sch.call("vector", "tensor_add", dtms.ap(), tma.ap(), tmb.ap(),
                       wait=[ssp])
        vw = sch.call("vector", "tensor_mul", w_m.ap(), dtms.ap(), xum.ap(),
                      inc=VS)
        vym = sch.call("vector", "tensor_scalar", ym.ap(), xum.ap(),
                       m_dvec.ap()[0:64, :], None, op0=ALU.mult, inc=VS)
        vsc = None
        vym_prev = vym
        tkb_prev = None
        for n in range(MN):
            sda = sch.call("scalar", "activation", tma.ap(), dtms.ap(),
                           AF.Exp, scale=m_aneg.ap()[0:64, n:n + 1],
                           wait=[vw, vsc], inc=SS)
            fence("sync", LD)
            gb = load(xbcm.ap(), xdm.ap()[4 + n:5 + n, :],
                      wait=[tok_xdm, tkb_prev])
            tkb = mm512(bcm.ap(), ones_row.ap()[0:1, 0:64],
                        xbcm.ap(), 4, 64, wait=[vw, vym_prev, gb])
            vdb = sch.call("vector", "tensor_mul", tmb.ap(), w_m.ap(),
                           bcm.ap(), wait=[tkb])
            vsc = sch.call("vector", "tensor_tensor_scan", sm.ap(), tma.ap(),
                           tmb.ap(), 0.0, wait=[sda],
                           op0=ALU.mult, op1=ALU.add, inc=VS)
            fence("sync", LD)
            gc = load(xbcm.ap(), xdm.ap()[8 + n:9 + n, :],
                      wait=[tok_xdm, tkb])
            tkb_prev = tkc = mm512(bcm.ap(), ones_row.ap()[0:1, 0:64],
                                   xbcm.ap(), 4, 64, wait=[vsc, gc])
            sch.call("vector", "tensor_mul", sm.ap(), sm.ap(), bcm.ap(),
                     wait=[tkc])
            vym_prev = vym = sch.call("vector", "tensor_add", ym.ap(),
                                      ym.ap(), sm.ap(), inc=VS)
        vyg = sch.call("vector", "tensor_mul", w_m.ap(), ym.ap(), szm.ap(),
                       wait=[szk], inc=VS)
        tok_dtm = mm512(dtm_sb.ap()[:, b * L:(b + 1) * L], w_mout.ap(),
                        w_m.ap(), 4, 64, wait=[vyg])
        store(scr_dtm[:, b * L:(b + 1) * L],
              dtm_sb.ap()[:, b * L:(b + 1) * L], wait=[tok_dtm])
    if run_abc:
        store(scr_xbc, x_dbl.ap()[64:96, :], wait=[A_done_S])
    if dbg and run_abc:
        fence("gpsimd", ST)
        store(d_xdbl, x_dbl.ap(), wait=[A_done_S])
        store(d_dtm, dtm_sb.ap(), wait=[tok_dtm])
    bi_scope.close()
    xd_scope.close()
    fence("gpsimd", ST)
    spill_tok = (ST, ST.n)

    bm_scope = ExitStack()
    ub = salloc(bm_scope, "ub", [128, NT * CH])
    bc_sb = salloc(bm_scope, "bc_sb", [128, 2, 2 * NST * CH])
    s3 = salloc(bm_scope, "s3", [128, 2, NST, CH])
    da3 = salloc(bm_scope, "da3", [128, 2, NST, CH])
    dbu_b = salloc(bm_scope, "dbu_b", [128, CH])
    dt4 = salloc(bm_scope, "dt4", [128, 2, CH])
    w4 = salloc(bm_scope, "w4", [128, 4, CH])
    t14 = salloc(bm_scope, "t14", [128, 2, CH])
    t24 = salloc(bm_scope, "t24", [128, 2, CH])
    yg_sb = salloc(bm_scope, "yg_sb", [128, 2 * CH])
    xbc = salloc(bm_scope, "xbc", [65, 11 * CH])
    dtm_c = salloc(bm_scope, "dtm_c", [DTR, 2, CH])

    def xbc_ap(n):
        return xbc.ap()[(n // 11) * 32:(n // 11) * 32 + 1,
                        (n % 11) * CH:(n % 11 + 1) * CH]

    ut_tok = {}
    ztoks = {}
    da_t = {}
    pd_free = [A_done_V, A_done_V]
    py_free = [A_done_V, A_done_V]
    yg_st = [None, None]
    last_z = (VS, VS.n)
    BI = len(iters)

    def u_slice(t):
        return ub.ap()[:, t * CH:(t + 1) * CH]

    def emit_u_group(it, t0, wait_tok):
        b, c = iters[it]
        fence("sync", LD)
        load(ub.ap()[:, t0 * CH:(t0 + 4) * CH],
             scr_u[b, t0 * 128:(t0 + 4) * 128, c * CH:(c + 1) * CH]
             .rearrange("(k p) j -> p k j", p=128),
             wait=[spill_tok, wait_tok])
        tok = (LD, LD.n)
        for t in range(t0, t0 + 4):
            ut_tok[(it, t)] = tok

    for g4 in range(4):
        emit_u_group(0, 4 * g4, None)
    gld = {}
    tok_bc = {}
    dtm_tok = {}
    iter_pd = {}

    def emit_ldbc(it):
        bn, cn = iters[it]
        war = tok_bc.get(it - 1, spill_tok)
        off = bn * L + cn * CH
        fence("sync", LD)
        for n in range(2 * NST):
            load(xbc_ap(n), scr_xbc[n:n + 1, off:off + CH], wait=[war])
        gld[it] = (LD, LD.n)

    def emit_lddtm(it):
        bn, cn = iters[it]
        fence("sync", LD)
        load(dtm_c.ap()[:, it % 2, :],
             scr_dtm[:, bn * L + cn * CH:bn * L + (cn + 1) * CH],
             wait=[iter_pd.get(it - 2, spill_tok)])
        dtm_tok[it] = (LD, LD.n)

    def emit_bc(it):
        for n in range(2 * NST):
            bi = 4 + (n % 2)
            bp = (n // 11) * 32
            tmmi = sch.call("tensor", "matmul", pb[bi].ap()[:, 0:CH],
                            ones_3.ap()[bp:bp + 1, :], xbc_ap(n),
                            wait=[gld[it], bank_free[bi], tok_ident],
                            start=True, stop=True, inc=TS)
            bank_free[bi] = sch.call("scalar", "copy",
                                     bc_sb.ap()[:, it % 2,
                                                n * CH:(n + 1) * CH],
                                     pb[bi].ap()[:, 0:CH],
                                     wait=[tmmi, last_z, spill_tok],
                                     inc=SS)
        tok_bc[it] = bank_free[5]

    if run_abc:
        emit_ldbc(0)
        emit_lddtm(0)
        emit_bc(0)
        if BI > 1:
            emit_ldbc(1)
            emit_lddtm(1)
    pend = []

    def flush_pend(tok, nxt_it):
        while pend:
            pit, ps, ygap = pend.pop(0)
            pb_, pc_ = iters[pit]
            yg_st[ps % 2] = store(
                scr_yg[pb_, ps * 128:(ps + 1) * 128,
                       pc_ * CH:(pc_ + 1) * CH], ygap, wait=[tok])
            if nxt_it is not None and ps % 4 == 3:
                emit_u_group(nxt_it, ps - 3, tok)

    pend_vev = [None]

    def flush_vev():
        if pend_vev[0] is None:
            return
        vit, vs, py, tok_y = pend_vev[0]
        pend_vev[0] = None
        par = vs % 2
        ygap = yg_sb.ap()[:, par * CH:(par + 1) * CH]
        vev = sch.call("vector", "scalar_tensor_tensor", ygap, u_slice(vs),
                       dvec_c.ap()[:, vs:vs + 1], py.ap()[:, 0:CH],
                       wait=[tok_y, yg_st[par]],
                       op0=ALU.mult, op1=ALU.add, inc=VS, raw=True)
        py_free[par] = vev
        pend.append((vit, vs, ygap))

    def emit_scans(it, s):
        b, c = iters[it]
        par = s % 2
        g = it * NT + s
        dab = da3.ap()[:, par, :, :]
        s3p = s3.ap()[:, par, :, :]
        sti = states.ap()[:, (b * NT + s) * NST:(b * NT + s + 1) * NST]
        for n in range(NST):
            sch.call("vector", "tensor_mul", dbu_b.ap(),
                     w4.ap()[:, s % 4, :],
                     bc_sb.ap()[:, it % 2, n * CH:(n + 1) * CH],
                     wait=[da_t[g], tok_bc[it]])
            sch.call("vector", "tensor_tensor_scan", s3p[:, n, :],
                     dab[:, n, :], dbu_b.ap(), sti[:, n:n + 1],
                     op0=ALU.mult, op1=ALU.add)
        sch.call("vector", "tensor_copy", sti, s3p[:, :, CH - 1],
                 raw=True)
        for n in range(NST):
            sch.call("vector", "tensor_mul", s3p[:, n, :],
                     s3p[:, n, :],
                     bc_sb.ap()[:, it % 2,
                                (NST + n) * CH:(NST + n + 1) * CH])
        flush_vev()
        ztoks[g] = tok_z = sch.mark("vector", VS)
        nxt = it + 1 if it + 1 < BI else None
        flush_pend(tok_z, nxt)
        py = pb[2 + par]
        for n in range(NST):
            sch.call("tensor", "matmul", py.ap()[:, 0:CH], ident.ap(),
                     s3p[:, n, :], wait=[tok_z, py_free[par]],
                     start=(n == 0), stop=(n == NST - 1))
        tok_y = sch.mark("tensor", TS)
        pend_vev[0] = (it, s, py, tok_y)

    for it, (b, c) in (enumerate(iters) if run_abc else []):
        sp_t = {}

        def emit_dsp(t):
            par = t % 2
            g = it * NT + t
            pd = pb[par]
            tok_pd = sch.call("tensor", "matmul", pd.ap()[:, 0:CH],
                              w_dtow.ap()[:, t * 128:(t + 1) * 128],
                              dtm_c.ap()[:, it % 2, :],
                              wait=[dtm_tok[it], pd_free[par]],
                              start=True, stop=True, inc=TS)
            iter_pd[it] = tok_pd
            bcol = dtob_c.ap()[:, t:t + 1]
            t1 = t14.ap()[:, par, :]
            t2 = t24.ap()[:, par, :]
            sch.call("scalar", "activation", t1, pd.ap()[:, 0:CH],
                     AF.Abs, bias=bcol, wait=[tok_pd, ztoks.get(g - 4)])
            sch.call("scalar", "activation", t1, t1, AF.Exp, scale=-1.0)
            sch.call("scalar", "add", t1, t1, 1.0)
            sch.call("scalar", "activation", t1, t1, AF.Ln)
            sp_t[t] = sch.call("scalar", "activation", t2, pd.ap()[:, 0:CH],
                               AF.Relu, bias=bcol, inc=SS)
            pd_free[par] = sp_t[t]

        emit_dsp(0)
        for t in range(NT):
            par = t % 2
            p4 = t % 4
            g = it * NT + t
            if t + 1 < NT:
                emit_dsp(t + 1)
            if t == NT // 2:
                if it + 1 < BI:
                    emit_bc(it + 1)
                if it + 2 < BI:
                    emit_ldbc(it + 2)
            sch.call("vector", "tensor_add", dt4.ap()[:, par, :],
                     t14.ap()[:, par, :], t24.ap()[:, par, :],
                     wait=[sp_t[t], ut_tok[(it, t)], da_t.get(g - 2)])
            wtok = sch.call("vector", "tensor_mul", w4.ap()[:, p4, :],
                            dt4.ap()[:, par, :], u_slice(t),
                            inc=VS if t < 2 else None)
            if t >= 2:
                emit_scans(it, t - 2)
            dab = da3.ap()[:, par, :, :]
            for n in range(NST):
                sch.call("scalar", "activation", dab[:, n, :],
                         dt4.ap()[:, par, :], AF.Exp,
                         scale=a_neg.ap()[:, t * NST + n:t * NST + n + 1],
                         wait=[ztoks.get(g - 2), wtok])
            da_t[g] = sch.mark("scalar", SS)
        emit_scans(it, NT - 2)
        emit_scans(it, NT - 1)
        if it + 2 < BI:
            emit_lddtm(it + 2)
        flush_vev()
        ztoks[it * NT + NT] = fin = sch.call("vector", "engine_nop",
                                             inc=VS)
        flush_pend(fin, it + 1 if it + 1 < BI else None)
        last_z = fin
        fence("gpsimd", ST)

    bm_scope.close()
    B_done_V = (VS, VS.n)
    B_done_S = (SS, SS.n)
    B_done_T = (TS, TS.n)
    B_done_ST = (ST, ST.n)

    c_scope = ExitStack()
    mean_r = rows.ap()[:, 0:CH]
    var_r = rows.ap()[:, CH:2 * CH]
    m2_r = rows.ap()[:, 2 * CH:3 * CH]
    rstd_r = rows.ap()[:, 3 * CH:4 * CH]
    brow_r = rows.ap()[:, 4 * CH:5 * CH]
    wop = salloc(c_scope, "wop", [128, NT * DM])
    ygc = salloc(c_scope, "ygc", [128, NT * CH])
    szc = salloc(c_scope, "szc", [128, NT * CH])
    xc = salloc(c_scope, "xc", [128, 8 * CH])
    oc = salloc(c_scope, "oc", [128, 8 * CH])
    sqc = salloc(c_scope, "sqc", [128, CH])

    fence("sync", LD)
    for k in range(NT):
        load(wop.ap()[:, k * DM:(k + 1) * DM],
             d_opWT[k * 128:(k + 1) * 128, :], wait=[B_done_T])
    tok_wop = (LD, LD.n)

    prevC = {"vdone": B_done_V, "tdone": B_done_T, "sdone": B_done_S,
             "stores": B_done_ST, "rows_S": None, "stats_rd": None,
             "rowsT": None}

    for it, (b, c) in (enumerate(iters) if run_abc else []):
        fence("sync", LD)
        load(ygc.ap(),
             scr_yg[b, :, c * CH:(c + 1) * CH]
             .rearrange("(k p) j -> p k j", p=128),
             wait=[B_done_ST, prevC["vdone"], prevC["tdone"]])
        load(szc.ap(),
             scr_sz[b, :, c * CH:(c + 1) * CH]
             .rearrange("(k p) j -> p k j", p=128))
        load(xc.ap(),
             dx[b, :, c * CH:(c + 1) * CH]
             .rearrange("(k p) j -> p k j", p=128))
        cld = (LD, LD.n)

        for t in range(NT):
            sch.call("vector", "tensor_mul", ygc.ap()[:, t * CH:(t + 1) * CH],
                     ygc.ap()[:, t * CH:(t + 1) * CH],
                     szc.ap()[:, t * CH:(t + 1) * CH],
                     wait=[cld, prevC["tdone"]])
        tok_yf = sch.mark("vector", VS)

        sq_prev = None
        tok_pb2 = None
        for o in range(8):
            bi = o % 2
            for k in range(NT):
                sch.call("tensor", "matmul", pb[bi].ap()[:, 0:CH],
                         wop.ap()[:, k * DM + o * 128:k * DM + (o + 1) * 128],
                         ygc.ap()[:, k * CH:(k + 1) * CH],
                         wait=[tok_yf, tok_wop, bank_free[bi]],
                         start=(k == 0), stop=(k == NT - 1))
            tmm = sch.mark("tensor", TS)
            ocap = oc.ap()[:, o * CH:(o + 1) * CH]
            bank_free[bi] = sch.call("scalar", "copy", ocap,
                                     pb[bi].ap()[:, 0:CH],
                                     wait=[tmm, prevC["vdone"]], inc=SS)
            stok = sch.call("scalar", "activation", sqc.ap(), ocap, AF.Square,
                            wait=[sq_prev], inc=SS)
            sch.call("tensor", "matmul", pb[2].ap()[0:1, 0:CH], ones_col.ap(),
                     ocap, wait=[bank_free[bi], prevC["stats_rd"]],
                     start=(o == 0), stop=(o == 7))
            if o == 7:
                tok_pb2 = sch.mark("tensor", TS)
            sq_prev = sch.call("tensor", "matmul", pb[3].ap()[0:1, 0:CH],
                               ones_col.ap(), sqc.ap(),
                               wait=[stok, prevC["stats_rd"]],
                               start=(o == 0), stop=(o == 7), inc=TS)
        sch.call("vector", "tensor_scalar", mean_r, pb[2].ap()[0:1, 0:CH],
                 1.0 / DM, None, wait=[tok_pb2, sq_prev, prevC["rowsT"],
                                       prevC["rows_S"]], op0=ALU.mult)
        sch.call("vector", "tensor_scalar", var_r, pb[3].ap()[0:1, 0:CH],
                 1.0 / DM, None, op0=ALU.mult)
        sch.call("vector", "tensor_tensor", m2_r, mean_r, mean_r, op=ALU.mult)
        sch.call("vector", "tensor_tensor", var_r, var_r, m2_r,
                 op=ALU.subtract)
        stats_rd = sch.call("vector", "tensor_scalar", var_r, var_r, EPS,
                            None, op0=ALU.add, inc=VS)
        prevC["stats_rd"] = stats_rd
        sl = sch.call("scalar", "activation", rstd_r, var_r, AF.Ln,
                      wait=[stats_rd])
        sl = sch.call("scalar", "activation", rstd_r, rstd_r, AF.Exp,
                      scale=-0.5, inc=SS)
        sch.call("vector", "tensor_tensor", brow_r, mean_r, rstd_r,
                 wait=[sl], op=ALU.mult)
        rows_done = sch.call("vector", "tensor_scalar", brow_r, brow_r, -1.0,
                             None, op0=ALU.mult, inc=VS)
        prevC["rows_S"] = sl
        tb1 = sch.call("tensor", "matmul", pb[4].ap()[:, 0:CH],
                       ones_row.ap(), rstd_r,
                       wait=[rows_done, bank_free[4]],
                       start=True, stop=True)
        tb2 = sch.call("tensor", "matmul", pb[5].ap()[:, 0:CH],
                       ones_row.ap(), brow_r,
                       wait=[rows_done, bank_free[5]],
                       start=True, stop=True, inc=TS)
        prevC["rowsT"] = tb2
        for o in range(8):
            ocap = oc.ap()[:, o * CH:(o + 1) * CH]
            sch.call("vector", "tensor_tensor", ocap, ocap,
                     pb[4].ap()[:, 0:CH], wait=[tb2, prevC["stores"]],
                     op=ALU.mult)
            sch.call("vector", "tensor_tensor", ocap, ocap,
                     pb[5].ap()[:, 0:CH], op=ALU.add)
            vx2 = sch.call("vector", "tensor_tensor", ocap, ocap,
                           xc.ap()[:, o * CH:(o + 1) * CH], op=ALU.add,
                           inc=VS)
        store(scr_x2[b, :, c * CH:(c + 1) * CH]
              .rearrange("(o p) j -> p o j", p=128),
              oc.ap(), wait=[vx2])
        bank_free[4] = bank_free[5] = (VS, VS.n)
        prevC["vdone"] = (VS, VS.n)
        prevC["tdone"] = (TS, TS.n)
        prevC["stores"] = (ST, ST.n)
        fence("gpsimd", ST)

    c_scope.close()
    mk_scope.close()
    C_done_V = (VS, VS.n)
    C_done_S = (SS, SS.n)
    C_done_T = (TS, TS.n)
    C_done_ST = (ST, ST.n)

    f0_scope = ExitStack()
    wt0 = salloc(f0_scope, "wt0", [128, 2 * 512])
    absb = salloc(f0_scope, "absb", [128, 1024])
    ones128 = salloc(f0_scope, "ones128", [128, 128])
    colacc = salloc(f0_scope, "colacc", [128, 8])
    tcol = salloc(f0_scope, "tcol", [128, 1])
    mcol = salloc(f0_scope, "mcol", [128, 1])
    q32a = salloc(f0_scope, "q32a", [128, 1024])
    q32b = salloc(f0_scope, "q32b", [128, 1024])
    wqb = salloc(f0_scope, "wqb", [128, 1024], BF16)
    wq_c = salloc(f0_scope, "wq_c", [128, 2 * 3])

    mats = [(d_gWT, 8, 8, DM * 4096), (d_uWT, 8, 8, DM * 4096),
            (d_dWT, 32, 2, DM * 4096)]
    wq_kind = "ExternalOutput" if dbg else "Internal"
    scr_wq = [nc.dram_tensor("scr_wgq", [DM, 4096], BF16, kind=wq_kind).ap(),
              nc.dram_tensor("scr_wuq", [DM, 4096], BF16, kind=wq_kind).ap(),
              nc.dram_tensor("scr_wdq", [4096, DM], BF16, kind=wq_kind).ap()]

    prevF0 = {"s": C_done_S, "v": C_done_V, "t": C_done_T, "v2": None}
    sch.call("vector", "memset", ones128.ap(), 1.0, wait=[C_done_T], inc=VS)
    for mi, (dsrc, nk, ncc, nel) in enumerate(mats):
        vz = sch.call("vector", "memset", colacc.ap(), 0.0,
                      wait=[prevF0["t"]], inc=VS)
        sab_prev = None
        vac = None
        for k in range(nk):
            for cc in range(ncc // 2):
                fence("sync", LD)
                wl = load(wt0.ap(),
                          dsrc[k * 128:(k + 1) * 128,
                               cc * 1024:(cc + 1) * 1024],
                          wait=[prevF0["s"], prevF0["t"], prevF0["v2"],
                                vac, sab_prev])
                for hh in range(2):
                    sab = sch.call("scalar", "activation",
                                   absb.ap()[:, hh * 512:(hh + 1) * 512],
                                   wt0.ap()[:, hh * 512:(hh + 1) * 512],
                                   AF.Abs, wait=[wl, vac], inc=SS)
                    sab_prev = sab
                    sch.call("vector", "tensor_reduce", tcol.ap(),
                             absb.ap()[:, hh * 512:(hh + 1) * 512],
                             AX.X, ALU.add, wait=[sab])
                    vac = sch.call("vector", "tensor_add",
                                   colacc.ap()[:, 0:1],
                                   colacc.ap()[:, 0:1], tcol.ap(), inc=VS)
        tbt = sch.call("tensor", "matmul", pb[3].ap()[:, 0:8], ones128.ap(),
                       colacc.ap(), wait=[vac, bank_free[3]],
                       start=True, stop=True, inc=TS)
        smc = sch.call("scalar", "copy", mcol.ap(), pb[3].ap()[:, 0:1],
                       wait=[tbt], inc=SS)
        bank_free[3] = smc
        sch.call("vector", "tensor_scalar", mcol.ap(), mcol.ap(), 1.0 / nel,
                 None, wait=[smc], op0=ALU.mult)
        vmc = sch.call("vector", "tensor_scalar", mcol.ap(), mcol.ap(), 1e-5,
                       None, op0=ALU.max, inc=VS)
        sch.call("vector", "tensor_copy", cells.ap()[:, mi:mi + 1],
                 mcol.ap()[0:1, 0:1])
        sch.call("vector", "tensor_scalar", wq_c.ap()[:, 2 * mi:2 * mi + 1],
                 mcol.ap(), 0.5, None, op0=ALU.mult)
        vcn = sch.call("vector", "tensor_scalar",
                       wq_c.ap()[:, 2 * mi + 1:2 * mi + 2],
                       wq_c.ap()[:, 2 * mi:2 * mi + 1], -0.5 / 0.5, None,
                       op0=ALU.mult, inc=VS)
        vq_prev = vcn
        st_prev = None
        for k in range(nk):
            for cc in range(ncc // 2):
                fence("sync", LD)
                wl = load(wt0.ap(),
                          dsrc[k * 128:(k + 1) * 128,
                               cc * 1024:(cc + 1) * 1024], wait=[vq_prev])
                sch.call("vector", "tensor_scalar", q32a.ap(), wt0.ap(),
                         wq_c.ap()[:, 2 * mi:2 * mi + 1], None,
                         wait=[wl, vcn, st_prev], op0=ALU.is_gt)
                sch.call("vector", "tensor_scalar", q32b.ap(), wt0.ap(),
                         wq_c.ap()[:, 2 * mi + 1:2 * mi + 2], None,
                         op0=ALU.is_lt)
                vq_prev = sch.call("vector", "tensor_tensor", wqb.ap(),
                                   q32a.ap(), q32b.ap(), op=ALU.subtract,
                                   inc=VS)
                st_prev = store(scr_wq[mi][k * 128:(k + 1) * 128,
                                           cc * 1024:(cc + 1) * 1024],
                                wqb.ap(), wait=[vq_prev])
                fence("gpsimd", ST)
        prevF0["s"] = (SS, SS.n)
        prevF0["t"] = (TS, TS.n)
        prevF0["v2"] = vq_prev
    f0_scope.close()
    F0_done_V = (VS, VS.n)
    F0_done_ST = (ST, ST.n)

    f_scope = ExitStack()
    TB = 2 * CH
    x2blk = salloc(f_scope, "x2blk", [128, 8 * TB])
    xh3 = salloc(f_scope, "xh3", [128, 8 * TB])
    xq_b = salloc(f_scope, "xq_b", [128, 8 * TB], BF16)
    hp = salloc(f_scope, "hp", [128, 32 * TB])
    hq_b = salloc(f_scope, "hq_b", [128, 32 * TB], BF16)
    fc = salloc(f_scope, "fc", [128, 8 * TB])
    wgq_sb = salloc(f_scope, "wgq_sb", [128, 2 * 8 * 128], BF16)
    wuq_sb = salloc(f_scope, "wuq_sb", [128, 2 * 8 * 128], BF16)
    wdq_sb = salloc(f_scope, "wdq_sb", [128, 2 * 32 * 128], BF16)
    gdt = salloc(f_scope, "gdt", [128, TB])
    sgt = salloc(f_scope, "sgt", [128, TB])
    i32b = salloc(f_scope, "i32b", [128, TB], I32)
    rmx = salloc(f_scope, "rmx", [128, TB])
    sq_f = salloc(f_scope, "sq_f", [128, TB])
    bcf = salloc(f_scope, "bcf", [128, 4 * TB])
    f_rows = salloc(f_scope, "f_rows", [1, 5 * TB])

    BCR, BCB, BCSX, BCIG = range(4)
    BCSH, BCDR = 0, 1

    def bcf_s(i):
        return bcf.ap()[:, i * TB:(i + 1) * TB]

    mean_f = f_rows.ap()[:, 0 * TB:1 * TB]
    var_f = f_rows.ap()[:, 1 * TB:2 * TB]
    rstd_f = f_rows.ap()[:, 2 * TB:3 * TB]
    brow_f = f_rows.ap()[:, 3 * TB:4 * TB]
    aux_f = f_rows.ap()[:, 4 * TB:5 * TB]

    colT = salloc(f_scope, "colT", [128, 8])

    def part_tree_max(src_ap, out_row, wait=()):
        w = list(wait) + [tok_ident]
        tokv = None
        for cb in range(TB // 128):
            bk = pb[4 + (cb % 2)]
            tt = sch.call("tensor", "transpose", bk.ap()[:, 0:128],
                          src_ap[:, cb * 128:(cb + 1) * 128], ident.ap(),
                          wait=w + [bank_free[4 + (cb % 2)]], inc=TS)
            tokv = sch.call("vector", "tensor_reduce", colT.ap()[:, cb:cb + 1],
                            bk.ap()[:, 0:128], AX.X, ALU.max,
                            wait=[tt], inc=VS)
            bank_free[4 + (cb % 2)] = tokv
        toks = None
        for cb in range(TB // 128):
            bk = pb[4 + (cb % 2)]
            tt = sch.call("tensor", "transpose", bk.ap()[0:1, 0:128],
                          colT.ap()[:, cb:cb + 1], ident.ap(),
                          wait=[tokv, bank_free[4 + (cb % 2)]], inc=TS)
            toks = sch.call("scalar", "copy",
                            out_row[:, cb * 128:(cb + 1) * 128],
                            bk.ap()[0:1, 0:128], wait=[tt], inc=SS)
            bank_free[4 + (cb % 2)] = toks
        return toks

    def ln_rows(tok_sum, tok_sumsq, pbs, pbq, prev_rowuser):
        sch.call("vector", "tensor_scalar", mean_f, pbs, 1.0 / DM, None,
                 wait=[tok_sum, tok_sumsq, prev_rowuser], op0=ALU.mult)
        sch.call("vector", "tensor_scalar", var_f, pbq, 1.0 / DM, None,
                 op0=ALU.mult)
        sch.call("vector", "tensor_tensor", aux_f, mean_f, mean_f,
                 op=ALU.mult)
        sch.call("vector", "tensor_tensor", var_f, var_f, aux_f,
                 op=ALU.subtract)
        vrd = sch.call("vector", "tensor_scalar", var_f, var_f, EPS, None,
                       op0=ALU.add, inc=VS)
        sch.call("scalar", "activation", rstd_f, var_f, AF.Ln, wait=[vrd])
        sl = sch.call("scalar", "activation", rstd_f, rstd_f, AF.Exp,
                      scale=-0.5, inc=SS)
        sch.call("vector", "tensor_tensor", brow_f, mean_f, rstd_f,
                 wait=[sl], op=ALU.mult)
        return sch.call("vector", "tensor_scalar", brow_f, brow_f, -1.0,
                        None, op0=ALU.mult, inc=VS), vrd

    def bcast2(r1, r2, wait=()):
        t1b = sch.call("tensor", "matmul", pb[4].ap()[:, 0:TB], ones_row.ap(),
                       r1[0], wait=list(wait) + [bank_free[4]],
                       start=True, stop=True, inc=TS)
        s1b = sch.call("scalar", "copy", bcf_s(r1[1]), pb[4].ap()[:, 0:TB],
                       wait=[t1b, r1[2]], inc=SS)
        bank_free[4] = s1b
        if r2 is None:
            return s1b
        t2b = sch.call("tensor", "matmul", pb[5].ap()[:, 0:TB], ones_row.ap(),
                       r2[0], wait=list(wait) + [bank_free[5]],
                       start=True, stop=True, inc=TS)
        s2b = sch.call("scalar", "copy", bcf_s(r2[1]), pb[5].ap()[:, 0:TB],
                       wait=[t2b, r2[2]], inc=SS)
        bank_free[5] = s2b
        return s2b

    prevF = {"v": F0_done_V, "t": (TS, TS.n), "s": (SS, SS.n),
             "stores": (ST, ST.n), "stats": bank_free[3], "rowuser": None}

    for j in range(NCH):
        fence("sync", LD)
        for bb in range(B):
            load(x2blk.ap().rearrange("p (o z) -> p o z", z=TB)
                 [:, :, bb * CH:(bb + 1) * CH],
                 scr_x2[bb, :, j * CH:(j + 1) * CH]
                 .rearrange("(o p) j -> p o j", p=128),
                 wait=[C_done_ST, prevF["v"], prevF["t"]])
        xl = (LD, LD.n)

        sqp = None
        tok_sum = None
        for o in range(8):
            xo = x2blk.ap()[:, o * TB:(o + 1) * TB]
            stok = sch.call("scalar", "activation", sq_f.ap(), xo, AF.Square,
                            wait=[xl, prevF["stats"], sqp], inc=SS)
            sch.call("tensor", "matmul", pb[3].ap()[0:1, 0:TB], ones_col.ap(),
                     xo, wait=[xl, prevF["stats"]],
                     start=(o == 0), stop=(o == 7))
            if o == 7:
                tok_sum = sch.mark("tensor", TS)
            sqp = sch.call("tensor", "matmul", pb[6].ap()[0:1, 0:TB],
                           ones_col.ap(), sq_f.ap(), wait=[stok],
                           start=(o == 0), stop=(o == 7), inc=TS)
        rows_done, _ = ln_rows(tok_sum, sqp, pb[3].ap()[0:1, 0:TB],
                               pb[6].ap()[0:1, 0:TB], prevF["rowuser"])
        sb = bcast2((rstd_f, BCR, None), (brow_f, BCB, None),
                    wait=[rows_done])
        vx = None
        for o in range(8):
            xo = x2blk.ap()[:, o * TB:(o + 1) * TB]
            ho = xh3.ap()[:, o * TB:(o + 1) * TB]
            sch.call("vector", "tensor_tensor", ho, xo, bcf_s(BCR),
                     wait=[sb], op=ALU.mult)
            vt = sch.call("vector", "tensor_tensor", ho, ho, bcf_s(BCB),
                          op=ALU.add, inc=VS)
            sab = sch.call("scalar", "activation", sq_f.ap(), ho, AF.Abs,
                           wait=[vt], inc=SS)
            if o == 0:
                sch.call("vector", "tensor_copy", rmx.ap(), sq_f.ap(),
                         wait=[sab])
            else:
                sch.call("vector", "tensor_tensor", rmx.ap(), rmx.ap(),
                         sq_f.ap(), wait=[sab], op=ALU.max)
        vx = sch.mark("vector", VS)
        ptm = part_tree_max(rmx.ap(), mean_f, wait=[vx])
        sch.call("vector", "tensor_scalar", mean_f, mean_f, 1e-5, None,
                 wait=[ptm], op0=ALU.max)
        sch.call("vector", "reciprocal", var_f, mean_f)
        sch.call("vector", "tensor_scalar", rstd_f, var_f, 127.0, None,
                 op0=ALU.mult)
        sch.call("vector", "tensor_scalar", brow_f, mean_f,
                 cells.ap()[:, 0:1], None, op0=ALU.mult)
        vig = sch.call("vector", "tensor_scalar", brow_f, brow_f,
                       1.0 / 127.0, None, op0=ALU.mult, inc=VS)
        sb2 = bcast2((rstd_f, BCSX, None), (brow_f, BCIG, None), wait=[vig])
        vq = None
        for o in range(8):
            ho = xh3.ap()[:, o * TB:(o + 1) * TB]
            sch.call("vector", "tensor_tensor", ho, ho, bcf_s(BCSX),
                     wait=[sb2], op=ALU.mult)
            sch.call("vector", "tensor_copy", i32b.ap(), ho)
            vq = sch.call("vector", "tensor_copy",
                          xq_b.ap()[:, o * TB:(o + 1) * TB], i32b.ap(),
                          inc=VS)
        if dbg and j == 0:
            for o in range(8):
                cv = sch.call("vector", "tensor_copy", sq_f.ap(),
                              xq_b.ap()[:, o * TB:(o + 1) * TB],
                              wait=[vq, (ST, ST.n)], inc=VS)
                store(d_xq[:, o * TB:(o + 1) * TB], sq_f.ap(), wait=[cv])
            fence("gpsimd", ST)
        vhm = None
        wld = {}
        tsu = {}
        for et in range(33):
            if et < 32:
                par = et % 2
                fence("sync", LD)
                load(wgq_sb.ap()[:, par * 1024:(par + 1) * 1024],
                     scr_wq[0][:, et * 128:(et + 1) * 128]
                     .rearrange("(k p) j -> p k j", p=128),
                     wait=[F0_done_ST, tsu.get(et - 2)])
                load(wuq_sb.ap()[:, par * 1024:(par + 1) * 1024],
                     scr_wq[1][:, et * 128:(et + 1) * 128]
                     .rearrange("(k p) j -> p k j", p=128))
                wld[et] = (LD, LD.n)
            if et == 0:
                continue
            e = et - 1
            par = e % 2
            bg = 0 if par == 0 else 7
            bu = 1 if par == 0 else 2
            for k in range(8):
                sch.call("tensor", "matmul", pb[bg].ap()[:, 0:TB],
                         wgq_sb.ap()[:, (par * 8 + k) * 128:
                                     (par * 8 + k + 1) * 128],
                         xq_b.ap()[:, k * TB:(k + 1) * TB],
                         wait=[wld[e], vq, bank_free[bg]],
                         start=(k == 0), stop=(k == 7))
            tg = sch.mark("tensor", TS)
            for k in range(8):
                sch.call("tensor", "matmul", pb[bu].ap()[:, 0:TB],
                         wuq_sb.ap()[:, (par * 8 + k) * 128:
                                     (par * 8 + k + 1) * 128],
                         xq_b.ap()[:, k * TB:(k + 1) * TB],
                         wait=[wld[e], vq, bank_free[bu]],
                         start=(k == 0), stop=(k == 7))
            tu = sch.mark("tensor", TS)
            tsu[e] = tu
            vg = sch.call("vector", "tensor_tensor", gdt.ap(),
                          pb[bg].ap()[:, 0:TB], bcf_s(BCIG),
                          wait=[tg, sb2], op=ALU.mult, inc=VS)
            bank_free[bg] = vg
            sg = sch.call("scalar", "activation", sgt.ap(), gdt.ap(),
                          AF.Silu, wait=[vg], inc=SS)
            he = hp.ap()[:, e * TB:(e + 1) * TB]
            vh = sch.call("vector", "tensor_tensor", he, sgt.ap(),
                          pb[1 if par == 0 else 2].ap()[:, 0:TB],
                          wait=[sg, tu], op=ALU.mult, inc=VS)
            bank_free[bu] = vh
            sabh = sch.call("scalar", "activation", sq_f.ap(), he, AF.Abs,
                            wait=[vh], inc=SS)
            if e == 0:
                vhm = sch.call("vector", "tensor_copy", rmx.ap(), sq_f.ap(),
                               wait=[sabh], inc=VS)
            else:
                vhm = sch.call("vector", "tensor_tensor", rmx.ap(), rmx.ap(),
                               sq_f.ap(), wait=[sabh], op=ALU.max, inc=VS)
        if dbg and j == 0:
            for e in range(32):
                store(d_hp[:, e * TB:(e + 1) * TB],
                      hp.ap()[:, e * TB:(e + 1) * TB], wait=[vhm])
            fence("gpsimd", ST)
            sch.call("vector", "nop", wait=[(ST, ST.n)])
        ptm2 = part_tree_max(rmx.ap(), var_f, wait=[vhm])
        sch.call("vector", "tensor_scalar", aux_f, var_f,
                 cells.ap()[:, 1:2], None, wait=[ptm2], op0=ALU.mult)
        sch.call("vector", "tensor_tensor", aux_f, aux_f, mean_f,
                 op=ALU.mult)
        sch.call("vector", "tensor_scalar", aux_f, aux_f, 1.0 / 127.0, None,
                 op0=ALU.mult)
        sch.call("vector", "tensor_scalar", aux_f, aux_f, 1e-5, None,
                 op0=ALU.max)
        sch.call("vector", "reciprocal", var_f, aux_f)
        sch.call("vector", "tensor_tensor", var_f, var_f, mean_f,
                 op=ALU.mult)
        sch.call("vector", "tensor_scalar", var_f, var_f,
                 cells.ap()[:, 1:2], None, op0=ALU.mult)
        sch.call("vector", "tensor_scalar", aux_f, aux_f,
                 cells.ap()[:, 2:3], None, op0=ALU.mult)
        vdr = sch.call("vector", "tensor_scalar", aux_f, aux_f, 1.0 / 127.0,
                       None, op0=ALU.mult, inc=VS)
        sb3 = bcast2((var_f, BCSH, None), (aux_f, BCDR, None), wait=[vdr])
        vq2 = None
        for e in range(32):
            he = hp.ap()[:, e * TB:(e + 1) * TB]
            sch.call("vector", "tensor_tensor", he, he, bcf_s(BCSH),
                     wait=[sb3], op=ALU.mult)
            sch.call("vector", "tensor_copy", i32b.ap(), he)
            vq2 = sch.call("vector", "tensor_copy",
                           hq_b.ap()[:, e * TB:(e + 1) * TB], i32b.ap(),
                           inc=VS)
        wld2 = {}
        tsd = {}
        for ot in range(9):
            if ot < 8:
                par = ot % 2
                fence("sync", LD)
                load(wdq_sb.ap()[:, par * 4096:(par + 1) * 4096],
                     scr_wq[2][:, ot * 128:(ot + 1) * 128]
                     .rearrange("(k p) j -> p k j", p=128),
                     wait=[F0_done_ST, tsd.get(ot - 2)])
                wld2[ot] = (LD, LD.n)
            if ot == 0:
                continue
            o = ot - 1
            par = o % 2
            for k in range(32):
                sch.call("tensor", "matmul", pb[2].ap()[:, 0:TB],
                         wdq_sb.ap()[:, (par * 32 + k) * 128:
                                     (par * 32 + k + 1) * 128],
                         hq_b.ap()[:, k * TB:(k + 1) * TB],
                         wait=[wld2[o], vq2, bank_free[2]],
                         start=(k == 0), stop=(k == 31))
            td = sch.mark("tensor", TS)
            tsd[o] = td
            fo = fc.ap()[:, o * TB:(o + 1) * TB]
            vf = sch.call("vector", "tensor_tensor", fo,
                          pb[2].ap()[:, 0:TB], bcf_s(BCDR),
                          wait=[td, sb3], op=ALU.mult, inc=VS)
            bank_free[2] = vf
        if dbg and j == 0:
            for o in range(8):
                store(d_fc[:, o * TB:(o + 1) * TB],
                      fc.ap()[:, o * TB:(o + 1) * TB], wait=[vf])
            fence("gpsimd", ST)
            sch.call("vector", "nop", wait=[(ST, ST.n)])
        sqp = None
        tok_sum = None
        for o in range(8):
            fo = fc.ap()[:, o * TB:(o + 1) * TB]
            stok = sch.call("scalar", "activation", sq_f.ap(), fo, AF.Square,
                            wait=[vf, sqp], inc=SS)
            sch.call("tensor", "matmul", pb[3].ap()[0:1, 0:TB], ones_col.ap(),
                     fo, wait=[vf, bank_free[3]],
                     start=(o == 0), stop=(o == 7))
            if o == 7:
                tok_sum = sch.mark("tensor", TS)
            sqp = sch.call("tensor", "matmul", pb[6].ap()[0:1, 0:TB],
                           ones_col.ap(), sq_f.ap(), wait=[stok],
                           start=(o == 0), stop=(o == 7), inc=TS)
        rows_done, vrd4 = ln_rows(tok_sum, sqp, pb[3].ap()[0:1, 0:TB],
                                  pb[6].ap()[0:1, 0:TB], None)
        bank_free[3] = vrd4
        sb4 = bcast2((rstd_f, BCSX, None), (brow_f, BCIG, None),
                     wait=[rows_done])
        for o in range(8):
            fo = fc.ap()[:, o * TB:(o + 1) * TB]
            xo = x2blk.ap()[:, o * TB:(o + 1) * TB]
            sch.call("vector", "tensor_tensor", fo, fo, bcf_s(BCSX),
                     wait=[sb4, prevF["stores"]], op=ALU.mult)
            sch.call("vector", "tensor_tensor", fo, fo, bcf_s(BCIG),
                     op=ALU.add)
            vx3 = sch.call("vector", "tensor_tensor", fo, fo, xo,
                           op=ALU.add, inc=VS)
        for bb in range(B):
            store(d_out[bb, :, j * CH:(j + 1) * CH]
                  .rearrange("(o p) j -> p o j", p=128),
                  fc.ap().rearrange("p (o z) -> p o z", z=TB)
                  [:, :, bb * CH:(bb + 1) * CH], wait=[vx3])
        prevF["v"] = (VS, VS.n)
        prevF["t"] = (TS, TS.n)
        prevF["stats"] = vrd4
        prevF["rowuser"] = sb4
        prevF["stores"] = (ST, ST.n)
        fence("gpsimd", ST)

    f_scope.close()
    F_done_V = (VS, VS.n)
    F_done_S = (SS, SS.n)
    F_done_T = (TS, TS.n)

    sch.call("gpsimd", "nop", wait=[(ST, ST.n), (LD, LD.n), C_done_V,
                                    C_done_S, C_done_T])

    sch.finalize()
    stack.close()
    return nc


def _prep_inputs(inputs):
    def T(a):
        return np.ascontiguousarray(np.asarray(a, dtype=np.float32).T)

    def C(a):
        return np.ascontiguousarray(np.asarray(a, dtype=np.float32))

    def V(a):
        return np.ascontiguousarray(
            np.asarray(a, dtype=np.float32).reshape(-1, 1))

    return {
        "inWT": T(inputs["L0_attn_in_proj_w"]),
        "convW": C(inputs["L0_attn_conv_w"]),
        "convB": V(inputs["L0_attn_conv_b"]),
        "xprojWT": T(inputs["L0_attn_x_proj_w"]),
        "Alog": C(inputs["L0_attn_A_log"]),
        "Dvec": V(inputs["L0_attn_D"]),
        "dtowT": T(inputs["L0_attn_dt_out_w"]),
        "dtoB": V(inputs["L0_attn_dt_out_b"]),
        "opWT": T(inputs["L0_attn_out_proj_w"]),
        "minWT": T(inputs["L0_attn_m_in_w"]),
        "mconvW": C(inputs["L0_attn_m_conv_w"]),
        "mconvB": V(inputs["L0_attn_m_conv_b"]),
        "mxprojWT": T(inputs["L0_attn_m_x_proj_w"]),
        "mdtWT": T(inputs["L0_attn_m_dt_w"]),
        "mdtB": V(inputs["L0_attn_m_dt_b"]),
        "mAlog": C(inputs["L0_attn_m_A_log"]),
        "mDvec": V(inputs["L0_attn_m_D"]),
        "moutWT": T(inputs["L0_attn_m_out_w"]),
        "gWT": T(inputs["L0_ffn_w_gate"]),
        "uWT": T(inputs["L0_ffn_w_up"]),
        "dWT": T(inputs["L0_ffn_w_down"]),
    }


def kernel(**inputs):
    ncores = 8
    x = np.asarray(inputs["L0_attn_hidden_states"], dtype=np.float32)
    common = _prep_inputs(inputs)
    nc = build_nc()
    in_maps = []
    for i in range(ncores):
        m = dict(common)
        m["x_fm"] = np.ascontiguousarray(x[2 * i:2 * i + 2].transpose(0, 2, 1))
        in_maps.append(m)
    res = run_bass_kernel_spmd(nc, in_maps, core_ids=list(range(ncores)))
    outs = [np.asarray(r["out_fm"]).transpose(0, 2, 1) for r in res.results]
    return np.ascontiguousarray(np.concatenate(outs, axis=0))
